# version 09
# baseline (speedup 1.0000x reference)
import os
from contextlib import ExitStack

import numpy as np
import ml_dtypes
import concourse.bass as bass
import concourse.mybir as mybir
from concourse.bass_utils import run_bass_kernel_spmd

F32 = mybir.dt.float32
BF16 = mybir.dt.bfloat16
AF = mybir.ActivationFunctionType
ALU = mybir.AluOpType
AX = mybir.AxisListType

NCORES = 8
TOK = 2048
NT = TOK // 128
D = 1024
NC_D = D // 128
B = 4
SEQ = 4096
POS = SEQ // NCORES
EPS_LN = 1e-5


class Sem:
    def __init__(self, h):
        self.h = h
        self.n = 0

    def inc(self, inst, k=1):
        inst.then_inc(self.h, k)
        self.n += k
        return self.n



def _f32(x):
    return np.ascontiguousarray(x, dtype=np.float32)


def prep_host(inp):
    h = {}
    x = np.asarray(inp["L0_attn_iQ"], np.float32)
    xt3 = np.ascontiguousarray(x.reshape(B * SEQ, D).T).reshape(D, B, SEQ)
    h["xT_rot"] = [np.ascontiguousarray(np.roll(xt3, -POS * c, axis=2).reshape(D, B * SEQ))
                   for c in range(NCORES)]
    h["x_own"] = [np.ascontiguousarray(x[:, POS * c:POS * (c + 1), :].reshape(TOK, D))
                  for c in range(NCORES)]

    ba = np.asarray(inp["L0_attn_ba"], np.float32)
    Wo0 = np.asarray(inp["L0_attn_Wo"], np.float32)
    h["WaT"] = _f32(np.asarray(inp["L0_attn_Wa"]).T)
    h["baQ"] = _f32(ba[0:D].reshape(NC_D, 128).T)
    h["baK"] = _f32(ba[D:2 * D].reshape(NC_D, 128).T)
    h["baVt"] = _f32(np.tile(ba[2 * D:3 * D].reshape(1, -1), (128, 1)))
    h["snbT"] = _f32(np.tile(np.asarray(inp["L0_attn_sn_bias"]).reshape(1, 1), (128, 1)))
    h["WoTh"] = _f32(np.concatenate(
        [Wo0.T.reshape(16, 64, D).transpose(1, 0, 2).reshape(64, 16 * D),
         np.zeros((64, 16 * D), np.float32)], axis=0))
    h["boT"] = _f32(np.tile(np.asarray(inp["L0_attn_bo"]).reshape(1, -1), (128, 1)))
    rel = np.asarray(inp["L0_attn_rel_emb"], np.float32)
    idx = np.clip(np.arange(511) - 255, -16, 16) + 16
    h["relXT_std"] = _f32(rel[idx].T)
    idxA = np.clip(np.arange(255) - 255, -16, 16) + 16
    idxB = np.clip(np.arange(255) + 1, -16, 16) + 16
    h["relXT_wA"] = [_f32(rel[np.full(255, 32) if c == 0 else idxA].T) for c in range(NCORES)]
    h["relXT_wB"] = [_f32(rel[np.full(255, 0) if c == NCORES - 1 else idxB].T) for c in range(NCORES)]
    NJ = SEQ // 128
    h["chi"] = []
    h["augsel"] = []
    for c in range(NCORES):
        regions = np.zeros((NJ, 4), np.int8)
        for j in range(NJ):
            for t in range(4):
                if j == (t - 1) % NJ or j == t or j == t + 1:
                    regions[j, t] = 0
                elif j < t - 1 or (c > 0 and j >= NJ - 4 * c):
                    regions[j, t] = 1
                else:
                    regions[j, t] = 2
        variants, vmap, vj = [], {}, np.zeros(NJ, np.int64)
        for j in range(NJ):
            key = tuple(regions[j])
            if key not in vmap:
                vmap[key] = len(variants)
                variants.append(key)
            vj[j] = vmap[key]
        assert len(variants) <= 8, (c, len(variants))
        chi = np.zeros((8, SEQ), np.float32)
        for j in range(NJ):
            chi[vj[j], 128 * j:128 * (j + 1)] = 1.0
        h["chi"].append(chi)
        asel = np.zeros((4, 64, 8), np.float32)
        for v, key in enumerate(variants):
            for s in range(4):
                if key[s] == 1:
                    asel[s, :, v] = rel[0]
                elif key[s] == 2:
                    asel[s, :, v] = rel[32]
        h["augsel"].append(np.ascontiguousarray(asel.reshape(4 * 64, 8)))

    h["rwT"] = _f32(np.asarray(inp["L0_ffn_router_w"]).T)
    h["rb_row"] = _f32(np.asarray(inp["L0_ffn_router_b"]).reshape(1, -1))
    wu = np.asarray(inp["L0_ffn_w_up"], np.float32)
    wd = np.asarray(inp["L0_ffn_w_down"], np.float32)
    swu = np.asarray(inp["L0_ffn_shared_w_up"], np.float32)[0]
    swd = np.asarray(inp["L0_ffn_shared_w_down"], np.float32)[0]
    h["upT"] = _f32(np.stack([swu.T] + [wu[e].T for e in range(8)]))
    h["dnT"] = _f32(np.stack([swd.T] + [wd[e].T for e in range(8)]))

    h["qwT"] = _f32(np.asarray(inp["L1_attn_q_w"]).T)
    h["kwT"] = _f32(np.asarray(inp["L1_attn_k_w"]).T)
    h["vwT"] = _f32(np.asarray(inp["L1_attn_v_w"]).T)
    h["owT"] = _f32(np.asarray(inp["L1_attn_out_w"]).T)
    h["qw_nat"] = _f32(np.asarray(inp["L1_attn_q_w"]))
    h["qb1_pc"] = _f32(np.asarray(inp["L1_attn_q_b"]).reshape(NC_D, 128).T)
    h["qbBt"] = _f32(np.tile(np.asarray(inp["L1_attn_q_b"]).reshape(1, -1), (128, 1)))
    h["kbBt"] = _f32(np.tile(np.asarray(inp["L1_attn_k_b"]).reshape(1, -1), (128, 1)))
    h["vbBt"] = _f32(np.tile(np.asarray(inp["L1_attn_v_b"]).reshape(1, -1), (128, 1)))
    h["obBt"] = _f32(np.tile(np.asarray(inp["L1_attn_out_b"]).reshape(1, -1), (128, 1)))
    h["grepT"] = _f32(np.asarray(inp["L1_attn_grep_w"]).T)
    h["grepb_t"] = _f32(np.tile(np.asarray(inp["L1_attn_grep_b"]), 16).reshape(1, 128))
    h["gaBt"] = _f32(np.tile(np.asarray(inp["L1_attn_grep_a"]).reshape(1, -1), (128, 1)))
    rel1 = np.asarray(inp["L1_attn_rel_emb"], np.float32)
    pos4 = np.arange(4)
    relm = pos4[None, :] - pos4[:, None]
    bucket = (relm > 0).astype(np.int64) * 16 + np.abs(relm)
    h["pbBt"] = _f32(np.tile(rel1[bucket].transpose(2, 0, 1).reshape(1, 256), (128, 1)))

    h["WgT"] = _f32(np.asarray(inp["L1_ffn_Wg"]).T)
    h["bg_row"] = _f32(np.asarray(inp["L1_ffn_bg"]).reshape(1, -1))
    h["WiT"] = _f32(np.asarray(inp["L1_ffn_Wi"]).T)
    h["WoT1"] = _f32(np.asarray(inp["L1_ffn_Wo"]).T)
    h["AT"] = _f32(np.asarray(inp["L1_ffn_A"]).transpose(0, 2, 1).reshape(4 * D, 4))
    Bm = np.asarray(inp["L1_ffn_B"], np.float32)
    h["BTc"] = _f32(np.concatenate([Bm[e].T for e in range(4)], axis=1))

    h["ident"] = np.eye(128, dtype=np.float32)
    h["halfsel"] = np.stack([np.r_[np.ones(64), np.zeros(64)],
                             np.r_[np.zeros(64), np.ones(64)]]).astype(np.float32)
    return h



def emit_xT(nc, blk, sems, x_hbm, xT, stage, idb, pbank2, marks, tag):
    t_tr, v_tr = sems["t_" + tag], sems["v_" + tag]
    gs = [sems["g_" + tag + "0"], sems["g_" + tag + "1"]]
    g_bases = [gs[0].n, gs[1].n]
    t_base, v_base = t_tr.n, v_tr.n

    def g(gp):
        for i in range(NT):
            if i >= 2:
                gp.wait_ge(t_tr.h, t_base + (i - 1) * NC_D)
            gs[i % 2].inc(gp.dma_start(stage[:, (i % 2) * D:(i % 2 + 1) * D],
                                       x_hbm[i * 128:(i + 1) * 128, :]), 16)
    blk.gpsimd(g)

    def t(te):
        for i in range(NT):
            te.wait_ge(gs[i % 2].h, g_bases[i % 2] + 16 * (i // 2 + 1))
            for c in range(NC_D):
                k = i * NC_D + c
                if k >= 2:
                    te.wait_ge(v_tr.h, v_base + k - 1)
                dst = pbank2[k % 2][:, :].bitcast(BF16)[:, 0:128]
                t_tr.inc(te.transpose(
                    dst, stage[:, (i % 2) * D + c * 128:(i % 2) * D + (c + 1) * 128],
                    idb[:, :]))
    blk.tensor(t)

    def v(ve):
        for i in range(NT):
            for c in range(NC_D):
                k = i * NC_D + c
                ve.wait_ge(t_tr.h, t_base + k + 1)
                src = pbank2[k % 2][:, :].bitcast(BF16)[:, 0:128]
                v_tr.inc(ve.tensor_copy(xT[:, c * TOK + i * 128:c * TOK + (i + 1) * 128], src))
    blk.vector(v)
    marks["xT_done_" + tag] = (v_tr, v_tr.n)


def emit_ln_epilogue(nc, blk, sems, acc_src_fn, x_res_hbm, out_hbm, stage, rs2,
                     x2t, sqd, stats, marks, wait_marks, tag, out_extra=None,
                     eps_col=None, vf=None, sf=None):
    v_e, s_e = sems["v_" + tag], sems["s_" + tag]
    xos = [sems["x_" + tag + "0"], sems["x_" + tag + "1"]]
    xo_bases = [xos[0].n, xos[1].n]
    grs = [sems["g_" + tag + "0"], sems["g_" + tag + "1"]]
    gr_bases = [grs[0].n, grs[1].n]
    v_base, s_base = v_e.n, s_e.n

    def g(gp):
        for sm, cnt in wait_marks:
            gp.wait_ge(sm.h, cnt)
        for i in range(NT):
            if i >= 2:
                gp.wait_ge(v_e.h, v_base + (i - 1))
            grs[i % 2].inc(gp.dma_start(stage[:, (i % 2) * D:(i % 2 + 1) * D],
                                        x_res_hbm[i * 128:(i + 1) * 128, :]), 16)
    blk.gpsimd(g)

    def v(ve):
        ve.memset(eps_col, EPS_LN)
        for sm, cnt in wait_marks:
            ve.wait_ge(sm.h, cnt)
        for i in range(NT):
            s = i % 2
            rs = rs2[:, s * D:(s + 1) * D]
            xm = x2t[:, s * D:(s + 1) * D]
            mu = stats[:, 0:1]
            ve.wait_ge(grs[i % 2].h, gr_bases[i % 2] + 16 * (i // 2 + 1))
            if i >= 2:
                ve.wait_ge(xos[i % 2].h, xo_bases[i % 2] + 16 * (i // 2))
                ve.wait_ge(s_e.h, s_base + (i - 1))
            ve.tensor_add(rs, acc_src_fn(i), stage[:, s * D:(s + 1) * D]); ve.drain()
            ve.reduce_sum(mu, rs, AX.X); ve.drain()
            ve.tensor_scalar(mu, mu, 1.0 / D, None, ALU.mult); ve.drain()
            v_e.inc(ve.tensor_scalar(xm, rs, mu, None, ALU.subtract))
    blk.vector(v)

    def sc(se):
        for i in range(NT):
            s = i % 2
            rs = rs2[:, s * D:(s + 1) * D]
            xm = x2t[:, s * D:(s + 1) * D]
            ss = stats[:, 1:2]
            lnv = stats[:, 2:3]
            rstd = stats[:, 3:4]
            se.wait_ge(v_e.h, v_base + i + 1)
            se.activation(sqd[:, :], xm, AF.Square, accum_out=ss)
            se.drain()
            se.activation(lnv, ss, AF.Ln, bias=eps_col, scale=1.0 / D)
            se.drain()
            se.activation(rstd, lnv, AF.Exp, scale=-0.5)
            se.drain()
            inst = se.activation(rs, xm, AF.Identity, scale=rstd)
            if out_extra is not None:
                out_extra(i, se, rs)
            s_e.inc(inst)
    blk.scalar(sc)

    def sy(sn):
        for i in range(NT):
            s = i % 2
            rs = rs2[:, s * D:(s + 1) * D]
            sn.wait_ge(s_e.h, s_base + i + 1)
            xos[i % 2].inc(sn.dma_start(out_hbm[i * 128:(i + 1) * 128, :], rs), 16)
        sn.wait_ge(xos[0].h, xo_bases[0] + 16 * (NT // 2))
        sn.wait_ge(xos[1].h, xo_bases[1] + 16 * (NT // 2))
    blk.sync(sy)


SEM_NAMES_COMMON = ["g_x", "t_x", "v_x", "g_w", "g_m", "t_m", "v_m", "s_m",
                    "t_a", "v_a", "s_a", "g_ep", "v_ep", "s_ep", "x_ep",
                    "t_b", "v_b", "s_b", "g_b", "x_b"]


def new_launch():
    nc = bass.Bass("TRN2", target_bir_lowering=False)
    return nc


def emit_gate_smallops(blk, rsc, rbB, c_all, pb, pbase, t_rt, v_rt, s_rt, NE,
                       renorm=True):
    v_base = v_rt.n

    def v2(ve):
        lg = rsc[:, 0:NE]; ex = rsc[:, 8:8 + NE]; pr = rsc[:, 16:16 + NE]
        m1 = rsc[:, 24:25]; nm = rsc[:, 25:26]; se_ = rsc[:, 26:27]
        rse = rsc[:, 27:28]; m2 = rsc[:, 28:29]; den = rsc[:, 29:30]
        rden = rsc[:, 30:31]; msk = rsc[:, 32:32 + NE]; p2 = rsc[:, 40:40 + NE]
        ve.wait_ge(t_rt.h, 1)
        v_rt.inc(ve.tensor_copy(rbB[:, 0:NE], pb[pbase][:, 0:NE]))
        ve.drain()
        for i in range(NT):
            bank = pb[pbase + (i + 1) % 2]
            ve.wait_ge(t_rt.h, i + 2)
            ve.tensor_add(lg, bank[:, 0:NE], rbB[:, 0:NE]); ve.drain()
            v_rt.inc(ve.reduce_max(nm, lg, AX.X, negate=True))
            ve.wait_ge(s_rt.h, i + 1)
            ve.reciprocal(rse, se_); ve.drain()
            ve.tensor_scalar_mul(pr, ex, rse); ve.drain()
            ve.reduce_max(m1, pr, AX.X); ve.drain()
            ve.tensor_scalar(msk, pr, m1, None, ALU.is_ge); ve.drain()
            ve.scalar_tensor_tensor(p2, msk, -1e30, pr, ALU.mult, ALU.add); ve.drain()
            ve.reduce_max(m2, p2, AX.X); ve.drain()
            if renorm:
                ve.tensor_add(den, m1, m2); ve.drain()
                ve.reciprocal(rden, den); ve.drain()
                ve.tensor_scalar(msk, pr, m2, None, ALU.is_ge); ve.drain()
                ve.scalar_tensor_tensor(c_all[:, i * NE:(i + 1) * NE],
                                        msk, rden, pr, ALU.mult, ALU.mult)
            else:
                ve.tensor_scalar(msk, pr, m2, None, ALU.is_ge); ve.drain()
                ve.tensor_mul(c_all[:, i * NE:(i + 1) * NE], msk, pr)
            v_rt.inc(ve.drain())
    blk.vector(v2)

    def s2(sc):
        lg = rsc[:, 0:NE]; ex = rsc[:, 8:8 + NE]
        nm = rsc[:, 25:26]; se_ = rsc[:, 26:27]
        for i in range(NT):
            sc.wait_ge(v_rt.h, v_base + 1 + 2 * i + 1)
            s_rt.inc(sc.activation(ex, lg, AF.Exp, bias=nm, scale=1.0, accum_out=se_))
    blk.scalar(s2)



def emit_l0ffn(nc, blk, ctx, dram, entry, shared=None):
    x1, rwT, rb_row, upT, dnT, ident, x2 = (
        dram["x1"], dram["rwT"], dram["rb_row"], dram["upT"], dram["dnT"],
        dram["ident"], dram["x2"])

    ctx2 = ExitStack()
    a = lambda nm, sh, dt: ctx2.enter_context(nc.sbuf_tensor("F_" + nm, sh, dt))
    xT = a("xT", [128, NC_D * TOK], BF16)
    stage = a("stage", [128, 2 * D], BF16)
    wub = [a(f"wub{i}", [128, NC_D * 2048], BF16) for i in range(2)]
    wdb = a("wdb", [128, 16 * D], BF16)
    hsil = a("hsil", [128, 16 * 512], BF16)
    acc = a("acc", [128, NT * D], BF16)
    rwn = a("rwn", [128, NC_D * 8], BF16)
    rbr = a("rbr", [1, 8], BF16)
    rbB = a("rbB", [128, 8], F32)
    ones1 = a("ones1", [1, 128], BF16)
    idb = a("idb", [128, 128], BF16)
    c_all = a("c_all", [128, NT * 8], F32)
    rsc = a("rsc", [128, 64], F32)
    rs2 = a("rs2", [128, 2 * D], F32)
    x2t = a("x2t", [128, 2 * D], F32)
    sqd = a("sqd", [128, D], F32)
    stats = a("stats", [128, 8], F32)
    epsc = a("epsc", [128, 1], F32)

    pb = [ctx2.enter_context(nc.psum_tensor(f"F_pb{i}", [128, 512], F32)) for i in range(8)]

    if True:
        sems = mk_block_sems(ctx, nc, "F_",
                ["g_x0", "g_x1", "t_x", "v_x", "g_m", "t_rt", "v_rt", "s_rt",
                 "g_wu0", "g_wu1", "g_wd", "t_h", "s_sil", "t_dn", "v_dn",
                 "g_ep0", "g_ep1", "v_ep", "s_ep", "x_ep0", "x_ep1"], shared)
        marks = {}
        emit_entry_barrier(blk, entry)

        g_m = sems["g_m"]

        def g0(gp):
            g_m.inc(gp.dma_start(idb[:, :], ident[:, :]), 16)
            g_m.inc(gp.dma_start(rwn[:, :].rearrange("p (c e) -> p c e", e=8),
                                 rwT[:, :].rearrange("(c p) e -> p c e", p=128)), 16)
            g_m.inc(gp.dma_start(rbr[:, :], rb_row[:, :]), 16)
        blk.gpsimd(g0)
        v_rt = sems["v_rt"]

        def v0(ve):
            v_rt.inc(ve.memset(ones1[:, :], 1.0))
        blk.vector(v0)
        ONES_OK = v_rt.n

        t_x = sems["t_x"]

        def tw(te):
            te.wait_ge(g_m.h, 48)
        blk.tensor(tw)
        emit_xT(nc, blk, {"g_x0": sems["g_x0"], "g_x1": sems["g_x1"],
                          "t_x": t_x, "v_x": sems["v_x"]},
                x1, xT, stage, idb, pb[0:2], marks, "x")
        XT = marks["xT_done_x"]

        t_rt, s_rt = sems["t_rt"], sems["s_rt"]

        def t2(te):
            te.wait_ge(g_m.h, 48)
            te.wait_ge(v_rt.h, ONES_OK)
            t_rt.inc(te.matmul(pb[2][:, 0:8], ones1[:, :], rbr[:, :], start=True, stop=True))
            te.wait_ge(XT[0].h, XT[1])
            for i in range(NT):
                bank = pb[2 + (i + 1) % 2]
                if i == 1:
                    te.wait_ge(v_rt.h, ONES_OK + 1)
                elif i >= 2:
                    te.wait_ge(v_rt.h, ONES_OK + 1 + 2 * (i - 2) + 1)
                for c in range(NC_D):
                    inst = te.matmul(bank[:, 0:8],
                                     xT[:, c * TOK + i * 128:c * TOK + (i + 1) * 128],
                                     rwn[:, c * 8:(c + 1) * 8],
                                     start=(c == 0), stop=(c == NC_D - 1))
                t_rt.inc(inst)
        blk.tensor(t2)

        emit_gate_smallops(blk, rsc, rbB, c_all, pb, 2, t_rt, v_rt, s_rt, 8)
        RT_V_DONE = v_rt.n

        g_wu = [sems["g_wu0"], sems["g_wu1"]]
        g_wd = sems["g_wd"]
        t_h, s_sil, t_dn, v_dn = (sems["t_h"], sems["s_sil"],
                                  sems["t_dn"], sems["v_dn"])

        def g3(gp):
            for ei in range(9):
                p = ei % 2
                if ei >= 2:
                    gp.wait_ge(t_h.h, 64 * (ei - 1))
                g_wu[p].inc(gp.dma_start(wub[p][:, :].rearrange("p (c f) -> p c f", f=2048),
                                         upT[ei, :, :].rearrange("(c p) f -> p c f", p=128)), 16)
                if ei >= 1:
                    gp.wait_ge(t_dn.h, 32 * ei)
                g_wd.inc(gp.dma_start(wdb[:, :].rearrange("p (c f) -> p c f", f=D),
                                      dnT[ei, :, :].rearrange("(c p) f -> p c f", p=128)), 16)
        blk.gpsimd(g3)

        def t3(te):
            te.wait_ge(v_rt.h, RT_V_DONE)
            hg = 0
            dg = 0
            for ei in range(9):
                te.wait_ge(g_wu[ei % 2].h, 16 * (ei // 2 + 1))
                for tg in range(4):
                    for fc in range(16):
                        bank = pb[4 + hg % 2]
                        if hg >= 2:
                            te.wait_ge(s_sil.h, hg - 1)
                        for c in range(NC_D):
                            inst = te.matmul(
                                bank[:, :],
                                wub[ei % 2][:, c * 2048 + fc * 128:c * 2048 + (fc + 1) * 128],
                                xT[:, c * TOK + tg * 512:c * TOK + (tg + 1) * 512],
                                start=(c == 0), stop=(c == NC_D - 1))
                        t_h.inc(inst)
                        hg += 1
                    te.wait_ge(g_wd.h, 16 * (ei + 1))
                    te.wait_ge(s_sil.h, hg)
                    for tl in range(4):
                        for nd in range(2):
                            bank = pb[6 + dg % 2]
                            if dg >= 2:
                                te.wait_ge(v_dn.h, dg - 1)
                            for fc in range(16):
                                inst = te.matmul(
                                    bank[:, :],
                                    hsil[:, fc * 512 + tl * 128:fc * 512 + (tl + 1) * 128],
                                    wdb[:, fc * D + nd * 512:fc * D + (nd + 1) * 512],
                                    start=(fc == 0), stop=(fc == 15))
                            t_dn.inc(inst)
                            dg += 1
        blk.tensor(t3)

        def s3(sc):
            hg = 0
            for ei in range(9):
                for tg in range(4):
                    gidx = ei * 4 + tg
                    for fc in range(16):
                        sc.wait_ge(t_h.h, hg + 1)
                        if gidx >= 1 and fc == 0:
                            sc.wait_ge(t_dn.h, (gidx - 1) * 8 + 8)
                        bank = pb[4 + hg % 2]
                        s_sil.inc(sc.activation(hsil[:, fc * 512:(fc + 1) * 512],
                                                bank[:, :], AF.Silu))
                        hg += 1
        blk.scalar(s3)

        def v3(ve):
            dg = 0
            for ei in range(9):
                for tg in range(4):
                    for tl in range(4):
                        tau = tg * 4 + tl
                        for nd in range(2):
                            ve.wait_ge(t_dn.h, dg + 1)
                            bank = pb[6 + dg % 2]
                            dst = acc[:, tau * D + nd * 512:tau * D + (nd + 1) * 512]
                            if ei == 0:
                                v_dn.inc(ve.tensor_copy(dst, bank[:, :]))
                            else:
                                ccol = c_all[:, tau * 8 + (ei - 1):tau * 8 + ei]
                                v_dn.inc(ve.scalar_tensor_tensor(
                                    dst, bank[:, :], ccol, dst, ALU.mult, ALU.add))
                            dg += 1
        blk.vector(v3)
        DN_DONE = (v_dn, v_dn.n + 0)
        DN_TOTAL = 9 * 32
        assert v_dn.n == DN_TOTAL

        emit_ln_epilogue(nc, blk,
                         {"g_ep0": sems["g_ep0"], "g_ep1": sems["g_ep1"],
                          "v_ep": sems["v_ep"], "s_ep": sems["s_ep"],
                          "x_ep0": sems["x_ep0"], "x_ep1": sems["x_ep1"]},
                         lambda i: acc[:, i * D:(i + 1) * D],
                         x1, x2, stage, rs2, x2t, sqd, stats, marks,
                         [(v_dn, DN_TOTAL)], "ep", eps_col=epsc[:, :])
    ctx2.close()
    return [(sems["x_ep0"], sems["x_ep0"].n), (sems["x_ep1"], sems["x_ep1"].n)]


def inmap_l0ffn(h, c, x1_chunks):
    return dict(x1=x1_chunks[c], rwT=h["rwT"], rb_row=h["rb_row"],
                upT=h["upT"], dnT=h["dnT"], ident=h["ident"])



def emit_l1ffn(nc, blk, ctx, dram, entry, shared=None):
    (x3, WgT, bg_row, WiT, WoT1, AT, BTc, ident, out) = (
        dram["x3"], dram["WgT"], dram["bg_row"], dram["WiT"], dram["WoT1"],
        dram["AT"], dram["BTc"], dram["ident"], dram["out"])

    ctx2 = ExitStack()
    a = lambda nm, sh, dt: ctx2.enter_context(nc.sbuf_tensor("G_" + nm, sh, dt))
    xT = a("xT", [128, NC_D * TOK], BF16)
    stage = a("stage", [128, 2 * D], BF16)
    wgn = a("wgn", [128, NC_D * 4], BF16)
    bgr = a("bgr", [1, 4], BF16)
    bgB = a("bgB", [128, 8], F32)
    ones1 = a("ones1", [1, 128], BF16)
    idb = a("idb", [128, 128], BF16)
    w_all = a("w_all", [128, NT * 4], F32)
    rsc = a("rsc", [128, 64], F32)
    atn = a("atn", [128, 4 * NC_D * 4], BF16)
    btn = a("btn", [128, 4 * 2048], BF16)
    xa = a("xa", [128, 4 * TOK], BF16)
    win = a("win", [128, NC_D * 2048], BF16)
    won = a("won", [128, 16 * D], BF16)
    baset = a("baset", [128, 2 * 2048], BF16)
    rbuf = a("rbuf", [128, 4 * 2048], BF16)
    hbt = a("hbt", [128, 2048], BF16)
    hbT = a("hbT", [128, 16 * 128], BF16)
    accl = a("accl", [128, NT * D], BF16)
    ep32 = win[:, :].bitcast(F32)
    rs2 = ep32[:, 0:2 * D]
    x2t = ep32[:, 2 * D:4 * D]
    sqd = ep32[:, 4 * D:5 * D]
    stats = a("stats", [128, 8], F32)
    epsc = a("epsc", [128, 1], F32)

    pb = [ctx2.enter_context(nc.psum_tensor(f"G_pb{i}", [128, 512], F32)) for i in range(8)]

    if True:
        sems = mk_block_sems(ctx, nc, "G_",
                ["g_x0", "g_x1", "t_x", "v_x", "g_m", "t_rt", "v_rt", "s_rt", "g_w",
                 "t_xa", "v_xa", "t_bs", "v_bs", "t_rl", "s_rl", "v_hb",
                 "t_tp", "v_tp", "t_dn", "v_dn",
                 "g_ep0", "g_ep1", "v_ep", "s_ep", "x_ep0", "x_ep1"], shared)
        marks = {}
        emit_entry_barrier(blk, entry)

        g_m = sems["g_m"]

        def g0(gp):
            g_m.inc(gp.dma_start(idb[:, :], ident[:, :]), 16)
            g_m.inc(gp.dma_start(wgn[:, :].rearrange("p (c e) -> p c e", e=4),
                                 WgT[:, :].rearrange("(c p) e -> p c e", p=128)), 16)
            g_m.inc(gp.dma_start(bgr[:, :], bg_row[:, :]), 16)
            g_m.inc(gp.dma_start(atn[:, :].rearrange("p (k r) -> p k r", r=4),
                                 AT[:, :].rearrange("(k p) r -> p k r", p=128)), 16)
        blk.gpsimd(g0)
        v_rt = sems["v_rt"]

        def v0(ve):
            v_rt.inc(ve.memset(ones1[:, :], 1.0))
        blk.vector(v0)

        def vpad(ve):
            ve.memset(xa[:, :], 0.0)
            ve.memset(btn[:, :], 0.0)
        blk.vector(vpad)

        t_x = sems["t_x"]

        def tw(te):
            te.wait_ge(g_m.h, 64)
        blk.tensor(tw)
        emit_xT(nc, blk, {"g_x0": sems["g_x0"], "g_x1": sems["g_x1"],
                          "t_x": t_x, "v_x": sems["v_x"]},
                x3, xT, stage, idb, pb[0:2], marks, "x")
        XT = marks["xT_done_x"]

        t_rt, s_rt = sems["t_rt"], sems["s_rt"]

        def t2(te):
            te.wait_ge(g_m.h, 64)
            te.wait_ge(v_rt.h, 1)
            t_rt.inc(te.matmul(pb[2][:, 0:4], ones1[:, :], bgr[:, :], start=True, stop=True))
            te.wait_ge(XT[0].h, XT[1])
            for i in range(NT):
                bank = pb[2 + (i + 1) % 2]
                if i == 1:
                    te.wait_ge(v_rt.h, 2)
                elif i >= 2:
                    te.wait_ge(v_rt.h, 2 + 2 * (i - 2) + 1)
                for c in range(NC_D):
                    inst = te.matmul(bank[:, 0:4],
                                     xT[:, c * TOK + i * 128:c * TOK + (i + 1) * 128],
                                     wgn[:, c * 4:(c + 1) * 4],
                                     start=(c == 0), stop=(c == NC_D - 1))
                t_rt.inc(inst)
        blk.tensor(t2)
        emit_gate_smallops(blk, rsc, bgB, w_all, pb, 2, t_rt, v_rt, s_rt, 4,
                           renorm=False)
        GT_V_DONE = v_rt.n

        t_xa, v_xa = sems["t_xa"], sems["v_xa"]

        def t3(te):
            te.wait_ge(g_m.h, 64)
            te.wait_ge(XT[0].h, XT[1])
            k = 0
            for e in range(4):
                for tg in range(4):
                    bank = pb[4 + k % 2]
                    if k >= 2:
                        te.wait_ge(v_xa.h, k - 1)
                    for c in range(NC_D):
                        inst = te.matmul(bank[0:4, :],
                                         atn[:, (e * NC_D + c) * 4:(e * NC_D + c + 1) * 4],
                                         xT[:, c * TOK + tg * 512:c * TOK + (tg + 1) * 512],
                                         start=(c == 0), stop=(c == NC_D - 1))
                    t_xa.inc(inst)
                    k += 1
        blk.tensor(t3)

        def v3(ve):
            k = 0
            for e in range(4):
                for tg in range(4):
                    ve.wait_ge(t_xa.h, k + 1)
                    bank = pb[4 + k % 2]
                    v_xa.inc(ve.tensor_copy(xa[0:4, e * TOK + tg * 512:e * TOK + (tg + 1) * 512],
                                            bank[0:4, :]))
                    k += 1
        blk.vector(v3)
        XA_DONE = v_xa.n

        g_w = sems["g_w"]
        t_bs, v_bs = sems["t_bs"], sems["v_bs"]
        t_rl, s_rl = sems["t_rl"], sems["s_rl"]
        v_hb = sems["v_hb"]
        t_tp, v_tp = sems["t_tp"], sems["v_tp"]
        t_dn, v_dn = sems["t_dn"], sems["v_dn"]

        def gw(gp):
            for fh in range(2):
                if fh == 1:
                    gp.wait_ge(t_dn.h, 32)
                g_w.inc(gp.dma_start(win[:, :].rearrange("p (c f) -> p c f", f=2048),
                                     WiT[:, fh * 2048:(fh + 1) * 2048]
                                     .rearrange("(c p) f -> p c f", p=128)), 16)
                g_w.inc(gp.dma_start(won[:, :].rearrange("p (c f) -> p c f", f=D),
                                     WoT1[fh * 2048:(fh + 1) * 2048, :]
                                     .rearrange("(c p) f -> p c f", p=128)), 16)
                if fh == 0:
                    gp.wait_ge(v_xa.h, 1)
                g_w.inc(gp.dma_start(btn[0:4, :].rearrange("p (e f) -> p e f", f=2048),
                                     BTc[:, :].rearrange("p (e f) -> p e f", f=4096)[:, :, fh * 2048:(fh + 1) * 2048]), 16)
        blk.gpsimd(gw)

        def t4(te):
            te.wait_ge(v_rt.h, GT_V_DONE)
            te.wait_ge(v_xa.h, XA_DONE)
            nb = 0
            nr = 0
            ntp = 0
            nd = 0
            for fh in range(2):
                te.wait_ge(g_w.h, 48 * fh + 48)
                for tau in range(NT):
                    for ft in range(4):
                        bank = pb[0 + nb % 2]
                        if nb >= 2:
                            te.wait_ge(v_bs.h, nb - 1)
                        for c in range(NC_D):
                            inst = te.matmul(
                                bank[:, :],
                                xT[:, c * TOK + tau * 128:c * TOK + (tau + 1) * 128],
                                win[:, c * 2048 + ft * 512:c * 2048 + (ft + 1) * 512],
                                start=(c == 0), stop=(c == NC_D - 1))
                        t_bs.inc(inst)
                        nb += 1
                    te.wait_ge(v_bs.h, nb)
                    for e in range(4):
                        for ft in range(4):
                            bank = pb[2 + nr % 2]
                            if nr >= 2:
                                te.wait_ge(s_rl.h, nr - 1)
                            slot = (fh * NT + tau) % 2
                            te.matmul(bank[:, :], idb[:, :],
                                      baset[:, slot * 2048 + ft * 512:
                                            slot * 2048 + (ft + 1) * 512],
                                      start=True, stop=False, skip_group_check=True)
                            inst = te.matmul(
                                bank[:, :],
                                xa[:, e * TOK + tau * 128:e * TOK + (tau + 1) * 128],
                                btn[:, e * 2048 + ft * 512:e * 2048 + (ft + 1) * 512],
                                start=False, stop=True, skip_group_check=True)
                            t_rl.inc(inst)
                            nr += 1
                    te.wait_ge(v_hb.h, (fh * NT + tau) + 1)
                    for tc in range(16):
                        bank = pb[4 + ntp % 2]
                        if ntp >= 2:
                            te.wait_ge(v_tp.h, ntp - 1)
                        dstp = bank[:, :].bitcast(BF16)[:, 0:128]
                        t_tp.inc(te.transpose(dstp, hbt[:, tc * 128:(tc + 1) * 128], idb[:, :]))
                        ntp += 1
                    te.wait_ge(v_tp.h, ntp)
                    for ndt in range(2):
                        bank = pb[6 + nd % 2]
                        if nd >= 2:
                            te.wait_ge(v_dn.h, nd - 1)
                        for tc in range(16):
                            inst = te.matmul(
                                bank[:, :],
                                hbT[:, tc * 128:(tc + 1) * 128],
                                won[:, tc * D + ndt * 512:tc * D + (ndt + 1) * 512],
                                start=(tc == 0), stop=(tc == 15))
                        t_dn.inc(inst)
                        nd += 1
        blk.tensor(t4)

        def s4(sc):
            nr = 0
            for fh in range(2):
                for tau in range(NT):
                    k = fh * NT + tau
                    for e in range(4):
                        wcol = w_all[:, tau * 4 + e:tau * 4 + e + 1]
                        for ft in range(4):
                            sc.wait_ge(t_rl.h, nr + 1)
                            if k >= 1 and e == 0 and ft == 0:
                                sc.wait_ge(v_hb.h, k)
                            bank = pb[2 + nr % 2]
                            s_rl.inc(sc.activation(
                                rbuf[:, e * 2048 + ft * 512:e * 2048 + (ft + 1) * 512],
                                bank[:, :], AF.Relu, scale=wcol))
                            nr += 1
        blk.scalar(s4)

        def v456(ve):
            nb = 0
            ntp = 0
            nd = 0
            for fh in range(2):
                for tau in range(NT):
                    k = fh * NT + tau
                    slot = k % 2
                    for ft in range(4):
                        ve.wait_ge(t_bs.h, nb + 1)
                        if k >= 2 and ft == 0:
                            ve.wait_ge(t_rl.h, 16 * (k - 1))
                        bank = pb[0 + nb % 2]
                        v_bs.inc(ve.tensor_copy(
                            baset[:, slot * 2048 + ft * 512:slot * 2048 + (ft + 1) * 512],
                            bank[:, :]))
                        nb += 1
                    ve.wait_ge(s_rl.h, 16 * (k + 1))
                    if k >= 1:
                        ve.wait_ge(t_tp.h, 16 * k)
                    ve.tensor_add(hbt[:, 0:2048], rbuf[:, 0:2048], rbuf[:, 2048:4096])
                    ve.drain()
                    ve.tensor_add(hbt[:, 0:2048], hbt[:, 0:2048], rbuf[:, 4096:6144])
                    ve.drain()
                    v_hb.inc(ve.tensor_add(hbt[:, 0:2048], hbt[:, 0:2048],
                                           rbuf[:, 6144:8192]))
                    for tc in range(16):
                        ve.wait_ge(t_tp.h, ntp + 1)
                        bank = pb[4 + ntp % 2]
                        srcp = bank[:, :].bitcast(BF16)[:, 0:128]
                        v_tp.inc(ve.tensor_copy(hbT[:, tc * 128:(tc + 1) * 128], srcp))
                        ntp += 1
                    for ndt in range(2):
                        ve.wait_ge(t_dn.h, nd + 1)
                        bank = pb[6 + nd % 2]
                        dst = accl[:, tau * D + ndt * 512:tau * D + (ndt + 1) * 512]
                        if fh == 0:
                            v_dn.inc(ve.tensor_copy(dst, bank[:, :]))
                        else:
                            v_dn.inc(ve.tensor_add(dst, dst, bank[:, :]))
                        nd += 1
        blk.vector(v456)

        DN_TOTAL = v_dn.n

        emit_ln_epilogue(nc, blk,
                         {"g_ep0": sems["g_ep0"], "g_ep1": sems["g_ep1"],
                          "v_ep": sems["v_ep"], "s_ep": sems["s_ep"],
                          "x_ep0": sems["x_ep0"], "x_ep1": sems["x_ep1"]},
                         lambda i: accl[:, i * D:(i + 1) * D],
                         x3, out, stage, rs2, x2t, sqd, stats, marks,
                         [(v_dn, DN_TOTAL)], "ep", eps_col=epsc[:, :])
    ctx2.close()
    return [(sems["x_ep0"], sems["x_ep0"].n), (sems["x_ep1"], sems["x_ep1"].n)]


def inmap_l1ffn(h, c, x3_chunks):
    return dict(x3=x3_chunks[c], WgT=h["WgT"], bg_row=h["bg_row"], WiT=h["WiT"],
                WoT1=h["WoT1"], AT=h["AT"], BTc=h["BTc"], ident=h["ident"])



def emit_l1attn(nc, blk, ctx, dram, entry, shared=None):
    TR = 9
    (x2, qwT, kwT, vwT, owT, qw_nat, qb1_pc, qbBt, kbBt, vbBt, obBt, grepT,
     grepb_t, gaBt, pbBt, ident, x3) = (
        dram["x2"], dram["qwT"], dram["kwT"], dram["vwT"], dram["owT"],
        dram["qw_nat"], dram["qb1_pc"], dram["qbBt"], dram["kbBt"],
        dram["vbBt"], dram["obBt"], dram["grepT"], dram["grepb_t"],
        dram["gaBt"], dram["pbBt"], dram["ident"], dram["x3"])

    SC = 0.125

    ctx2 = ExitStack()
    a = lambda nm, sh, dt: ctx2.enter_context(nc.sbuf_tensor("L_" + nm, sh, dt))
    xT = a("xT", [128, NC_D * TOK], BF16)
    stage = a("stage", [128, 2 * D], BF16)
    idb = a("idb", [128, 128], BF16)
    ones1 = a("ones1", [1, 128], BF16)
    qwn = a("qwn", [128, NC_D * D], BF16)
    grepb = a("grepb", [128, 8], BF16)
    qbc = a("qbc", [128, NC_D], BF16)
    grepbt = a("grepbt", [1, 128], F32)
    bgrow = a("bgrow", [1, 128], F32)
    bgrow16 = a("bgrow16", [1, 128], BF16)
    WGT = a("WGT", [128, NC_D * 128], BF16)
    biasGB = a("biasGB", [128, 128], F32)
    pbiasB = a("pbiasB", [128, 256], F32)
    grepaB = a("grepaB", [128, 16], F32)
    qbB = a("qbB", [128, D], F32)
    kbB = a("kbB", [128, D], F32)
    vbB = a("vbB", [128, D], F32)
    obB = a("obB", [128, D], F32)
    wdb = [a(f"wdb{i}", [128, NC_D * D], BF16) for i in range(2)]
    qb_ = a("qb_", [128, 8 * D], BF16)
    kb_ = a("kb_", [128, 8 * D], BF16)
    vb_ = a("vb_", [128, 8 * D], BF16)
    ctxb = a("ctxb", [128, 8 * D], BF16)
    ctxT = qwn[:, :]
    acc = a("acc", [128, NT * D], BF16)
    gates = a("gates", [128, 8 * 16], F32)
    g2b = a("g2b", [128, 32], F32)
    sgb = a("sgb", [128, 32], F32)
    scb = a("scb", [128, 2 * 256], F32)
    prb = a("prb", [128, 2 * 256], F32)
    nm64 = a("nm64", [128, 64], F32)
    s64 = a("s64", [128, 64], F32)
    tmpv = stage[:, 0:D]
    rs2 = kb_[:, :].bitcast(F32)[:, 0:2 * D]
    x2t = kb_[:, :].bitcast(F32)[:, 2 * D:4 * D]
    sqd = vb_[:, :].bitcast(F32)[:, 0:D]
    stats = a("stats", [128, 8], F32)
    epsc = a("epsc", [128, 1], F32)

    pb = [ctx2.enter_context(nc.psum_tensor(f"L_pb{i}", [128, 512], F32)) for i in range(8)]

    if True:
        sems = mk_block_sems(ctx, nc, "L_",
                ["g_x0", "g_x1", "t_x", "v_x", "g_m", "g_w0", "g_w1",
                 "t_wg", "v_wg", "t_pj", "v_pj", "t_g", "v_g", "s_g",
                 "v_sc", "s_sc", "v_cx", "t_tp", "v_tp", "t_ou", "v_ou",
                 "g_ep0", "g_ep1", "v_ep", "s_ep", "x_ep0", "x_ep1"], shared)
        marks = {}
        emit_entry_barrier(blk, entry)

        g_m = sems["g_m"]

        def g0(gp):
            g_m.inc(gp.dma_start(idb[:, :], ident[:, :]), 16)
            g_m.inc(gp.dma_start(qwn[:, :].rearrange("p (c f) -> p c f", f=D),
                                 qw_nat[:, :].rearrange("(c p) f -> p c f", p=128)), 16)
            g_m.inc(gp.dma_start(grepb[0:64, :], grepT[:, :]), 16)
            g_m.inc(gp.dma_start(grepb[64:128, :], grepT[:, :]), 16)
            g_m.inc(gp.dma_start(qbc[:, :], qb1_pc[:, :]), 16)
            g_m.inc(gp.dma_start(qbB[:, :], qbBt[:, :]), 16)
            g_m.inc(gp.dma_start(kbB[:, :], kbBt[:, :]), 16)
            g_m.inc(gp.dma_start(vbB[:, :], vbBt[:, :]), 16)
            g_m.inc(gp.dma_start(obB[:, :], obBt[:, :]), 16)
            g_m.inc(gp.dma_start(grepbt[:, :], grepb_t[:, :]), 16)
            g_m.inc(gp.dma_start(grepaB[:, :], gaBt[:, :]), 16)
            g_m.inc(gp.dma_start(pbiasB[:, :], pbBt[:, :]), 16)
        blk.gpsimd(g0)
        GM_ALL = g_m.n

        v_wg = sems["v_wg"]

        def v0(ve):
            ve.memset(ones1[:, :], 1.0)
            ve.wait_ge(g_m.h, GM_ALL)
            ve.tensor_scalar(qbB[:, :], qbB[:, :], SC, None, ALU.mult)
            v_wg.inc(ve.drain())
        blk.vector(v0)
        V0_DONE = v_wg.n

        t_x = sems["t_x"]

        def tw(te):
            te.wait_ge(g_m.h, GM_ALL)
        blk.tensor(tw)
        emit_xT(nc, blk, {"g_x0": sems["g_x0"], "g_x1": sems["g_x1"],
                          "t_x": t_x, "v_x": sems["v_x"]},
                x2, xT, stage, idb, pb[0:2], marks, "x")
        XT = marks["xT_done_x"]
        if TR <= 1:
            def fin(sn):
                sn.wait_ge(XT[0].h, XT[1])
            blk.sync(fin)
            return nc

        t_wg = sems["t_wg"]
        NB = 11

        SUB = int(os.environ.get("L1A_SUB", "3"))

        def t_wgf(te):
            te.wait_ge(v_wg.h, V0_DONE)
            for h in range(16):
                for dc in range(NC_D):
                    k = h * NC_D + dc
                    bank = pb[2 + k % 2]
                    if k >= 2:
                        te.wait_ge(v_wg.h, V0_DONE + k - 1)
                    lhsT = qwn[(h % 2) * 64:(h % 2) * 64 + 64, :]\
                        .rearrange("p (c f) -> p c f", f=D)[:, h // 2, dc * 128:(dc + 1) * 128]
                    t_wg.inc(te.matmul(bank[:, 0:8], lhsT,
                                       grepb[(h % 2) * 64:(h % 2) * 64 + 64, :],
                                       start=True, stop=True))
            if SUB <= 1:
                return
            for h in range(16):
                k2 = 128 + h
                bank = pb[2 + k2 % 2]
                te.wait_ge(v_wg.h, V0_DONE + k2 - 1)
                lhs_qb = qbc[(h % 2) * 64:(h % 2) * 64 + 64, :]\
                    .rearrange("p (c u) -> p c u", u=1)[:, h // 2, :]
                t_wg.inc(te.matmul(bank[0:1, 0:8], lhs_qb,
                                   grepb[(h % 2) * 64:(h % 2) * 64 + 64, :],
                                   start=True, stop=True))
            if SUB <= 2:
                return
            te.wait_ge(v_wg.h, V0_DONE + 144 + 2)
            t_wg.inc(te.matmul(pb[5][:, 0:128], ones1[:, :], bgrow16[:, :],
                               start=True, stop=True))
        blk.tensor(t_wgf)

        def v_wgf(ve):
            for h in range(16):
                for dc in range(NC_D):
                    k = h * NC_D + dc
                    ve.wait_ge(t_wg.h, k + 1)
                    bank = pb[2 + k % 2]
                    ve.tensor_scalar(WGT[:, dc * 128 + h * 8:dc * 128 + (h + 1) * 8],
                                     bank[:, 0:8], SC, None, ALU.mult)
                    v_wg.inc(ve.drain())
            if SUB <= 1:
                return
            for h in range(16):
                ve.wait_ge(t_wg.h, 128 + h + 1)
                bank = pb[2 + (128 + h) % 2]
                v_wg.inc(ve.tensor_copy(bgrow[0:1, h * 8:(h + 1) * 8], bank[0:1, 0:8]))
            if SUB <= 2:
                return
            ve.drain()
            ve.scalar_tensor_tensor(bgrow[0:1, :], bgrow[0:1, :],
                                    SC, grepbt[0:1, :], ALU.mult, ALU.add)
            ve.drain()
            ve.tensor_copy(bgrow16[0:1, :], bgrow[0:1, :])
            v_wg.inc(ve.drain())
            v_wg.inc(ve.drain())
            ve.wait_ge(t_wg.h, 145)
            v_wg.inc(ve.tensor_copy(biasGB[:, :], pb[5][:, 0:128]))
        blk.vector(v_wgf)
        WG_DONE = (v_wg, v_wg.n)
        if TR <= 2:
            def fin(sn):
                sn.wait_ge(v_wg.h, WG_DONE[1])
            blk.sync(fin)
            return nc

        g_w = [sems["g_w0"], sems["g_w1"]]
        t_pj, v_pj = sems["t_pj"], sems["v_pj"]
        t_g, v_g, s_g = sems["t_g"], sems["v_g"], sems["s_g"]
        v_sc, s_sc = sems["v_sc"], sems["s_sc"]
        v_cx = sems["v_cx"]
        t_tp, v_tp = sems["t_tp"], sems["v_tp"]
        t_ou, v_ou = sems["t_ou"], sems["v_ou"]

        def tauglob(ph, i, pc):
            return i * 4 + ph * 2 + pc

        def gw(gp):
            k = 0
            for ph in range(2):
                for wi, wt in enumerate([qwT, kwT, vwT, owT]):
                    p = k % 2
                    if k >= 2:
                        if k % 4 == 3:
                            gp.wait_ge(t_pj.h, (ph * 3 + 2) * 16)
                        elif wi == 0:
                            gp.wait_ge(t_pj.h, 3 * 16)
                        elif wi == 1:
                            gp.wait_ge(t_ou.h, 16)
                        elif wi == 2:
                            gp.wait_ge(t_pj.h, (ph * 3 + 1) * 16)
                    g_w[p].inc(gp.dma_start(wdb[p][:, :].rearrange("p (c f) -> p c f", f=D),
                                            wt[:, :].rearrange("(c p) f -> p c f", p=128)), 16)
                    k += 1
        blk.gpsimd(gw)

        def t_main(te):
            te.wait_ge(XT[0].h, XT[1])
            te.wait_ge(v_wg.h, WG_DONE[1])
            npj = 0
            ng = 0
            ntp = 0
            nou = 0
            for ph in range(2):
                if ph == 1 and TR <= 8:
                    break
                for wi in range(3):
                    kk = ph * 4 + wi
                    te.wait_ge(g_w[kk % 2].h, 16 * (kk // 2 + 1))
                    for tl in range(8):
                        i, pc = tl // 2, tl % 2
                        tg = tauglob(ph, i, pc)
                        for n2 in range(2):
                            bank = pb[0 + npj % 2]
                            if npj >= 2:
                                te.wait_ge(v_pj.h, npj - 1)
                            for c in range(NC_D):
                                inst = te.matmul(
                                    bank[:, :],
                                    xT[:, c * TOK + tg * 128:c * TOK + (tg + 1) * 128],
                                    wdb[kk % 2][:, c * D + n2 * 512:c * D + (n2 + 1) * 512],
                                    start=(c == 0), stop=(c == NC_D - 1))
                            t_pj.inc(inst)
                            npj += 1
                if TR <= 3:
                    continue
                for tl in range(8):
                    i, pc = tl // 2, tl % 2
                    tg = tauglob(ph, i, pc)
                    bank = pb[2 + ng % 2]
                    if ng >= 2:
                        te.wait_ge(v_g.h, 3 * (ng - 1))
                    for c in range(NC_D):
                        inst = te.matmul(bank[:, 0:128],
                                         xT[:, c * TOK + tg * 128:c * TOK + (tg + 1) * 128],
                                         WGT[:, c * 128:(c + 1) * 128],
                                         start=(c == 0), stop=(c == NC_D - 1))
                    t_g.inc(inst)
                    ng += 1
                if TR <= 6:
                    continue
                for tl in range(8):
                    te.wait_ge(v_cx.h, ph * 8 + tl + 1)
                    for dc in range(NC_D):
                        bank = pb[4 + ntp % 2]
                        if ntp >= 2:
                            te.wait_ge(v_tp.h, ntp - 1)
                        dstp = bank[:, :].bitcast(BF16)[:, 0:128]
                        t_tp.inc(te.transpose(dstp, ctxb[:, tl * D + dc * 128:tl * D + (dc + 1) * 128],
                                              idb[:, :]))
                        ntp += 1
                if TR <= 7:
                    continue
                kk_ow = ph * 4 + 3
                te.wait_ge(g_w[kk_ow % 2].h, 16 * (kk_ow // 2 + 1))
                for tl in range(8):
                    i, pc = tl // 2, tl % 2
                    tg = tauglob(ph, i, pc)
                    te.wait_ge(v_tp.h, ph * 64 + (tl + 1) * 8)
                    for n2 in range(2):
                        bank = pb[6 + nou % 2]
                        if nou >= 2:
                            te.wait_ge(v_ou.h, nou - 1)
                        for dc in range(NC_D):
                            inst = te.matmul(
                                bank[:, :],
                                ctxT[:, dc * D + tl * 128:dc * D + (tl + 1) * 128],
                                wdb[kk_ow % 2][:, dc * D + n2 * 512:dc * D + (n2 + 1) * 512],
                                start=(dc == 0), stop=(dc == NC_D - 1))
                        t_ou.inc(inst)
                        nou += 1
        blk.tensor(t_main)

        def v_main(ve):
            npj = 0
            ng = 0
            nsc = 0
            ntp = 0
            nou = 0
            for ph in range(2):
                if ph == 1 and TR <= 8:
                    break
                for wi in range(3):
                    dstb = [qb_, kb_, vb_][wi]
                    bB = [qbB, kbB, vbB][wi]
                    for tl in range(8):
                        for n2 in range(2):
                            ve.wait_ge(t_pj.h, npj + 1)
                            bank = pb[0 + npj % 2]
                            dst = dstb[:, tl * D + n2 * 512:tl * D + (n2 + 1) * 512]
                            if wi == 0:
                                ve.scalar_tensor_tensor(dst, bank[:, :], SC,
                                                        bB[:, n2 * 512:(n2 + 1) * 512],
                                                        ALU.mult, ALU.add)
                            else:
                                ve.tensor_add(dst, bank[:, :], bB[:, n2 * 512:(n2 + 1) * 512])
                            v_pj.inc(ve.drain())
                            npj += 1
                if TR <= 3:
                    continue
                for tl in range(8):
                    ve.wait_ge(t_g.h, ng + 1)
                    bank = pb[2 + ng % 2]
                    ve.tensor_add(tmpv[:, 0:128], bank[:, 0:128], biasGB[:, :])
                    ve.drain()
                    ve.tensor_reduce(g2b[:, :], tmpv[:, 0:128].rearrange("p (s f) -> p s f", f=4),
                                     AX.X, ALU.add)
                    v_g.inc(ve.drain())
                    ve.wait_ge(s_g.h, ng + 1)
                    ga = sgb[:, :].rearrange("p (h a) -> p h a", a=2)[:, :, 0]
                    gb = sgb[:, :].rearrange("p (h a) -> p h a", a=2)[:, :, 1]
                    ve.tensor_mul(tmpv[:, 0:16], gb, grepaB[:, :])
                    ve.drain()
                    ve.tensor_scalar(tmpv[:, 0:16], tmpv[:, 0:16], 1.0, None, ALU.subtract)
                    ve.drain()
                    ve.tensor_mul(tmpv[:, 0:16], ga, tmpv[:, 0:16])
                    v_g.inc(ve.drain())
                    ve.tensor_scalar(gates[:, tl * 16:(tl + 1) * 16], tmpv[:, 0:16],
                                     2.0, None, ALU.add)
                    v_g.inc(ve.drain())
                    ng += 1
                if TR <= 4:
                    continue
                for pc in range(2):
                    sc_ = scb[:, pc * 256:(pc + 1) * 256]
                    pr_ = prb[:, pc * 256:(pc + 1) * 256]
                    for i in range(4):
                        for j in range(4):
                            qt = qb_[:, (i * 2 + pc) * D:(i * 2 + pc + 1) * D]
                            kt = kb_[:, (j * 2 + pc) * D:(j * 2 + pc + 1) * D]
                            ve.tensor_mul(tmpv[:, :], qt, kt)
                            ve.drain()
                            ve.tensor_reduce(
                                sc_.rearrange("p (h x) -> p h x", x=16)[:, :, i * 4 + j],
                                tmpv[:, :].rearrange("p (h d) -> p h d", d=64),
                                AX.X, ALU.add)
                            ve.drain()
                    for i in range(4):
                        gslice = gates[:, (i * 2 + pc) * 16:(i * 2 + pc + 1) * 16]
                        gexp = gslice.rearrange("p (h u) -> p h u", u=1).to_broadcast([128, 16, 16])
                        pslice = pbiasB[:, :].rearrange("p (h x) -> p h x", x=16)
                        scv = sc_.rearrange("p (h x) -> p h x", x=16)
                        ve.tensor_mul(tmpv[:, 0:64].rearrange("p (h x) -> p h x", x=4),
                                      gexp[:, :, 0:4], pslice[:, :, i * 4:(i + 1) * 4])
                            
                        ve.drain()
                        ve.tensor_add(scv[:, :, i * 4:(i + 1) * 4],
                                      scv[:, :, i * 4:(i + 1) * 4],
                                      tmpv[:, 0:64].rearrange("p (h x) -> p h x", x=4))
                        ve.drain()
                    ve.tensor_reduce(nm64[:, :], sc_.rearrange("p (s j) -> p s j", j=4),
                                     AX.X, ALU.max, negate=True)
                    ve.drain()
                    ve.tensor_add(sc_.rearrange("p (s j) -> p s j", j=4),
                                  sc_.rearrange("p (s j) -> p s j", j=4),
                                  nm64[:, :].rearrange("p (s u) -> p s u", u=1)
                                  .to_broadcast([128, 64, 4]))
                    v_sc.inc(ve.drain())
                    ve.wait_ge(s_sc.h, ph * 2 + pc + 1)
                    ve.tensor_reduce(s64[:, :], pr_.rearrange("p (s j) -> p s j", j=4),
                                     AX.X, ALU.add)
                    ve.drain()
                    ve.reciprocal(s64[:, :], s64[:, :])
                    ve.drain()
                    ve.tensor_mul(pr_.rearrange("p (s j) -> p s j", j=4),
                                  pr_.rearrange("p (s j) -> p s j", j=4),
                                  s64[:, :].rearrange("p (s u) -> p s u", u=1)
                                  .to_broadcast([128, 64, 4]))
                    v_sc.inc(ve.drain())
                    nsc += 2
                if TR <= 5:
                    continue
                for tl in range(8):
                    i, pc = tl // 2, tl % 2
                    pr_ = prb[:, pc * 256:(pc + 1) * 256]
                    ctile = ctxb[:, tl * D:(tl + 1) * D]
                    for j in range(4):
                        pj = pr_.rearrange("p (h x) -> p h x", x=16)[:, :, i * 4 + j]
                        pjb = pj.rearrange("p (h u) -> p h u", u=1)\
                            .to_broadcast([128, 16, 64])
                        vt = vb_[:, (j * 2 + pc) * D:(j * 2 + pc + 1) * D]\
                            .rearrange("p (h d) -> p h d", d=64)
                        if j == 0:
                            ve.tensor_mul(ctile.rearrange("p (h d) -> p h d", d=64), vt, pjb)
                            ve.drain()
                        else:
                            ve.tensor_mul(tmpv[:, :].rearrange("p (h d) -> p h d", d=64), vt, pjb)
                            ve.drain()
                            if j < 3:
                                ve.tensor_add(ctile, ctile, tmpv[:, :])
                                ve.drain()
                            else:
                                v_cx.inc(ve.tensor_add(ctile, ctile, tmpv[:, :]))
                if TR <= 6:
                    continue
                for tl in range(8):
                    for dc in range(NC_D):
                        ve.wait_ge(t_tp.h, ntp + 1)
                        bank = pb[4 + ntp % 2]
                        srcp = bank[:, :].bitcast(BF16)[:, 0:128]
                        v_tp.inc(ve.tensor_copy(ctxT[:, dc * D + tl * 128:dc * D + (tl + 1) * 128],
                                                srcp))
                        ntp += 1
                if TR <= 7:
                    continue
                for tl in range(8):
                    i, pc = tl // 2, tl % 2
                    tg = tauglob(ph, i, pc)
                    for n2 in range(2):
                        ve.wait_ge(t_ou.h, nou + 1)
                        bank = pb[6 + nou % 2]
                        v_ou.inc(ve.tensor_add(
                            acc[:, tg * D + n2 * 512:tg * D + (n2 + 1) * 512],
                            bank[:, :], obB[:, n2 * 512:(n2 + 1) * 512]))
                        nou += 1
        blk.vector(v_main)

        def s_main(sc):
            ng = 0
            nsc = 0
            for ph in range(2):
                if ph == 1 and TR <= 8:
                    break
                if TR <= 3:
                    continue
                for tl in range(8):
                    sc.wait_ge(v_g.h, 3 * ng + 1)
                    s_g.inc(sc.activation(sgb[:, :], g2b[:, :], AF.Sigmoid))
                    ng += 1
                for pc in range(2):
                    if TR <= 4:
                        continue
                    sc.wait_ge(v_sc.h, nsc + 1)
                    s_sc.inc(sc.activation(prb[:, pc * 256:(pc + 1) * 256],
                                           scb[:, pc * 256:(pc + 1) * 256], AF.Exp))
                    nsc += 2
        blk.scalar(s_main)
        OU_TOTAL = v_ou.n
        if TR <= 8:
            def fin(sn):
                if v_ou.n:
                    sn.wait_ge(v_ou.h, v_ou.n)
                elif sems["v_cx"].n:
                    sn.wait_ge(sems["v_cx"].h, sems["v_cx"].n)
                elif sems["v_sc"].n:
                    sn.wait_ge(sems["v_sc"].h, sems["v_sc"].n)
                elif sems["v_g"].n:
                    sn.wait_ge(sems["v_g"].h, sems["v_g"].n)
                else:
                    sn.wait_ge(sems["v_pj"].h, sems["v_pj"].n)
            blk.sync(fin)
            return nc

        emit_ln_epilogue(nc, blk,
                         {"g_ep0": sems["g_ep0"], "g_ep1": sems["g_ep1"],
                          "v_ep": sems["v_ep"], "s_ep": sems["s_ep"],
                          "x_ep0": sems["x_ep0"], "x_ep1": sems["x_ep1"]},
                         lambda i: acc[:, i * D:(i + 1) * D],
                         x2, x3, stage, rs2, x2t, sqd, stats, marks,
                         [(v_ou, OU_TOTAL)], "ep", eps_col=epsc[:, :])
    ctx2.close()
    return [(sems["x_ep0"], sems["x_ep0"].n), (sems["x_ep1"], sems["x_ep1"].n)]


def inmap_l1attn(h, c, x2_chunks):
    return dict(x2=x2_chunks[c], qwT=h["qwT"], kwT=h["kwT"], vwT=h["vwT"],
                owT=h["owT"], qw_nat=h["qw_nat"], qb1_pc=h["qb1_pc"],
                qbBt=h["qbBt"], kbBt=h["kbBt"], vbBt=h["vbBt"], obBt=h["obBt"],
                grepT=h["grepT"], grepb_t=h["grepb_t"],
                gaBt=h["gaBt"], pbBt=h["pbBt"], ident=h["ident"])



def emit_l0attn(nc, blk, ctx, dram, entry, shared=None):
    NJ = 32
    (xTr, x_own, WaT, baQ, baK, baVt, snbT, WoTh, boT, relXs, relXA, relXB,
     chiT, augS, ident, x1, KTh, Vh, Eh, EhA, EhB) = (
        dram["xTr"], dram["x_own"], dram["WaT"], dram["baQ"], dram["baK"],
        dram["baVt"], dram["snbT"], dram["WoTh"], dram["boT"], dram["relXs"],
        dram["relXA"], dram["relXB"], dram["chiT"], dram["augS"], dram["ident"],
        dram["x1"], dram["KTh"], dram["Vh"], dram["Eh"], dram["EhA"], dram["EhB"])

    ctx2 = ExitStack()
    a = lambda nm, sh, dt: ctx2.enter_context(nc.sbuf_tensor("A_" + nm, sh, dt))
    wab = a("wab", [128, NC_D * 3 * D], BF16)
    xst = a("xst", [128, 2 * NC_D * 512], BF16)
    ktile = a("ktile", [128, 2 * NC_D * 512], BF16)
    vtile = a("vtile", [128, 2 * 4 * D], BF16)
    QTb = a("QTb", [128, NC_D * TOK], BF16)
    acc = a("acc", [128, NT * D], BF16)
    woH = a("woH", [128, 16 * D], BF16)
    KTb = [wab[:, 0:SEQ], wab[:, SEQ:2 * SEQ]]
    Vb = [wab[:, 8192:8192 + NJ * 65 + 63], wab[:, 10272:10272 + NJ * 65 + 63]]
    rhsT = [wab[:, 12352:12864], wab[:, 12864:13376]]
    tbuf = wab[:, 13376:15424]
    t2buf = wab[:, 15424:17472]
    augsb = wab[:, 17472:19520].bitcast(F32)
    etile = wab[:, 19520:21564].bitcast(F32)
    stage = wab[:, 21568:23616]
    wtile = [xst[:, 0:3072].bitcast(F32), xst[:, 3072:6144].bitcast(F32)]
    sqd = xst[:, 6144:8192].bitcast(F32)
    ctxP = ktile[0:64, 0:8192]
    rssc = vtile[:, 0:1024].bitcast(F32)
    rsb16 = vtile[:, 1024:1536]
    rs2 = vtile[:, 1600:5696].bitcast(F32)
    augb = a("augb", [128, 4 * 8], BF16)
    t2x = a("t2x", [128, 2048], BF16)
    tbx = a("tbx", [128, 2048], BF16)
    relXb = a("relXb", [128, 511 + 2 * 255], BF16)
    onesP = a("onesP", [128, 64], BF16)
    idb = a("idb", [128, 128], BF16)
    idf = a("idf", [128, 128], F32)
    baQs = a("baQs", [128, NC_D], F32)
    baKs = a("baKs", [128, NC_D], F32)
    baVb = a("baVb", [128, D], BF16)
    snb8 = a("snb8", [128, 1], F32)
    eps8 = a("eps8", [128, 1], F32)
    boTb = a("boTb", [128, D], BF16)
    x2t = QTb[:, 0:4096].bitcast(F32)
    stats = a("stats", [128, 8], F32)
    epsc = a("epsc", [128, 1], F32)

    scps = [ctx2.enter_context(nc.psum_tensor(f"A_scps{i}", [128, 1024], F32)) for i in range(2)]
    ctxps = ctx2.enter_context(nc.psum_tensor("A_ctxps", [128, 512], F32))
    miscps = [ctx2.enter_context(nc.psum_tensor(f"A_miscps{i}", [128, 512], F32)) for i in range(2)]
    outps = ctx2.enter_context(nc.psum_tensor("A_outps", [128, 512], F32))

    BH = B * 16

    if True:
        sems = mk_block_sems(ctx, nc, "A_", 
                ["g_m", "m_sy", "g_xs0", "g_xs1", "t_kv", "s_kt", "s_qt", "v_kv",
                 "x_kt0", "x_kt1", "x_v0", "x_v1",
                 "s_kl0", "s_kl1", "s_vl0", "s_vl1", "g_qr0", "g_qr1",
                 "g_ew0", "g_ew1", "g_er0", "g_er1",
                 "t_au", "v_au", "s_au", "t_qe", "v_qe",
                 "t_sc", "s_rl", "v_t2", "t_av", "v_cx", "s_dv",
                 "t_op", "v_op", "g_chi", "v_ms",
                 "g_ep0", "g_ep1", "v_ep", "s_ep", "x_ep0", "x_ep1"], shared)
        marks = {}
        emit_entry_barrier(blk, entry)
        g_m = sems["g_m"]

        def g0(gp):
            g_m.inc(gp.dma_start(wab[:, :].rearrange("p (c f) -> p c f", f=3 * D),
                                 WaT[:, :].rearrange("(c p) f -> p c f", p=128)), 16)
            g_m.inc(gp.dma_start(idb[:, :], ident[:, :]), 16)
            g_m.inc(gp.dma_start(augb[0:64, :].rearrange("p (s e) -> p s e", e=8),
                                 augS[:, :].rearrange("(s p) e -> p s e", p=64)), 16)
            g_m.inc(gp.dma_start(augb[64:128, :].rearrange("p (s e) -> p s e", e=8),
                                 augS[:, :].rearrange("(s p) e -> p s e", p=64)), 16)
            g_m.inc(gp.dma_start(relXb[0:64, 0:511], relXs[:, :]), 16)
            g_m.inc(gp.dma_start(relXb[64:128, 0:511], relXs[:, :]), 16)
            g_m.inc(gp.dma_start(relXb[0:64, 511:766], relXA[:, :]), 16)
            g_m.inc(gp.dma_start(relXb[64:128, 511:766], relXA[:, :]), 16)
            g_m.inc(gp.dma_start(relXb[0:64, 766:1021], relXB[:, :]), 16)
            g_m.inc(gp.dma_start(relXb[64:128, 766:1021], relXB[:, :]), 16)
            g_m.inc(gp.dma_start(woH[:, :], WoTh[:, :]), 16)
            g_m.inc(gp.dma_start(baVb[:, :], baVt[:, :]), 16)
            g_m.inc(gp.dma_start(boTb[:, :], boT[:, :]), 16)
        blk.gpsimd(g0)

        m_sy = sems["m_sy"]

        def sy0(sn):
            m_sy.inc(sn.dma_start(idf[:, :], ident[:, :]), 16)
            m_sy.inc(sn.dma_start(baQs[:, :], baQ[:, :]), 16)
            m_sy.inc(sn.dma_start(baKs[:, :], baK[:, :]), 16)
            m_sy.inc(sn.dma_start(snb8[:, :], snbT[:, :]), 16)
        blk.sync(sy0)
        GM_ALL = g_m.n
        MSY_ALL = m_sy.n
        v_kv = sems["v_kv"]

        def v0(ve):
            ve.wait_ge(g_m.h, GM_ALL)
            ve.wait_ge(m_sy.h, MSY_ALL)
            ve.memset(onesP[:, :], 1.0)
            ve.memset(eps8[:, :], 1e-8)
            ve.tensor_scalar(snb8[:, :], snb8[:, :], 8.0, None, ALU.mult)
            v_kv.inc(ve.drain())
        blk.vector(v0)
        V0_OK = v_kv.n

        g_xs = [sems["g_xs0"], sems["g_xs1"]]
        t_kv, s_kt, s_qt = sems["t_kv"], sems["s_kt"], sems["s_qt"]
        x_kt = [sems["x_kt0"], sems["x_kt1"]]
        x_v = [sems["x_v0"], sems["x_v1"]]
        OWN = (0, 8, 16, 24)

        def cum_t(i):
            return sum(16 + (8 if k in OWN else 0) for k in range(i + 1))

        def g1(gp):
            for i in range(32):
                p = i % 2
                if i >= 2:
                    gp.wait_ge(t_kv.h, cum_t(i - 1))
                g_xs[p].inc(gp.dma_start(
                    xst[:, p * NC_D * 512:(p + 1) * NC_D * 512]
                    .rearrange("p (c t) -> p c t", t=512),
                    xTr[:, i * 512:(i + 1) * 512].rearrange("(c p) t -> p c t", p=128)), 16)
        blk.gpsimd(g1)

        def t1(te):
            te.wait_ge(g_m.h, GM_ALL)
            te.wait_ge(m_sy.h, MSY_ALL)
            te.wait_ge(v_kv.h, V0_OK)
            nkt = 0
            nv = 0
            nqt = 0
            for i in range(32):
                p = i % 2
                te.wait_ge(g_xs[p].h, 16 * (i // 2 + 1))
                xb3 = xst[:, p * NC_D * 512:(p + 1) * NC_D * 512]\
                    .rearrange("p (c t) -> p c t", t=512)
                for fc in range(NC_D):
                    bank = scps[0][:, (nkt % 2) * 512:(nkt % 2) * 512 + 512]
                    if nkt >= 2:
                        te.wait_ge(s_kt.h, nkt - 1)
                    for c in range(NC_D):
                        inst = te.matmul(
                            bank,
                            wab[:, c * 3 * D + D + fc * 128:c * 3 * D + D + (fc + 1) * 128],
                            xb3[:, c, :],
                            start=(c == 0), stop=(c == NC_D - 1))
                    t_kv.inc(inst)
                    nkt += 1
                for tc in range(4):
                    for n2 in range(2):
                        bank = scps[1][:, (nv % 2) * 512:(nv % 2) * 512 + 512]
                        if nv >= 2:
                            te.wait_ge(v_kv.h, V0_OK + nv - 1)
                        for c in range(NC_D):
                            inst = te.matmul(
                                bank,
                                xb3[:, c, tc * 128:(tc + 1) * 128],
                                wab[:, c * 3 * D + 2 * D + n2 * 512:
                                    c * 3 * D + 2 * D + (n2 + 1) * 512],
                                start=(c == 0), stop=(c == NC_D - 1))
                        t_kv.inc(inst)
                        nv += 1
                if i in OWN:
                    bq = i // 8
                    for fc in range(NC_D):
                        bank = miscps[nqt % 2][:, :]
                        if nqt >= 2:
                            te.wait_ge(s_qt.h, nqt - 1)
                        for c in range(NC_D):
                            inst = te.matmul(
                                bank,
                                wab[:, c * 3 * D + fc * 128:c * 3 * D + (fc + 1) * 128],
                                xb3[:, c, :],
                                start=(c == 0), stop=(c == NC_D - 1))
                        t_kv.inc(inst)
                        nqt += 1
        blk.tensor(t1)

        def s1(sc):
            sc.wait_ge(m_sy.h, MSY_ALL)
            nkt = 0
            nqt = 0
            for i in range(32):
                p = i % 2
                base_tkv = cum_t(i - 1) if i > 0 else 0
                for fc in range(NC_D):
                    sc.wait_ge(t_kv.h, base_tkv + fc + 1)
                    if i >= 2 and fc == 0:
                        sc.wait_ge(x_kt[p].h, 16 * (i // 2))
                    bank = scps[0][:, (nkt % 2) * 512:(nkt % 2) * 512 + 512]
                    s_kt.inc(sc.activation(
                        ktile[:, p * NC_D * 512 + fc * 512:p * NC_D * 512 + (fc + 1) * 512],
                        bank, AF.Identity, bias=baKs[:, fc:fc + 1]))
                    nkt += 1
                if i in OWN:
                    bq = i // 8
                    for fc in range(NC_D):
                        sc.wait_ge(t_kv.h, base_tkv + 16 + fc + 1)
                        bank = miscps[nqt % 2][:, :]
                        s_qt.inc(sc.activation(
                            QTb[:, fc * TOK + bq * 512:fc * TOK + (bq + 1) * 512],
                            bank, AF.Identity, bias=baQs[:, fc:fc + 1]))
                        nqt += 1
        blk.scalar(s1)

        def v1f(ve):
            nv = 0
            for i in range(32):
                p = i % 2
                base_tkv = (cum_t(i - 1) if i > 0 else 0) + 8
                for g in range(8):
                    tc, n2 = g // 2, g % 2
                    ve.wait_ge(t_kv.h, base_tkv + g + 1)
                    if i >= 2 and g == 0:
                        ve.wait_ge(x_v[p].h, 16 * (i // 2))
                    bank = scps[1][:, (nv % 2) * 512:(nv % 2) * 512 + 512]
                    v_kv.inc(ve.tensor_add(
                        vtile[:, p * 4 * D + tc * D + n2 * 512:
                              p * 4 * D + tc * D + (n2 + 1) * 512],
                        bank, baVb[:, n2 * 512:(n2 + 1) * 512]))
                    nv += 1
        blk.vector(v1f)

        def sy1(sn):
            for i in range(32):
                p = i % 2
                sn.wait_ge(s_kt.h, 8 * (i + 1))
                x_kt[p].inc(sn.dma_start(
                    KTh[:, i * 512:(i + 1) * 512].rearrange("(c p) t -> p c t", p=128),
                    ktile[:, p * NC_D * 512:(p + 1) * NC_D * 512]
                    .rearrange("p (c t) -> p c t", t=512)), 16)
                sn.wait_ge(v_kv.h, V0_OK + 8 * (i + 1))
                x_v[p].inc(sn.dma_start(
                    Vh[i * 512:(i + 1) * 512, :].rearrange("(tc p) f -> p tc f", p=128),
                    vtile[:, p * 4 * D:(p + 1) * 4 * D]
                    .rearrange("p (tc f) -> p tc f", f=D)), 16)
        blk.sync(sy1)
        P1_KT = (x_kt[0].n, x_kt[1].n)
        P1_V = (x_v[0].n, x_v[1].n)
        P1_SQT = s_qt.n
        v_ms = sems["v_ms"]

        def vms(ve):
            ve.wait_ge(t_kv.h, cum_t(31))
            ve.wait_ge(s_qt.h, P1_SQT)
            v_ms.inc(ve.memset(wab[64:128, 0:2 * SEQ], 0.0))
            v_ms.inc(ve.memset(wab[64:128, 12352:13376], 0.0))
        blk.vector(vms)

        s_kl = [sems["s_kl0"], sems["s_kl1"]]
        s_vl = [sems["s_vl0"], sems["s_vl1"]]
        g_qr = [sems["g_qr0"], sems["g_qr1"]]
        g_ew = [sems["g_ew0"], sems["g_ew1"]]
        g_er = [sems["g_er0"], sems["g_er1"]]
        t_au, v_au, s_au = sems["t_au"], sems["v_au"], sems["s_au"]
        t_qe, v_qe = sems["t_qe"], sems["v_qe"]
        t_sc, s_rl, v_t2 = sems["t_sc"], sems["s_rl"], sems["v_t2"]
        t_av, v_cx, s_dv = sems["t_av"], sems["v_cx"], sems["s_dv"]
        t_op, v_op = sems["t_op"], sems["v_op"]

        band = {}
        for t_ in range(4):
            for dl in (-1, 0, 1):
                band.setdefault((t_ + dl) % NJ, []).append((t_, dl))
        BLKS = [(t_, dl) for t_ in range(4) for dl in (-1, 0, 1)]
        QSPECS = [(0, 0, 511), (1, 0, 511), (2, 0, 511), (3, 0, 511),
                  (0, 511, 255), (3, 766, 255)]

        def sy2(sn):
            sn.wait_ge(x_kt[0].h, P1_KT[0])
            sn.wait_ge(x_kt[1].h, P1_KT[1])
            sn.wait_ge(x_v[0].h, P1_V[0])
            sn.wait_ge(x_v[1].h, P1_V[1])
            for bh in range(BH):
                b, h = bh // 16, bh % 16
                pe = bh % 2
                if bh >= 2:
                    sn.wait_ge(t_sc.h, 32 * (bh - 1))
                s_kl[pe].inc(sn.dma_start(
                    KTb[pe][0:64, :], KTh[h * 64:(h + 1) * 64, b * SEQ:(b + 1) * SEQ]), 16)
                if bh >= 2:
                    sn.wait_ge(t_av.h, 32 * (bh - 1))
                s_vl[pe].inc(sn.dma_start(
                    Vb[pe][:, 0:NJ * 65].rearrange("p (j d) -> p j d", d=65)[:, :, 0:64],
                    Vh[b * SEQ:(b + 1) * SEQ, h * 64:(h + 1) * 64]
                    .rearrange("(j p) d -> p j d", p=128)), 16)
        blk.sync(sy2)

        def g2(gp):
            gp.wait_ge(s_qt.h, P1_SQT)
            gp.wait_ge(t_kv.h, cum_t(31))
            gp.wait_ge(sems["v_ms"].h, 2)
            g_chi = sems["g_chi"]
            g_chi.inc(gp.dma_start(KTb[0][64:72, :], chiT[:, :]), 16)
            g_chi.inc(gp.dma_start(KTb[1][64:72, :], chiT[:, :]), 16)
            for bh in range(BH):
                b, h = bh // 16, bh % 16
                pe = bh % 2
                for k in range(6):
                    gp.wait_ge(v_au.h, 10 * bh + 5 + k)
                    if bh >= 2 and k == 0:
                        gp.wait_ge(g_er[pe].h, 16 * 12 * (bh // 2))
                    if k < 4:
                        dst = bass.AP(Eh, pe * 512 * 511 + k * 128 * 511,
                                      [[511, 128], [1, 511]])
                        src = etile[:, (k % 2) * 511:(k % 2) * 511 + 511]
                    elif k == 4:
                        dst = bass.AP(EhA, pe * 128 * 255, [[255, 128], [1, 255]])
                        src = etile[:, 0:255]
                    else:
                        dst = bass.AP(EhB, pe * 128 * 255, [[255, 128], [1, 255]])
                        src = etile[:, 511:766]
                    g_ew[pe].inc(gp.dma_start(dst, src), 16)
                for bi, (t_, dl) in enumerate(BLKS):
                    if bi == 0:
                        gp.wait_ge(g_ew[pe].h, 16 * 6 * (bh // 2 + 1))
                        if bh >= 2:
                            gp.wait_ge(t_sc.h, 32 * (bh - 1))
                    if (t_, dl) == (0, -1):
                        srcap = bass.AP(EhA, pe * 128 * 255 + 127, [[254, 128], [1, 128]])
                    elif (t_, dl) == (3, 1):
                        srcap = bass.AP(EhB, pe * 128 * 255 + 127, [[254, 128], [1, 128]])
                    else:
                        srcap = bass.AP(Eh, pe * 512 * 511 + 128 * 511 * t_ + 128 * dl + 255,
                                        [[510, 128], [1, 128]])
                    g_er[pe].inc(gp.dma_start(
                        wtile[pe][:, bi * 128:(bi + 1) * 128], srcap), 16)
                if bh >= 2:
                    gp.wait_ge(t_sc.h, 32 * (bh - 1))
                g_qr[pe].inc(gp.dma_start(
                    rhsT[pe][0:64, :],
                    QTb[(h % 2) * 64:(h % 2) * 64 + 64,
                        (h // 2) * TOK + b * 512:(h // 2) * TOK + (b + 1) * 512]), 16)
        blk.gpsimd(g2)

        nmx_t = [0]

        def emit_augqr_step(te, bh, k):
            b, h = bh // 16, bh % 16
            hb = (h % 2) * 64

            def qsl(s):
                return QTb[hb:hb + 64, (h // 2) * TOK + b * 512 + s * 128:
                           (h // 2) * TOK + b * 512 + (s + 1) * 128]
            if k == 0 and bh >= 2:
                te.wait_ge(s_au.h, bh - 1)
            bank = miscps[nmx_t[0] % 2]
            if nmx_t[0] >= 2:
                te.wait_ge(v_au.h, nmx_t[0] - 1)
            if k < 4:
                t_au.inc(te.matmul(bank[64:72, 0:128],
                                   augb[hb:hb + 64, k * 8:(k + 1) * 8], qsl(k),
                                   start=True, stop=True))
            else:
                s, off, w = QSPECS[k - 4]
                t_qe.inc(te.matmul(bank[:, 0:w], qsl(s), relXb[hb:hb + 64, off:off + w],
                                   start=True, stop=True))
            nmx_t[0] += 1

        def emit_augqr(te, bh):
            for k in range(10):
                emit_augqr_step(te, bh, k)

        def tail_bh(te, pbh):
            pb_, ph = pbh // 16, pbh % 16
            te.wait_ge(s_dv.h, pbh + 1)
            te.wait_ge(v_op.h, 8 * (pbh // 16))
            t_op.inc(te.matmul(outps[0:64, :], onesP[64:65, 0:64],
                               rsb16[64:65, :], start=True, stop=True))
            if ph == 15:
                te.wait_ge(v_cx.h, 2 * (pbh + 1))
                for nop in range(8):
                    qs, n2 = nop // 2, nop % 2
                    te.wait_ge(v_op.h, 8 * pb_ + nop)
                    for hh in range(16):
                        inst = te.matmul(
                            outps[:, :],
                            ktile[:, hh * 512 + qs * 128:hh * 512 + (qs + 1) * 128],
                            woH[:, hh * D + n2 * 512:hh * D + (n2 + 1) * 512],
                            start=(hh == 0), stop=(hh == 15))
                    t_op.inc(inst)

        def t2f(te):
            te.wait_ge(s_qt.h, P1_SQT)
            te.wait_ge(sems["v_ms"].h, 2)
            te.wait_ge(s_kt.h, 8 * 32)
            te.wait_ge(v_kv.h, V0_OK + 8 * 32)
            te.wait_ge(sems["g_chi"].h, 32)
            emit_augqr(te, 0)
            emit_augqr(te, 1)
            for bh in range(BH):
                b, h = bh // 16, bh % 16
                pe = bh % 2
                te.wait_ge(s_kl[pe].h, 16 * (bh // 2 + 1))
                te.wait_ge(g_qr[pe].h, 16 * (bh // 2 + 1))
                te.wait_ge(s_au.h, bh + 1)
                te.wait_ge(s_vl[pe].h, 16 * (bh // 2 + 1))
                need_w = True
                for g in range(16):
                    for j in (2 * g, 2 * g + 1):
                        grp = scps[(j // 2) % 2][:, (j % 2) * 512:(j % 2) * 512 + 512]
                        if j % 2 == 0 and (16 * bh + g) >= 2:
                            te.wait_ge(s_rl.h, 16 * bh + g - 1)
                        isband = j in band
                        if isband:
                            if need_w:
                                te.wait_ge(g_er[pe].h, 16 * 12 * (bh // 2 + 1))
                                need_w = False
                            covered = set()
                            for (t_, dl) in band[j]:
                                bi = BLKS.index((t_, dl))
                                covered.add(t_)
                                te.transpose(grp[:, t_ * 128:(t_ + 1) * 128],
                                             wtile[pe][:, bi * 128:(bi + 1) * 128],
                                             idf[:, :])
                            for t_ in range(4):
                                inst = te.matmul(
                                    grp[:, t_ * 128:(t_ + 1) * 128],
                                    KTb[pe][:, j * 128:(j + 1) * 128],
                                    rhsT[pe][:, t_ * 128:(t_ + 1) * 128],
                                    start=(t_ not in covered), stop=True,
                                    skip_group_check=True)
                        else:
                            inst = te.matmul(grp, KTb[pe][:, j * 128:(j + 1) * 128],
                                             rhsT[pe][:, :],
                                             start=True, stop=True,
                                             skip_group_check=True)
                        t_sc.inc(inst)
                    if g >= 2:
                        for j in (2 * (g - 2), 2 * (g - 2) + 1):
                            te.wait_ge(v_t2.h, 16 * bh + g - 1)
                            if j == 0 and bh >= 1:
                                te.wait_ge(v_cx.h, 2 * bh)
                            k2 = j // 2
                            t2s = t2buf if k2 % 4 < 2 else t2x
                            inst = te.matmul(
                                ctxps[:, :], Vb[pe][:, j * 65:j * 65 + 128],
                                t2s[:, (k2 % 2) * 1024 + (j % 2) * 512:
                                    (k2 % 2) * 1024 + (j % 2) * 512 + 512],
                                start=(j == 0), stop=False, skip_group_check=True)
                            t_av.inc(inst)
                    if g < 10 and bh + 2 < BH:
                        emit_augqr_step(te, bh + 2, g)
                    if g == 1 and bh >= 1:
                        tail_bh(te, bh - 1)
                for k2 in (14, 15):
                    for j in (2 * k2, 2 * k2 + 1):
                        te.wait_ge(v_t2.h, 16 * bh + k2 + 1)
                        t2s = t2buf if k2 % 4 < 2 else t2x
                        inst = te.matmul(
                            ctxps[:, :], Vb[pe][:, j * 65:j * 65 + 128],
                            t2s[:, (k2 % 2) * 1024 + (j % 2) * 512:
                                (k2 % 2) * 1024 + (j % 2) * 512 + 512],
                            start=False, stop=(j == 31), skip_group_check=True)
                        t_av.inc(inst)
            tail_bh(te, BH - 1)
        blk.tensor(t2f)

        def s2f(sc):
            for bh in range(BH):
                b, h = bh // 16, bh % 16
                pe = bh % 2
                sc.wait_ge(v_au.h, 10 * bh + 4)
                if bh >= 2:
                    sc.wait_ge(t_sc.h, 32 * (bh - 1))
                s_au.inc(sc.activation(rhsT[pe][64:72, :],
                                       augsb[64:72, (bh % 2) * 512:(bh % 2) * 512 + 512],
                                       AF.Identity, bias=snb8[64:72, :]))
                for g in range(16):
                    sc.wait_ge(t_sc.h, 32 * bh + 2 * (g + 1))
                    gg = 16 * bh + g
                    if gg >= 4:
                        sc.wait_ge(v_t2.h, gg - 3)
                    tbs = tbuf if g % 4 < 2 else tbx
                    s_rl.inc(sc.activation(tbs[:, (g % 2) * 1024:(g % 2) * 1024 + 1024],
                                           scps[g % 2][:, :], AF.Relu, scale=0.125))
                sc.wait_ge(v_cx.h, 2 * bh + 1)
                sc.activation(rssc[64:65, :], ctxps[64:65, :], AF.Ln, bias=eps8[64:65, :])
                sc.drain()
                sc.activation(rssc[64:65, :], rssc[64:65, :], AF.Exp, scale=-1.0)
                sc.drain()
                s_dv.inc(sc.activation(rsb16[64:65, :], rssc[64:65, :], AF.Identity))
        blk.scalar(s2f)

        nmx_v = [0]
        nau_c = [0]
        nqe_c = [0]

        def v2f(ve):
            def ev_augqr_step(bh, k):
                pe = bh % 2
                bank = miscps[nmx_v[0] % 2]
                if k < 4:
                    ve.wait_ge(t_au.h, nau_c[0] + 1)
                    v_au.inc(ve.tensor_copy(
                        augsb[64:72, (bh % 2) * 512 + k * 128:(bh % 2) * 512 + (k + 1) * 128],
                        bank[64:72, 0:128]))
                    nau_c[0] += 1
                else:
                    kk = k - 4
                    s, off, w = QSPECS[kk]
                    ve.wait_ge(t_qe.h, nqe_c[0] + 1)
                    ew_total = 6 * (bh // 2) + kk
                    if ew_total >= 1:
                        ve.wait_ge(g_ew[pe].h, 16 * ew_total)
                    dsto = (kk % 2) * 511 if kk < 4 else (0 if kk == 4 else 511)
                    v_au.inc(ve.tensor_copy(etile[:, dsto:dsto + w], bank[:, 0:w]))
                    nqe_c[0] += 1
                nmx_v[0] += 1

            def ev_augqr(bh):
                for k in range(10):
                    ev_augqr_step(bh, k)

            ve.wait_ge(t_kv.h, cum_t(31))
            ve.memset(Vb[0][:, 0:NJ * 65].rearrange("p (j d) -> p j d", d=65)[:, :, 64:65], 1.0)
            ve.memset(Vb[1][:, 0:NJ * 65].rearrange("p (j d) -> p j d", d=65)[:, :, 64:65], 1.0)
            ve.drain()
            ev_augqr(0)
            ev_augqr(1)
            for bh in range(BH):
                b, h = bh // 16, bh % 16
                pe = bh % 2
                for g in range(16):
                    gg = 16 * bh + g
                    ve.wait_ge(s_rl.h, gg + 1)
                    war = 32 * bh + 2 * g - 6
                    if war > 0:
                        ve.wait_ge(t_av.h, war)
                    t2s = t2buf if g % 4 < 2 else t2x
                    tbs = tbuf if g % 4 < 2 else tbx
                    v_t2.inc(ve.tensor_mul(
                        t2s[:, (g % 2) * 1024:(g % 2) * 1024 + 1024],
                        tbs[:, (g % 2) * 1024:(g % 2) * 1024 + 1024],
                        tbs[:, (g % 2) * 1024:(g % 2) * 1024 + 1024]))
                    if g < 10 and bh + 2 < BH:
                        ev_augqr_step(bh + 2, g)
                ve.wait_ge(t_av.h, 32 * (bh + 1))
                v_cx.inc(ve.tensor_copy(ctxP[:, h * 512:(h + 1) * 512], ctxps[0:64, :]))
                ve.wait_ge(t_op.h, bh + 1 + 8 * b)
                v_cx.inc(ve.tensor_mul(ctxP[:, h * 512:(h + 1) * 512],
                                       ctxP[:, h * 512:(h + 1) * 512], outps[0:64, :]))
                if h == 15:
                    for nop in range(8):
                        qs, n2 = nop // 2, nop % 2
                        ve.wait_ge(t_op.h, bh + 2 + 8 * b + nop)
                        tau = b * 4 + qs
                        v_op.inc(ve.tensor_add(
                            acc[:, tau * D + n2 * 512:tau * D + (n2 + 1) * 512],
                            outps[:, :], boTb[:, n2 * 512:(n2 + 1) * 512]))
        blk.vector(v2f)
        OP_TOTAL = v_op.n

        emit_ln_epilogue(nc, blk,
                         {"g_ep0": sems["g_ep0"], "g_ep1": sems["g_ep1"],
                          "v_ep": sems["v_ep"], "s_ep": sems["s_ep"],
                          "x_ep0": sems["x_ep0"], "x_ep1": sems["x_ep1"]},
                         lambda i: acc[:, i * D:(i + 1) * D],
                         x_own, x1, stage, rs2, x2t, sqd, stats, marks,
                         [(v_op, OP_TOTAL)], "ep", eps_col=epsc[:, :])
    ctx2.close()
    return [(sems["x_ep0"], sems["x_ep0"].n), (sems["x_ep1"], sems["x_ep1"].n)]


def inmap_l0attn(h, c):
    return dict(xTr=h["xT_rot"][c], x_own=h["x_own"][c], WaT=h["WaT"],
                baQ=h["baQ"], baK=h["baK"], baVt=h["baVt"], snbT=h["snbT"],
                WoTh=h["WoTh"], boT=h["boT"], relXs=h["relXT_std"],
                relXA=h["relXT_wA"][c], relXB=h["relXT_wB"][c],
                chiT=h["chi"][c], augS=h["augsel"][c], ident=h["ident"])


def mk_block_sems(ctx, nc, px, names, shared):
    sems = {}
    for n in names:
        if shared and n in shared:
            sems[n] = shared[n]
        else:
            sems[n] = Sem(ctx.enter_context(nc.semaphore(px + n)))
    return sems


def emit_entry_barrier(blk, marks):
    if not marks:
        return

    def f(e):
        for s, n in marks:
            e.wait_ge(s.h, n)
    for reg in (blk.gpsimd, blk.sync, blk.tensor, blk.vector, blk.scalar):
        reg(f)


def build_fused():
    nc = new_launch()
    d = {}
    DT = nc.dram_tensor
    d["xTr"] = DT("xTr", [D, B * SEQ], F32, kind="ExternalInput")
    d["x_own"] = DT("x_own", [TOK, D], F32, kind="ExternalInput")
    d["WaT"] = DT("WaT", [D, 3 * D], F32, kind="ExternalInput")
    d["baQ"] = DT("baQ", [128, NC_D], F32, kind="ExternalInput")
    d["baK"] = DT("baK", [128, NC_D], F32, kind="ExternalInput")
    d["baVt"] = DT("baVt", [128, D], F32, kind="ExternalInput")
    d["snbT"] = DT("snbT", [128, 1], F32, kind="ExternalInput")
    d["WoTh"] = DT("WoTh", [128, 16 * D], F32, kind="ExternalInput")
    d["boT"] = DT("boT", [128, D], F32, kind="ExternalInput")
    d["relXs"] = DT("relXs", [64, 511], F32, kind="ExternalInput")
    d["relXA"] = DT("relXA", [64, 255], F32, kind="ExternalInput")
    d["relXB"] = DT("relXB", [64, 255], F32, kind="ExternalInput")
    d["chiT"] = DT("chiT", [8, SEQ], F32, kind="ExternalInput")
    d["augS"] = DT("augS", [4 * 64, 8], F32, kind="ExternalInput")
    d["ident"] = DT("ident", [128, 128], F32, kind="ExternalInput")
    d["rwT"] = DT("rwT", [D, 8], F32, kind="ExternalInput")
    d["rb_row"] = DT("rb_row", [1, 8], F32, kind="ExternalInput")
    d["upT"] = DT("upT", [9, D, 2048], F32, kind="ExternalInput")
    d["dnT"] = DT("dnT", [9, 2048, D], F32, kind="ExternalInput")
    d["qwT"] = DT("qwT", [D, D], F32, kind="ExternalInput")
    d["kwT"] = DT("kwT", [D, D], F32, kind="ExternalInput")
    d["vwT"] = DT("vwT", [D, D], F32, kind="ExternalInput")
    d["owT"] = DT("owT", [D, D], F32, kind="ExternalInput")
    d["qw_nat"] = DT("qw_nat", [D, D], F32, kind="ExternalInput")
    d["qb1_pc"] = DT("qb1_pc", [128, NC_D], F32, kind="ExternalInput")
    d["qbBt"] = DT("qbBt", [128, D], F32, kind="ExternalInput")
    d["kbBt"] = DT("kbBt", [128, D], F32, kind="ExternalInput")
    d["vbBt"] = DT("vbBt", [128, D], F32, kind="ExternalInput")
    d["obBt"] = DT("obBt", [128, D], F32, kind="ExternalInput")
    d["grepT"] = DT("grepT", [64, 8], F32, kind="ExternalInput")
    d["grepb_t"] = DT("grepb_t", [1, 128], F32, kind="ExternalInput")
    d["gaBt"] = DT("gaBt", [128, 16], F32, kind="ExternalInput")
    d["pbBt"] = DT("pbBt", [128, 256], F32, kind="ExternalInput")
    d["WgT"] = DT("WgT", [D, 4], F32, kind="ExternalInput")
    d["bg_row"] = DT("bg_row", [1, 4], F32, kind="ExternalInput")
    d["WiT"] = DT("WiT", [D, 4096], F32, kind="ExternalInput")
    d["WoT1"] = DT("WoT1", [4096, D], F32, kind="ExternalInput")
    d["AT"] = DT("AT", [4 * D, 4], F32, kind="ExternalInput")
    d["BTc"] = DT("BTc", [4, 4 * 4096], F32, kind="ExternalInput")
    d["out"] = DT("out", [TOK, D], F32, kind="ExternalOutput")
    d["x1"] = DT("x1", [TOK, D], F32, kind="Internal")
    d["x2"] = DT("x2", [TOK, D], F32, kind="Internal")
    d["x3"] = DT("x3", [TOK, D], F32, kind="Internal")
    d["KTh"] = DT("KTh", [D, B * SEQ], BF16, kind="Internal")
    d["Vh"] = DT("Vh", [B * SEQ, D], BF16, kind="Internal")
    d["Eh"] = DT("Eh", [2 * 512 * 511], F32, kind="Internal")
    d["EhA"] = DT("EhA", [2 * 128 * 255], F32, kind="Internal")
    d["EhB"] = DT("EhB", [2 * 128 * 255], F32, kind="Internal")

    with ExitStack() as ctx:
        blk = ctx.enter_context(nc.Block())
        shared = {n: Sem(ctx.enter_context(nc.semaphore("S_" + n))) for n in
                  ["g_ep0", "g_ep1", "v_ep", "s_ep", "x_ep0", "x_ep1",
                   "g_x0", "g_x1", "t_x", "v_x"]}
        m = emit_l0attn(nc, blk, ctx, d, [], shared)
        m = emit_l0ffn(nc, blk, ctx, d, m, shared)
        m = emit_l1attn(nc, blk, ctx, d, m, shared)
        m = emit_l1ffn(nc, blk, ctx, d, m, shared)
    return nc


def inmap_fused(h, c):
    im = dict(xTr=h["xT_rot"][c], x_own=h["x_own"][c], WaT=h["WaT"],
              baQ=h["baQ"], baK=h["baK"], baVt=h["baVt"], snbT=h["snbT"],
              WoTh=h["WoTh"], boT=h["boT"], relXs=h["relXT_std"],
              relXA=h["relXT_wA"][c], relXB=h["relXT_wB"][c],
              chiT=h["chi"][c], augS=h["augsel"][c], ident=h["ident"],
              rwT=h["rwT"], rb_row=h["rb_row"], upT=h["upT"], dnT=h["dnT"],
              qwT=h["qwT"], kwT=h["kwT"], vwT=h["vwT"], owT=h["owT"],
              qw_nat=h["qw_nat"], qb1_pc=h["qb1_pc"], qbBt=h["qbBt"],
              kbBt=h["kbBt"], vbBt=h["vbBt"], obBt=h["obBt"], grepT=h["grepT"],
              grepb_t=h["grepb_t"], gaBt=h["gaBt"], pbBt=h["pbBt"],
              WgT=h["WgT"], bg_row=h["bg_row"], WiT=h["WiT"], WoT1=h["WoT1"],
              AT=h["AT"], BTc=h["BTc"])
    return im



def kernel(**inputs):
    h = prep_host(inputs)
    cores = list(range(NCORES))
    nc = build_fused()
    res = run_bass_kernel_spmd(nc, [inmap_fused(h, c) for c in cores], core_ids=cores)
    outc = [res.results[c]["out"] for c in cores]
    out = np.concatenate([o.reshape(B, POS, D) for o in outc], axis=1)
    return np.ascontiguousarray(out, dtype=np.float32)


# version 10
# speedup vs baseline: 1.0460x; 1.0460x over previous
import os
from contextlib import ExitStack

import numpy as np
import ml_dtypes
import concourse.bass as bass
import concourse.mybir as mybir
from concourse.bass_utils import run_bass_kernel_spmd

F32 = mybir.dt.float32
BF16 = mybir.dt.bfloat16
AF = mybir.ActivationFunctionType
ALU = mybir.AluOpType
AX = mybir.AxisListType

NCORES = 8
TOK = 2048
NT = TOK // 128
D = 1024
NC_D = D // 128
B = 4
SEQ = 4096
POS = SEQ // NCORES
EPS_LN = 1e-5


class Sem:
    def __init__(self, h):
        self.h = h
        self.n = 0

    def inc(self, inst, k=1):
        inst.then_inc(self.h, k)
        self.n += k
        return self.n



def _f32(x):
    return np.ascontiguousarray(x, dtype=np.float32)


def prep_host(inp):
    h = {}
    x = np.asarray(inp["L0_attn_iQ"], np.float32)
    xt3 = np.ascontiguousarray(x.reshape(B * SEQ, D).T).reshape(D, B, SEQ)
    h["xT_rot"] = [np.ascontiguousarray(np.roll(xt3, -POS * c, axis=2).reshape(D, B * SEQ))
                   for c in range(NCORES)]
    h["x_own"] = [np.ascontiguousarray(x[:, POS * c:POS * (c + 1), :].reshape(TOK, D))
                  for c in range(NCORES)]

    ba = np.asarray(inp["L0_attn_ba"], np.float32)
    Wo0 = np.asarray(inp["L0_attn_Wo"], np.float32)
    h["WaT"] = _f32(np.asarray(inp["L0_attn_Wa"]).T)
    h["baQ"] = _f32(ba[0:D].reshape(NC_D, 128).T)
    h["baK"] = _f32(ba[D:2 * D].reshape(NC_D, 128).T)
    h["baVt"] = _f32(np.tile(ba[2 * D:3 * D].reshape(1, -1), (128, 1)))
    h["snbT"] = _f32(np.tile(np.asarray(inp["L0_attn_sn_bias"]).reshape(1, 1), (128, 1)))
    h["WoTh"] = _f32(np.concatenate(
        [Wo0.T.reshape(16, 64, D).transpose(1, 0, 2).reshape(64, 16 * D),
         np.zeros((64, 16 * D), np.float32)], axis=0))
    h["boT"] = _f32(np.tile(np.asarray(inp["L0_attn_bo"]).reshape(1, -1), (128, 1)))
    rel = np.asarray(inp["L0_attn_rel_emb"], np.float32)
    idx = np.clip(np.arange(511) - 255, -16, 16) + 16
    h["relXT_std"] = _f32(rel[idx].T)
    idxA = np.clip(np.arange(255) - 255, -16, 16) + 16
    idxB = np.clip(np.arange(255) + 1, -16, 16) + 16
    h["relXT_wA"] = [_f32(rel[np.full(255, 32) if c == 0 else idxA].T) for c in range(NCORES)]
    h["relXT_wB"] = [_f32(rel[np.full(255, 0) if c == NCORES - 1 else idxB].T) for c in range(NCORES)]
    NJ = SEQ // 128
    h["chi"] = []
    h["augsel"] = []
    for c in range(NCORES):
        regions = np.zeros((NJ, 4), np.int8)
        for j in range(NJ):
            for t in range(4):
                if j == (t - 1) % NJ or j == t or j == t + 1:
                    regions[j, t] = 0
                elif j < t - 1 or (c > 0 and j >= NJ - 4 * c):
                    regions[j, t] = 1
                else:
                    regions[j, t] = 2
        variants, vmap, vj = [], {}, np.zeros(NJ, np.int64)
        for j in range(NJ):
            key = tuple(regions[j])
            if key not in vmap:
                vmap[key] = len(variants)
                variants.append(key)
            vj[j] = vmap[key]
        assert len(variants) <= 8, (c, len(variants))
        chi = np.zeros((8, SEQ), np.float32)
        for j in range(NJ):
            chi[vj[j], 128 * j:128 * (j + 1)] = 1.0
        h["chi"].append(chi)
        asel = np.zeros((4, 64, 8), np.float32)
        for v, key in enumerate(variants):
            for s in range(4):
                if key[s] == 1:
                    asel[s, :, v] = rel[0]
                elif key[s] == 2:
                    asel[s, :, v] = rel[32]
        h["augsel"].append(np.ascontiguousarray(asel.reshape(4 * 64, 8)))

    h["rwT"] = _f32(np.asarray(inp["L0_ffn_router_w"]).T)
    h["rb_row"] = _f32(np.asarray(inp["L0_ffn_router_b"]).reshape(1, -1))
    wu = np.asarray(inp["L0_ffn_w_up"], np.float32)
    wd = np.asarray(inp["L0_ffn_w_down"], np.float32)
    swu = np.asarray(inp["L0_ffn_shared_w_up"], np.float32)[0]
    swd = np.asarray(inp["L0_ffn_shared_w_down"], np.float32)[0]
    h["upT"] = _f32(np.stack([swu.T] + [wu[e].T for e in range(8)]))
    h["dnT"] = _f32(np.stack([swd.T] + [wd[e].T for e in range(8)]))

    h["qwT"] = _f32(np.asarray(inp["L1_attn_q_w"]).T)
    h["kwT"] = _f32(np.asarray(inp["L1_attn_k_w"]).T)
    h["vwT"] = _f32(np.asarray(inp["L1_attn_v_w"]).T)
    h["owT"] = _f32(np.asarray(inp["L1_attn_out_w"]).T)
    h["qw_nat"] = _f32(np.asarray(inp["L1_attn_q_w"]))
    h["qb1_pc"] = _f32(np.asarray(inp["L1_attn_q_b"]).reshape(NC_D, 128).T)
    h["qbBt"] = _f32(np.tile(np.asarray(inp["L1_attn_q_b"]).reshape(1, -1), (128, 1)))
    h["kbBt"] = _f32(np.tile(np.asarray(inp["L1_attn_k_b"]).reshape(1, -1), (128, 1)))
    h["vbBt"] = _f32(np.tile(np.asarray(inp["L1_attn_v_b"]).reshape(1, -1), (128, 1)))
    h["obBt"] = _f32(np.tile(np.asarray(inp["L1_attn_out_b"]).reshape(1, -1), (128, 1)))
    h["grepT"] = _f32(np.asarray(inp["L1_attn_grep_w"]).T)
    h["grepb_t"] = _f32(np.tile(np.asarray(inp["L1_attn_grep_b"]), 16).reshape(1, 128))
    h["gaBt"] = _f32(np.tile(np.asarray(inp["L1_attn_grep_a"]).reshape(1, -1), (128, 1)))
    rel1 = np.asarray(inp["L1_attn_rel_emb"], np.float32)
    pos4 = np.arange(4)
    relm = pos4[None, :] - pos4[:, None]
    bucket = (relm > 0).astype(np.int64) * 16 + np.abs(relm)
    h["pbBt"] = _f32(np.tile(rel1[bucket].transpose(2, 0, 1).reshape(1, 256), (128, 1)))

    h["WgT"] = _f32(np.asarray(inp["L1_ffn_Wg"]).T)
    h["bg_row"] = _f32(np.asarray(inp["L1_ffn_bg"]).reshape(1, -1))
    h["WiT"] = _f32(np.asarray(inp["L1_ffn_Wi"]).T)
    h["WoT1"] = _f32(np.asarray(inp["L1_ffn_Wo"]).T)
    h["AT"] = _f32(np.asarray(inp["L1_ffn_A"]).transpose(0, 2, 1).reshape(4 * D, 4))
    Bm = np.asarray(inp["L1_ffn_B"], np.float32)
    h["BTc"] = _f32(np.concatenate([Bm[e].T for e in range(4)], axis=1))

    h["ident"] = np.eye(128, dtype=np.float32)
    h["halfsel"] = np.stack([np.r_[np.ones(64), np.zeros(64)],
                             np.r_[np.zeros(64), np.ones(64)]]).astype(np.float32)
    return h



def emit_xT(nc, blk, sems, x_hbm, xT, stage, idb, pbank2, marks, tag):
    t_tr, v_tr = sems["t_" + tag], sems["v_" + tag]
    gs = [sems["g_" + tag + "0"], sems["g_" + tag + "1"]]
    g_bases = [gs[0].n, gs[1].n]
    t_base, v_base = t_tr.n, v_tr.n

    def g(gp):
        for i in range(NT):
            if i >= 2:
                gp.wait_ge(t_tr.h, t_base + (i - 1) * NC_D)
            gs[i % 2].inc(gp.dma_start(stage[:, (i % 2) * D:(i % 2 + 1) * D],
                                       x_hbm[i * 128:(i + 1) * 128, :]), 16)
    blk.gpsimd(g)

    def t(te):
        for i in range(NT):
            te.wait_ge(gs[i % 2].h, g_bases[i % 2] + 16 * (i // 2 + 1))
            for c in range(NC_D):
                k = i * NC_D + c
                if k >= 2:
                    te.wait_ge(v_tr.h, v_base + k - 1)
                dst = pbank2[k % 2][:, :].bitcast(BF16)[:, 0:128]
                t_tr.inc(te.transpose(
                    dst, stage[:, (i % 2) * D + c * 128:(i % 2) * D + (c + 1) * 128],
                    idb[:, :]))
    blk.tensor(t)

    def v(ve):
        for i in range(NT):
            for c in range(NC_D):
                k = i * NC_D + c
                ve.wait_ge(t_tr.h, t_base + k + 1)
                src = pbank2[k % 2][:, :].bitcast(BF16)[:, 0:128]
                v_tr.inc(ve.tensor_copy(xT[:, c * TOK + i * 128:c * TOK + (i + 1) * 128], src))
    blk.vector(v)
    marks["xT_done_" + tag] = (v_tr, v_tr.n)


def emit_ln_epilogue(nc, blk, sems, acc_src_fn, x_res_hbm, out_hbm, stage, rs2,
                     x2t, sqd, stats, marks, wait_marks, tag, out_extra=None,
                     eps_col=None, vf=None, sf=None):
    v_e, s_e = sems["v_" + tag], sems["s_" + tag]
    xos = [sems["x_" + tag + "0"], sems["x_" + tag + "1"]]
    xo_bases = [xos[0].n, xos[1].n]
    grs = [sems["g_" + tag + "0"], sems["g_" + tag + "1"]]
    gr_bases = [grs[0].n, grs[1].n]
    v_base, s_base = v_e.n, s_e.n

    def g(gp):
        for sm, cnt in wait_marks:
            gp.wait_ge(sm.h, cnt)
        for i in range(NT):
            if i >= 2:
                gp.wait_ge(v_e.h, v_base + (i - 1))
            grs[i % 2].inc(gp.dma_start(stage[:, (i % 2) * D:(i % 2 + 1) * D],
                                        x_res_hbm[i * 128:(i + 1) * 128, :]), 16)
    blk.gpsimd(g)

    def v(ve):
        ve.memset(eps_col, EPS_LN)
        for sm, cnt in wait_marks:
            ve.wait_ge(sm.h, cnt)
        for i in range(NT):
            s = i % 2
            rs = rs2[:, s * D:(s + 1) * D]
            xm = x2t[:, s * D:(s + 1) * D]
            mu = stats[:, 0:1]
            ve.wait_ge(grs[i % 2].h, gr_bases[i % 2] + 16 * (i // 2 + 1))
            if i >= 2:
                ve.wait_ge(xos[i % 2].h, xo_bases[i % 2] + 16 * (i // 2))
                ve.wait_ge(s_e.h, s_base + (i - 1))
            ve.tensor_add(rs, acc_src_fn(i), stage[:, s * D:(s + 1) * D]); ve.drain()
            ve.reduce_sum(mu, rs, AX.X); ve.drain()
            ve.tensor_scalar(mu, mu, 1.0 / D, None, ALU.mult); ve.drain()
            v_e.inc(ve.tensor_scalar(xm, rs, mu, None, ALU.subtract))
    blk.vector(v)

    def sc(se):
        for i in range(NT):
            s = i % 2
            rs = rs2[:, s * D:(s + 1) * D]
            xm = x2t[:, s * D:(s + 1) * D]
            ss = stats[:, 1:2]
            lnv = stats[:, 2:3]
            rstd = stats[:, 3:4]
            se.wait_ge(v_e.h, v_base + i + 1)
            se.activation(sqd[:, :], xm, AF.Square, accum_out=ss)
            se.drain()
            se.activation(lnv, ss, AF.Ln, bias=eps_col, scale=1.0 / D)
            se.drain()
            se.activation(rstd, lnv, AF.Exp, scale=-0.5)
            se.drain()
            inst = se.activation(rs, xm, AF.Identity, scale=rstd)
            if out_extra is not None:
                out_extra(i, se, rs)
            s_e.inc(inst)
    blk.scalar(sc)

    def sy(sn):
        for i in range(NT):
            s = i % 2
            rs = rs2[:, s * D:(s + 1) * D]
            sn.wait_ge(s_e.h, s_base + i + 1)
            xos[i % 2].inc(sn.dma_start(out_hbm[i * 128:(i + 1) * 128, :], rs), 16)
        sn.wait_ge(xos[0].h, xo_bases[0] + 16 * (NT // 2))
        sn.wait_ge(xos[1].h, xo_bases[1] + 16 * (NT // 2))
    blk.sync(sy)


SEM_NAMES_COMMON = ["g_x", "t_x", "v_x", "g_w", "g_m", "t_m", "v_m", "s_m",
                    "t_a", "v_a", "s_a", "g_ep", "v_ep", "s_ep", "x_ep",
                    "t_b", "v_b", "s_b", "g_b", "x_b"]


def new_launch():
    nc = bass.Bass("TRN2", target_bir_lowering=False)
    return nc


def emit_gate_smallops(blk, rsc, rbB, c_all, pb, pbase, t_rt, v_rt, s_rt, NE,
                       renorm=True):
    v_base = v_rt.n

    def v2(ve):
        lg = rsc[:, 0:NE]; ex = rsc[:, 8:8 + NE]; pr = rsc[:, 16:16 + NE]
        m1 = rsc[:, 24:25]; nm = rsc[:, 25:26]; se_ = rsc[:, 26:27]
        rse = rsc[:, 27:28]; m2 = rsc[:, 28:29]; den = rsc[:, 29:30]
        rden = rsc[:, 30:31]; msk = rsc[:, 32:32 + NE]; p2 = rsc[:, 40:40 + NE]
        ve.wait_ge(t_rt.h, 1)
        v_rt.inc(ve.tensor_copy(rbB[:, 0:NE], pb[pbase][:, 0:NE]))
        ve.drain()
        for i in range(NT):
            bank = pb[pbase + (i + 1) % 2]
            ve.wait_ge(t_rt.h, i + 2)
            ve.tensor_add(lg, bank[:, 0:NE], rbB[:, 0:NE]); ve.drain()
            v_rt.inc(ve.reduce_max(nm, lg, AX.X, negate=True))
            ve.wait_ge(s_rt.h, i + 1)
            ve.reciprocal(rse, se_); ve.drain()
            ve.tensor_scalar_mul(pr, ex, rse); ve.drain()
            ve.reduce_max(m1, pr, AX.X); ve.drain()
            ve.tensor_scalar(msk, pr, m1, None, ALU.is_ge); ve.drain()
            ve.scalar_tensor_tensor(p2, msk, -1e30, pr, ALU.mult, ALU.add); ve.drain()
            ve.reduce_max(m2, p2, AX.X); ve.drain()
            if renorm:
                ve.tensor_add(den, m1, m2); ve.drain()
                ve.reciprocal(rden, den); ve.drain()
                ve.tensor_scalar(msk, pr, m2, None, ALU.is_ge); ve.drain()
                ve.scalar_tensor_tensor(c_all[:, i * NE:(i + 1) * NE],
                                        msk, rden, pr, ALU.mult, ALU.mult)
            else:
                ve.tensor_scalar(msk, pr, m2, None, ALU.is_ge); ve.drain()
                ve.tensor_mul(c_all[:, i * NE:(i + 1) * NE], msk, pr)
            v_rt.inc(ve.drain())
    blk.vector(v2)

    def s2(sc):
        lg = rsc[:, 0:NE]; ex = rsc[:, 8:8 + NE]
        nm = rsc[:, 25:26]; se_ = rsc[:, 26:27]
        for i in range(NT):
            sc.wait_ge(v_rt.h, v_base + 1 + 2 * i + 1)
            s_rt.inc(sc.activation(ex, lg, AF.Exp, bias=nm, scale=1.0, accum_out=se_))
    blk.scalar(s2)



def emit_l0ffn(nc, blk, ctx, dram, entry, shared=None):
    x1, rwT, rb_row, upT, dnT, ident, x2 = (
        dram["x1"], dram["rwT"], dram["rb_row"], dram["upT"], dram["dnT"],
        dram["ident"], dram["x2"])

    ctx2 = ExitStack()
    a = lambda nm, sh, dt: ctx2.enter_context(nc.sbuf_tensor("F_" + nm, sh, dt))
    xT = a("xT", [128, NC_D * TOK], BF16)
    stage = a("stage", [128, 2 * D], BF16)
    wub = [a(f"wub{i}", [128, NC_D * 2048], BF16) for i in range(2)]
    wdb = a("wdb", [128, 16 * D], BF16)
    hsil = a("hsil", [128, 16 * 512], BF16)
    acc = a("acc", [128, NT * D], BF16)
    rwn = a("rwn", [128, NC_D * 8], BF16)
    rbr = a("rbr", [1, 8], BF16)
    rbB = a("rbB", [128, 8], F32)
    ones1 = a("ones1", [1, 128], BF16)
    idb = a("idb", [128, 128], BF16)
    c_all = a("c_all", [128, NT * 8], F32)
    rsc = a("rsc", [128, 64], F32)
    rs2 = a("rs2", [128, 2 * D], F32)
    x2t = a("x2t", [128, 2 * D], F32)
    sqd = a("sqd", [128, D], F32)
    stats = a("stats", [128, 8], F32)
    epsc = a("epsc", [128, 1], F32)

    pb = [ctx2.enter_context(nc.psum_tensor(f"F_pb{i}", [128, 512], F32)) for i in range(8)]

    if True:
        sems = mk_block_sems(ctx, nc, "F_",
                ["g_x0", "g_x1", "t_x", "v_x", "g_m", "t_rt", "v_rt", "s_rt",
                 "g_wu0", "g_wu1", "g_wd", "t_h", "s_sil", "t_dn", "v_dn",
                 "g_ep0", "g_ep1", "v_ep", "s_ep", "x_ep0", "x_ep1"], shared)
        marks = {}
        emit_entry_barrier(blk, entry)

        g_m = sems["g_m"]

        def g0(gp):
            g_m.inc(gp.dma_start(idb[:, :], ident[:, :]), 16)
            g_m.inc(gp.dma_start(rwn[:, :].rearrange("p (c e) -> p c e", e=8),
                                 rwT[:, :].rearrange("(c p) e -> p c e", p=128)), 16)
            g_m.inc(gp.dma_start(rbr[:, :], rb_row[:, :]), 16)
        blk.gpsimd(g0)
        v_rt = sems["v_rt"]

        def v0(ve):
            v_rt.inc(ve.memset(ones1[:, :], 1.0))
        blk.vector(v0)
        ONES_OK = v_rt.n

        t_x = sems["t_x"]

        def tw(te):
            te.wait_ge(g_m.h, 48)
        blk.tensor(tw)
        emit_xT(nc, blk, {"g_x0": sems["g_x0"], "g_x1": sems["g_x1"],
                          "t_x": t_x, "v_x": sems["v_x"]},
                x1, xT, stage, idb, pb[0:2], marks, "x")
        XT = marks["xT_done_x"]

        t_rt, s_rt = sems["t_rt"], sems["s_rt"]

        def t2(te):
            te.wait_ge(g_m.h, 48)
            te.wait_ge(v_rt.h, ONES_OK)
            t_rt.inc(te.matmul(pb[2][:, 0:8], ones1[:, :], rbr[:, :], start=True, stop=True))
            te.wait_ge(XT[0].h, XT[1])
            for i in range(NT):
                bank = pb[2 + (i + 1) % 2]
                if i == 1:
                    te.wait_ge(v_rt.h, ONES_OK + 1)
                elif i >= 2:
                    te.wait_ge(v_rt.h, ONES_OK + 1 + 2 * (i - 2) + 1)
                for c in range(NC_D):
                    inst = te.matmul(bank[:, 0:8],
                                     xT[:, c * TOK + i * 128:c * TOK + (i + 1) * 128],
                                     rwn[:, c * 8:(c + 1) * 8],
                                     start=(c == 0), stop=(c == NC_D - 1))
                t_rt.inc(inst)
        blk.tensor(t2)

        emit_gate_smallops(blk, rsc, rbB, c_all, pb, 2, t_rt, v_rt, s_rt, 8)
        RT_V_DONE = v_rt.n

        g_wu = [sems["g_wu0"], sems["g_wu1"]]
        g_wd = sems["g_wd"]
        t_h, s_sil, t_dn, v_dn = (sems["t_h"], sems["s_sil"],
                                  sems["t_dn"], sems["v_dn"])

        def g3(gp):
            for ei in range(9):
                p = ei % 2
                if ei >= 2:
                    gp.wait_ge(t_h.h, 64 * (ei - 1))
                g_wu[p].inc(gp.dma_start(wub[p][:, :].rearrange("p (c f) -> p c f", f=2048),
                                         upT[ei, :, :].rearrange("(c p) f -> p c f", p=128)), 16)
                if ei >= 1:
                    gp.wait_ge(t_dn.h, 32 * ei)
                g_wd.inc(gp.dma_start(wdb[:, :].rearrange("p (c f) -> p c f", f=D),
                                      dnT[ei, :, :].rearrange("(c p) f -> p c f", p=128)), 16)
        blk.gpsimd(g3)

        def t3(te):
            te.wait_ge(v_rt.h, RT_V_DONE)
            hg = 0
            dg = 0
            for ei in range(9):
                te.wait_ge(g_wu[ei % 2].h, 16 * (ei // 2 + 1))
                for tg in range(4):
                    for fc in range(16):
                        bank = pb[4 + hg % 2]
                        if hg >= 2:
                            te.wait_ge(s_sil.h, hg - 1)
                        for c in range(NC_D):
                            inst = te.matmul(
                                bank[:, :],
                                wub[ei % 2][:, c * 2048 + fc * 128:c * 2048 + (fc + 1) * 128],
                                xT[:, c * TOK + tg * 512:c * TOK + (tg + 1) * 512],
                                start=(c == 0), stop=(c == NC_D - 1))
                        t_h.inc(inst)
                        hg += 1
                    te.wait_ge(g_wd.h, 16 * (ei + 1))
                    te.wait_ge(s_sil.h, hg)
                    for tl in range(4):
                        for nd in range(2):
                            bank = pb[6 + dg % 2]
                            if dg >= 2:
                                te.wait_ge(v_dn.h, dg - 1)
                            for fc in range(16):
                                inst = te.matmul(
                                    bank[:, :],
                                    hsil[:, fc * 512 + tl * 128:fc * 512 + (tl + 1) * 128],
                                    wdb[:, fc * D + nd * 512:fc * D + (nd + 1) * 512],
                                    start=(fc == 0), stop=(fc == 15))
                            t_dn.inc(inst)
                            dg += 1
        blk.tensor(t3)

        def s3(sc):
            hg = 0
            for ei in range(9):
                for tg in range(4):
                    gidx = ei * 4 + tg
                    for fc in range(16):
                        sc.wait_ge(t_h.h, hg + 1)
                        if gidx >= 1 and fc == 0:
                            sc.wait_ge(t_dn.h, (gidx - 1) * 8 + 8)
                        bank = pb[4 + hg % 2]
                        s_sil.inc(sc.activation(hsil[:, fc * 512:(fc + 1) * 512],
                                                bank[:, :], AF.Silu))
                        hg += 1
        blk.scalar(s3)

        def v3(ve):
            dg = 0
            for ei in range(9):
                for tg in range(4):
                    for tl in range(4):
                        tau = tg * 4 + tl
                        for nd in range(2):
                            ve.wait_ge(t_dn.h, dg + 1)
                            bank = pb[6 + dg % 2]
                            dst = acc[:, tau * D + nd * 512:tau * D + (nd + 1) * 512]
                            if ei == 0:
                                v_dn.inc(ve.tensor_copy(dst, bank[:, :]))
                            else:
                                ccol = c_all[:, tau * 8 + (ei - 1):tau * 8 + ei]
                                v_dn.inc(ve.scalar_tensor_tensor(
                                    dst, bank[:, :], ccol, dst, ALU.mult, ALU.add))
                            dg += 1
        blk.vector(v3)
        DN_DONE = (v_dn, v_dn.n + 0)
        DN_TOTAL = 9 * 32
        assert v_dn.n == DN_TOTAL

        emit_ln_epilogue(nc, blk,
                         {"g_ep0": sems["g_ep0"], "g_ep1": sems["g_ep1"],
                          "v_ep": sems["v_ep"], "s_ep": sems["s_ep"],
                          "x_ep0": sems["x_ep0"], "x_ep1": sems["x_ep1"]},
                         lambda i: acc[:, i * D:(i + 1) * D],
                         x1, x2, stage, rs2, x2t, sqd, stats, marks,
                         [(v_dn, DN_TOTAL)], "ep", eps_col=epsc[:, :])
    ctx2.close()
    return [(sems["x_ep0"], sems["x_ep0"].n), (sems["x_ep1"], sems["x_ep1"].n)]


def inmap_l0ffn(h, c, x1_chunks):
    return dict(x1=x1_chunks[c], rwT=h["rwT"], rb_row=h["rb_row"],
                upT=h["upT"], dnT=h["dnT"], ident=h["ident"])



def emit_l1ffn(nc, blk, ctx, dram, entry, shared=None):
    (x3, WgT, bg_row, WiT, WoT1, AT, BTc, ident, out) = (
        dram["x3"], dram["WgT"], dram["bg_row"], dram["WiT"], dram["WoT1"],
        dram["AT"], dram["BTc"], dram["ident"], dram["out"])

    ctx2 = ExitStack()
    a = lambda nm, sh, dt: ctx2.enter_context(nc.sbuf_tensor("G_" + nm, sh, dt))
    xT = a("xT", [128, NC_D * TOK], BF16)
    stage = a("stage", [128, 2 * D], BF16)
    wgn = a("wgn", [128, NC_D * 4], BF16)
    bgr = a("bgr", [1, 4], BF16)
    bgB = a("bgB", [128, 8], F32)
    ones1 = a("ones1", [1, 128], BF16)
    idb = a("idb", [128, 128], BF16)
    w_all = a("w_all", [128, NT * 4], F32)
    rsc = a("rsc", [128, 64], F32)
    atn = a("atn", [128, 4 * NC_D * 4], BF16)
    btn = a("btn", [128, 4 * 2048], BF16)
    xa = a("xa", [128, 4 * TOK], BF16)
    win = a("win", [128, NC_D * 2048], BF16)
    won = a("won", [128, 16 * D], BF16)
    baset = a("baset", [128, 2 * 2048], BF16)
    rbuf = a("rbuf", [128, 4 * 2048], BF16)
    hbt = a("hbt", [128, 2048], BF16)
    hbT = a("hbT", [128, 16 * 128], BF16)
    accl = a("accl", [128, NT * D], BF16)
    ep32 = win[:, :].bitcast(F32)
    rs2 = ep32[:, 0:2 * D]
    x2t = ep32[:, 2 * D:4 * D]
    sqd = ep32[:, 4 * D:5 * D]
    stats = a("stats", [128, 8], F32)
    epsc = a("epsc", [128, 1], F32)

    pb = [ctx2.enter_context(nc.psum_tensor(f"G_pb{i}", [128, 512], F32)) for i in range(8)]

    if True:
        sems = mk_block_sems(ctx, nc, "G_",
                ["g_x0", "g_x1", "t_x", "v_x", "g_m", "t_rt", "v_rt", "s_rt", "g_w",
                 "t_xa", "v_xa", "t_bs", "v_bs", "t_rl", "s_rl", "v_hb",
                 "t_tp", "v_tp", "t_dn", "v_dn",
                 "g_ep0", "g_ep1", "v_ep", "s_ep", "x_ep0", "x_ep1"], shared)
        marks = {}
        emit_entry_barrier(blk, entry)

        g_m = sems["g_m"]

        def g0(gp):
            g_m.inc(gp.dma_start(idb[:, :], ident[:, :]), 16)
            g_m.inc(gp.dma_start(wgn[:, :].rearrange("p (c e) -> p c e", e=4),
                                 WgT[:, :].rearrange("(c p) e -> p c e", p=128)), 16)
            g_m.inc(gp.dma_start(bgr[:, :], bg_row[:, :]), 16)
            g_m.inc(gp.dma_start(atn[:, :].rearrange("p (k r) -> p k r", r=4),
                                 AT[:, :].rearrange("(k p) r -> p k r", p=128)), 16)
        blk.gpsimd(g0)
        v_rt = sems["v_rt"]

        def v0(ve):
            v_rt.inc(ve.memset(ones1[:, :], 1.0))
        blk.vector(v0)

        def vpad(ve):
            ve.memset(xa[:, :], 0.0)
            ve.memset(btn[:, :], 0.0)
        blk.vector(vpad)

        t_x = sems["t_x"]

        def tw(te):
            te.wait_ge(g_m.h, 64)
        blk.tensor(tw)
        emit_xT(nc, blk, {"g_x0": sems["g_x0"], "g_x1": sems["g_x1"],
                          "t_x": t_x, "v_x": sems["v_x"]},
                x3, xT, stage, idb, pb[0:2], marks, "x")
        XT = marks["xT_done_x"]

        t_rt, s_rt = sems["t_rt"], sems["s_rt"]

        def t2(te):
            te.wait_ge(g_m.h, 64)
            te.wait_ge(v_rt.h, 1)
            t_rt.inc(te.matmul(pb[2][:, 0:4], ones1[:, :], bgr[:, :], start=True, stop=True))
            te.wait_ge(XT[0].h, XT[1])
            for i in range(NT):
                bank = pb[2 + (i + 1) % 2]
                if i == 1:
                    te.wait_ge(v_rt.h, 2)
                elif i >= 2:
                    te.wait_ge(v_rt.h, 2 + 2 * (i - 2) + 1)
                for c in range(NC_D):
                    inst = te.matmul(bank[:, 0:4],
                                     xT[:, c * TOK + i * 128:c * TOK + (i + 1) * 128],
                                     wgn[:, c * 4:(c + 1) * 4],
                                     start=(c == 0), stop=(c == NC_D - 1))
                t_rt.inc(inst)
        blk.tensor(t2)
        emit_gate_smallops(blk, rsc, bgB, w_all, pb, 2, t_rt, v_rt, s_rt, 4,
                           renorm=False)
        GT_V_DONE = v_rt.n

        t_xa, v_xa = sems["t_xa"], sems["v_xa"]

        def t3(te):
            te.wait_ge(g_m.h, 64)
            te.wait_ge(XT[0].h, XT[1])
            k = 0
            for e in range(4):
                for tg in range(4):
                    bank = pb[4 + k % 2]
                    if k >= 2:
                        te.wait_ge(v_xa.h, k - 1)
                    for c in range(NC_D):
                        inst = te.matmul(bank[0:4, :],
                                         atn[:, (e * NC_D + c) * 4:(e * NC_D + c + 1) * 4],
                                         xT[:, c * TOK + tg * 512:c * TOK + (tg + 1) * 512],
                                         start=(c == 0), stop=(c == NC_D - 1))
                    t_xa.inc(inst)
                    k += 1
        blk.tensor(t3)

        def v3(ve):
            k = 0
            for e in range(4):
                for tg in range(4):
                    ve.wait_ge(t_xa.h, k + 1)
                    bank = pb[4 + k % 2]
                    v_xa.inc(ve.tensor_copy(xa[0:4, e * TOK + tg * 512:e * TOK + (tg + 1) * 512],
                                            bank[0:4, :]))
                    k += 1
        blk.vector(v3)
        XA_DONE = v_xa.n

        g_w = sems["g_w"]
        t_bs, v_bs = sems["t_bs"], sems["v_bs"]
        t_rl, s_rl = sems["t_rl"], sems["s_rl"]
        v_hb = sems["v_hb"]
        t_tp, v_tp = sems["t_tp"], sems["v_tp"]
        t_dn, v_dn = sems["t_dn"], sems["v_dn"]

        def gw(gp):
            for fh in range(2):
                if fh == 1:
                    gp.wait_ge(t_dn.h, 32)
                g_w.inc(gp.dma_start(win[:, :].rearrange("p (c f) -> p c f", f=2048),
                                     WiT[:, fh * 2048:(fh + 1) * 2048]
                                     .rearrange("(c p) f -> p c f", p=128)), 16)
                g_w.inc(gp.dma_start(won[:, :].rearrange("p (c f) -> p c f", f=D),
                                     WoT1[fh * 2048:(fh + 1) * 2048, :]
                                     .rearrange("(c p) f -> p c f", p=128)), 16)
                if fh == 0:
                    gp.wait_ge(v_xa.h, 1)
                g_w.inc(gp.dma_start(btn[0:4, :].rearrange("p (e f) -> p e f", f=2048),
                                     BTc[:, :].rearrange("p (e f) -> p e f", f=4096)[:, :, fh * 2048:(fh + 1) * 2048]), 16)
        blk.gpsimd(gw)

        def t4(te):
            te.wait_ge(v_rt.h, GT_V_DONE)
            te.wait_ge(v_xa.h, XA_DONE)
            nb = 0
            nr = 0
            ntp = 0
            nd = 0
            for fh in range(2):
                te.wait_ge(g_w.h, 48 * fh + 48)
                for tau in range(NT):
                    for ft in range(4):
                        bank = pb[0 + nb % 2]
                        if nb >= 2:
                            te.wait_ge(v_bs.h, nb - 1)
                        for c in range(NC_D):
                            inst = te.matmul(
                                bank[:, :],
                                xT[:, c * TOK + tau * 128:c * TOK + (tau + 1) * 128],
                                win[:, c * 2048 + ft * 512:c * 2048 + (ft + 1) * 512],
                                start=(c == 0), stop=(c == NC_D - 1))
                        t_bs.inc(inst)
                        nb += 1
                    te.wait_ge(v_bs.h, nb)
                    for e in range(4):
                        for ft in range(4):
                            bank = pb[2 + nr % 2]
                            if nr >= 2:
                                te.wait_ge(s_rl.h, nr - 1)
                            slot = (fh * NT + tau) % 2
                            te.matmul(bank[:, :], idb[:, :],
                                      baset[:, slot * 2048 + ft * 512:
                                            slot * 2048 + (ft + 1) * 512],
                                      start=True, stop=False, skip_group_check=True)
                            inst = te.matmul(
                                bank[:, :],
                                xa[:, e * TOK + tau * 128:e * TOK + (tau + 1) * 128],
                                btn[:, e * 2048 + ft * 512:e * 2048 + (ft + 1) * 512],
                                start=False, stop=True, skip_group_check=True)
                            t_rl.inc(inst)
                            nr += 1
                    te.wait_ge(v_hb.h, (fh * NT + tau) + 1)
                    for tc in range(16):
                        bank = pb[4 + ntp % 2]
                        if ntp >= 2:
                            te.wait_ge(v_tp.h, ntp - 1)
                        dstp = bank[:, :].bitcast(BF16)[:, 0:128]
                        t_tp.inc(te.transpose(dstp, hbt[:, tc * 128:(tc + 1) * 128], idb[:, :]))
                        ntp += 1
                    te.wait_ge(v_tp.h, ntp)
                    for ndt in range(2):
                        bank = pb[6 + nd % 2]
                        if nd >= 2:
                            te.wait_ge(v_dn.h, nd - 1)
                        for tc in range(16):
                            inst = te.matmul(
                                bank[:, :],
                                hbT[:, tc * 128:(tc + 1) * 128],
                                won[:, tc * D + ndt * 512:tc * D + (ndt + 1) * 512],
                                start=(tc == 0), stop=(tc == 15))
                        t_dn.inc(inst)
                        nd += 1
        blk.tensor(t4)

        def s4(sc):
            nr = 0
            for fh in range(2):
                for tau in range(NT):
                    k = fh * NT + tau
                    for e in range(4):
                        wcol = w_all[:, tau * 4 + e:tau * 4 + e + 1]
                        for ft in range(4):
                            sc.wait_ge(t_rl.h, nr + 1)
                            if k >= 1 and e == 0 and ft == 0:
                                sc.wait_ge(v_hb.h, k)
                            bank = pb[2 + nr % 2]
                            s_rl.inc(sc.activation(
                                rbuf[:, e * 2048 + ft * 512:e * 2048 + (ft + 1) * 512],
                                bank[:, :], AF.Relu, scale=wcol))
                            nr += 1
        blk.scalar(s4)

        def v456(ve):
            nb = 0
            ntp = 0
            nd = 0
            for fh in range(2):
                for tau in range(NT):
                    k = fh * NT + tau
                    slot = k % 2
                    for ft in range(4):
                        ve.wait_ge(t_bs.h, nb + 1)
                        if k >= 2 and ft == 0:
                            ve.wait_ge(t_rl.h, 16 * (k - 1))
                        bank = pb[0 + nb % 2]
                        v_bs.inc(ve.tensor_copy(
                            baset[:, slot * 2048 + ft * 512:slot * 2048 + (ft + 1) * 512],
                            bank[:, :]))
                        nb += 1
                    ve.wait_ge(s_rl.h, 16 * (k + 1))
                    if k >= 1:
                        ve.wait_ge(t_tp.h, 16 * k)
                    ve.tensor_add(hbt[:, 0:2048], rbuf[:, 0:2048], rbuf[:, 2048:4096])
                    ve.drain()
                    ve.tensor_add(hbt[:, 0:2048], hbt[:, 0:2048], rbuf[:, 4096:6144])
                    ve.drain()
                    v_hb.inc(ve.tensor_add(hbt[:, 0:2048], hbt[:, 0:2048],
                                           rbuf[:, 6144:8192]))
                    for tc in range(16):
                        ve.wait_ge(t_tp.h, ntp + 1)
                        bank = pb[4 + ntp % 2]
                        srcp = bank[:, :].bitcast(BF16)[:, 0:128]
                        v_tp.inc(ve.tensor_copy(hbT[:, tc * 128:(tc + 1) * 128], srcp))
                        ntp += 1
                    for ndt in range(2):
                        ve.wait_ge(t_dn.h, nd + 1)
                        bank = pb[6 + nd % 2]
                        dst = accl[:, tau * D + ndt * 512:tau * D + (ndt + 1) * 512]
                        if fh == 0:
                            v_dn.inc(ve.tensor_copy(dst, bank[:, :]))
                        else:
                            v_dn.inc(ve.tensor_add(dst, dst, bank[:, :]))
                        nd += 1
        blk.vector(v456)

        DN_TOTAL = v_dn.n

        emit_ln_epilogue(nc, blk,
                         {"g_ep0": sems["g_ep0"], "g_ep1": sems["g_ep1"],
                          "v_ep": sems["v_ep"], "s_ep": sems["s_ep"],
                          "x_ep0": sems["x_ep0"], "x_ep1": sems["x_ep1"]},
                         lambda i: accl[:, i * D:(i + 1) * D],
                         x3, out, stage, rs2, x2t, sqd, stats, marks,
                         [(v_dn, DN_TOTAL)], "ep", eps_col=epsc[:, :])
    ctx2.close()
    return [(sems["x_ep0"], sems["x_ep0"].n), (sems["x_ep1"], sems["x_ep1"].n)]


def inmap_l1ffn(h, c, x3_chunks):
    return dict(x3=x3_chunks[c], WgT=h["WgT"], bg_row=h["bg_row"], WiT=h["WiT"],
                WoT1=h["WoT1"], AT=h["AT"], BTc=h["BTc"], ident=h["ident"])



def emit_l1attn(nc, blk, ctx, dram, entry, shared=None):
    TR = 9
    (x2, qwT, kwT, vwT, owT, qw_nat, qb1_pc, qbBt, kbBt, vbBt, obBt, grepT,
     grepb_t, gaBt, pbBt, ident, x3) = (
        dram["x2"], dram["qwT"], dram["kwT"], dram["vwT"], dram["owT"],
        dram["qw_nat"], dram["qb1_pc"], dram["qbBt"], dram["kbBt"],
        dram["vbBt"], dram["obBt"], dram["grepT"], dram["grepb_t"],
        dram["gaBt"], dram["pbBt"], dram["ident"], dram["x3"])

    SC = 0.125

    ctx2 = ExitStack()
    a = lambda nm, sh, dt: ctx2.enter_context(nc.sbuf_tensor("L_" + nm, sh, dt))
    xT = a("xT", [128, NC_D * TOK], BF16)
    stage = a("stage", [128, 2 * D], BF16)
    idb = a("idb", [128, 128], BF16)
    ones1 = a("ones1", [1, 128], BF16)
    qwn = a("qwn", [128, NC_D * D], BF16)
    grepb = a("grepb", [128, 8], BF16)
    qbc = a("qbc", [128, NC_D], BF16)
    grepbt = a("grepbt", [1, 128], F32)
    bgrow = a("bgrow", [1, 128], F32)
    bgrow16 = a("bgrow16", [1, 128], BF16)
    WGT = a("WGT", [128, NC_D * 128], BF16)
    biasGB = a("biasGB", [128, 128], F32)
    pbiasB = a("pbiasB", [128, 256], F32)
    grepaB = a("grepaB", [128, 16], F32)
    qbB = a("qbB", [128, D], F32)
    kbB = a("kbB", [128, D], F32)
    vbB = a("vbB", [128, D], F32)
    obB = a("obB", [128, D], F32)
    wdb = [a(f"wdb{i}", [128, NC_D * D], BF16) for i in range(2)]
    qb_ = a("qb_", [128, 8 * D], BF16)
    kb_ = a("kb_", [128, 8 * D], BF16)
    vb_ = a("vb_", [128, 8 * D], BF16)
    ctxb = a("ctxb", [128, 8 * D], BF16)
    ctxT = qwn[:, :]
    acc = a("acc", [128, NT * D], BF16)
    gates = a("gates", [128, 8 * 16], F32)
    g2b = a("g2b", [128, 32], F32)
    sgb = a("sgb", [128, 32], F32)
    scb = a("scb", [128, 2 * 256], F32)
    prb = a("prb", [128, 2 * 256], F32)
    nm64 = a("nm64", [128, 64], F32)
    s64 = a("s64", [128, 64], F32)
    tmpv = stage[:, 0:D]
    rs2 = kb_[:, :].bitcast(F32)[:, 0:2 * D]
    x2t = kb_[:, :].bitcast(F32)[:, 2 * D:4 * D]
    sqd = vb_[:, :].bitcast(F32)[:, 0:D]
    stats = a("stats", [128, 8], F32)
    epsc = a("epsc", [128, 1], F32)

    pb = [ctx2.enter_context(nc.psum_tensor(f"L_pb{i}", [128, 512], F32)) for i in range(8)]

    if True:
        sems = mk_block_sems(ctx, nc, "L_",
                ["g_x0", "g_x1", "t_x", "v_x", "g_m", "g_w0", "g_w1",
                 "t_wg", "v_wg", "t_pj", "v_pj", "t_g", "v_g", "s_g",
                 "v_sc", "s_sc", "v_cx", "t_tp", "v_tp", "t_ou", "v_ou",
                 "g_ep0", "g_ep1", "v_ep", "s_ep", "x_ep0", "x_ep1"], shared)
        marks = {}
        emit_entry_barrier(blk, entry)

        g_m = sems["g_m"]

        def g0(gp):
            g_m.inc(gp.dma_start(idb[:, :], ident[:, :]), 16)
            g_m.inc(gp.dma_start(qwn[:, :].rearrange("p (c f) -> p c f", f=D),
                                 qw_nat[:, :].rearrange("(c p) f -> p c f", p=128)), 16)
            g_m.inc(gp.dma_start(grepb[0:64, :], grepT[:, :]), 16)
            g_m.inc(gp.dma_start(grepb[64:128, :], grepT[:, :]), 16)
            g_m.inc(gp.dma_start(qbc[:, :], qb1_pc[:, :]), 16)
            g_m.inc(gp.dma_start(qbB[:, :], qbBt[:, :]), 16)
            g_m.inc(gp.dma_start(kbB[:, :], kbBt[:, :]), 16)
            g_m.inc(gp.dma_start(vbB[:, :], vbBt[:, :]), 16)
            g_m.inc(gp.dma_start(obB[:, :], obBt[:, :]), 16)
            g_m.inc(gp.dma_start(grepbt[:, :], grepb_t[:, :]), 16)
            g_m.inc(gp.dma_start(grepaB[:, :], gaBt[:, :]), 16)
            g_m.inc(gp.dma_start(pbiasB[:, :], pbBt[:, :]), 16)
        blk.gpsimd(g0)
        GM_ALL = g_m.n

        v_wg = sems["v_wg"]

        def v0(ve):
            ve.memset(ones1[:, :], 1.0)
            ve.wait_ge(g_m.h, GM_ALL)
            ve.tensor_scalar(qbB[:, :], qbB[:, :], SC, None, ALU.mult)
            v_wg.inc(ve.drain())
        blk.vector(v0)
        V0_DONE = v_wg.n

        t_x = sems["t_x"]

        def tw(te):
            te.wait_ge(g_m.h, GM_ALL)
        blk.tensor(tw)
        emit_xT(nc, blk, {"g_x0": sems["g_x0"], "g_x1": sems["g_x1"],
                          "t_x": t_x, "v_x": sems["v_x"]},
                x2, xT, stage, idb, pb[0:2], marks, "x")
        XT = marks["xT_done_x"]
        if TR <= 1:
            def fin(sn):
                sn.wait_ge(XT[0].h, XT[1])
            blk.sync(fin)
            return nc

        t_wg = sems["t_wg"]
        NB = 11

        SUB = int(os.environ.get("L1A_SUB", "3"))

        def t_wgf(te):
            te.wait_ge(v_wg.h, V0_DONE)
            for h in range(16):
                for dc in range(NC_D):
                    k = h * NC_D + dc
                    bank = pb[2 + k % 2]
                    if k >= 2:
                        te.wait_ge(v_wg.h, V0_DONE + k - 1)
                    lhsT = qwn[(h % 2) * 64:(h % 2) * 64 + 64, :]\
                        .rearrange("p (c f) -> p c f", f=D)[:, h // 2, dc * 128:(dc + 1) * 128]
                    t_wg.inc(te.matmul(bank[:, 0:8], lhsT,
                                       grepb[(h % 2) * 64:(h % 2) * 64 + 64, :],
                                       start=True, stop=True))
            if SUB <= 1:
                return
            for h in range(16):
                k2 = 128 + h
                bank = pb[2 + k2 % 2]
                te.wait_ge(v_wg.h, V0_DONE + k2 - 1)
                lhs_qb = qbc[(h % 2) * 64:(h % 2) * 64 + 64, :]\
                    .rearrange("p (c u) -> p c u", u=1)[:, h // 2, :]
                t_wg.inc(te.matmul(bank[0:1, 0:8], lhs_qb,
                                   grepb[(h % 2) * 64:(h % 2) * 64 + 64, :],
                                   start=True, stop=True))
            if SUB <= 2:
                return
            te.wait_ge(v_wg.h, V0_DONE + 144 + 2)
            t_wg.inc(te.matmul(pb[5][:, 0:128], ones1[:, :], bgrow16[:, :],
                               start=True, stop=True))
        blk.tensor(t_wgf)

        def v_wgf(ve):
            for h in range(16):
                for dc in range(NC_D):
                    k = h * NC_D + dc
                    ve.wait_ge(t_wg.h, k + 1)
                    bank = pb[2 + k % 2]
                    ve.tensor_scalar(WGT[:, dc * 128 + h * 8:dc * 128 + (h + 1) * 8],
                                     bank[:, 0:8], SC, None, ALU.mult)
                    v_wg.inc(ve.drain())
            if SUB <= 1:
                return
            for h in range(16):
                ve.wait_ge(t_wg.h, 128 + h + 1)
                bank = pb[2 + (128 + h) % 2]
                v_wg.inc(ve.tensor_copy(bgrow[0:1, h * 8:(h + 1) * 8], bank[0:1, 0:8]))
            if SUB <= 2:
                return
            ve.drain()
            ve.scalar_tensor_tensor(bgrow[0:1, :], bgrow[0:1, :],
                                    SC, grepbt[0:1, :], ALU.mult, ALU.add)
            ve.drain()
            ve.tensor_copy(bgrow16[0:1, :], bgrow[0:1, :])
            v_wg.inc(ve.drain())
            v_wg.inc(ve.drain())
            ve.wait_ge(t_wg.h, 145)
            v_wg.inc(ve.tensor_copy(biasGB[:, :], pb[5][:, 0:128]))
        blk.vector(v_wgf)
        WG_DONE = (v_wg, v_wg.n)
        if TR <= 2:
            def fin(sn):
                sn.wait_ge(v_wg.h, WG_DONE[1])
            blk.sync(fin)
            return nc

        g_w = [sems["g_w0"], sems["g_w1"]]
        t_pj, v_pj = sems["t_pj"], sems["v_pj"]
        t_g, v_g, s_g = sems["t_g"], sems["v_g"], sems["s_g"]
        v_sc, s_sc = sems["v_sc"], sems["s_sc"]
        v_cx = sems["v_cx"]
        t_tp, v_tp = sems["t_tp"], sems["v_tp"]
        t_ou, v_ou = sems["t_ou"], sems["v_ou"]

        def tauglob(ph, i, pc):
            return i * 4 + ph * 2 + pc

        def gw(gp):
            k = 0
            for ph in range(2):
                for wi, wt in enumerate([qwT, kwT, vwT, owT]):
                    p = k % 2
                    if k >= 2:
                        if k % 4 == 3:
                            gp.wait_ge(t_pj.h, (ph * 3 + 2) * 16)
                        elif wi == 0:
                            gp.wait_ge(t_pj.h, 3 * 16)
                        elif wi == 1:
                            gp.wait_ge(t_ou.h, 16)
                        elif wi == 2:
                            gp.wait_ge(t_pj.h, (ph * 3 + 1) * 16)
                    g_w[p].inc(gp.dma_start(wdb[p][:, :].rearrange("p (c f) -> p c f", f=D),
                                            wt[:, :].rearrange("(c p) f -> p c f", p=128)), 16)
                    k += 1
        blk.gpsimd(gw)

        def t_main(te):
            te.wait_ge(XT[0].h, XT[1])
            te.wait_ge(v_wg.h, WG_DONE[1])
            npj = 0
            ng = 0
            ntp = 0
            nou = 0
            for ph in range(2):
                if ph == 1 and TR <= 8:
                    break
                for wi in range(3):
                    kk = ph * 4 + wi
                    te.wait_ge(g_w[kk % 2].h, 16 * (kk // 2 + 1))
                    for tl in range(8):
                        i, pc = tl // 2, tl % 2
                        tg = tauglob(ph, i, pc)
                        for n2 in range(2):
                            bank = pb[0 + npj % 2]
                            if npj >= 2:
                                te.wait_ge(v_pj.h, npj - 1)
                            for c in range(NC_D):
                                inst = te.matmul(
                                    bank[:, :],
                                    xT[:, c * TOK + tg * 128:c * TOK + (tg + 1) * 128],
                                    wdb[kk % 2][:, c * D + n2 * 512:c * D + (n2 + 1) * 512],
                                    start=(c == 0), stop=(c == NC_D - 1))
                            t_pj.inc(inst)
                            npj += 1
                if TR <= 3:
                    continue
                for tl in range(8):
                    i, pc = tl // 2, tl % 2
                    tg = tauglob(ph, i, pc)
                    bank = pb[2 + ng % 2]
                    if ng >= 2:
                        te.wait_ge(v_g.h, 3 * (ng - 1))
                    for c in range(NC_D):
                        inst = te.matmul(bank[:, 0:128],
                                         xT[:, c * TOK + tg * 128:c * TOK + (tg + 1) * 128],
                                         WGT[:, c * 128:(c + 1) * 128],
                                         start=(c == 0), stop=(c == NC_D - 1))
                    t_g.inc(inst)
                    ng += 1
                if TR <= 6:
                    continue
                for tl in range(8):
                    te.wait_ge(v_cx.h, ph * 8 + tl + 1)
                    for dc in range(NC_D):
                        bank = pb[4 + ntp % 2]
                        if ntp >= 2:
                            te.wait_ge(v_tp.h, ntp - 1)
                        dstp = bank[:, :].bitcast(BF16)[:, 0:128]
                        t_tp.inc(te.transpose(dstp, ctxb[:, tl * D + dc * 128:tl * D + (dc + 1) * 128],
                                              idb[:, :]))
                        ntp += 1
                if TR <= 7:
                    continue
                kk_ow = ph * 4 + 3
                te.wait_ge(g_w[kk_ow % 2].h, 16 * (kk_ow // 2 + 1))
                for tl in range(8):
                    i, pc = tl // 2, tl % 2
                    tg = tauglob(ph, i, pc)
                    te.wait_ge(v_tp.h, ph * 64 + (tl + 1) * 8)
                    for n2 in range(2):
                        bank = pb[6 + nou % 2]
                        if nou >= 2:
                            te.wait_ge(v_ou.h, nou - 1)
                        for dc in range(NC_D):
                            inst = te.matmul(
                                bank[:, :],
                                ctxT[:, dc * D + tl * 128:dc * D + (tl + 1) * 128],
                                wdb[kk_ow % 2][:, dc * D + n2 * 512:dc * D + (n2 + 1) * 512],
                                start=(dc == 0), stop=(dc == NC_D - 1))
                        t_ou.inc(inst)
                        nou += 1
        blk.tensor(t_main)

        def v_main(ve):
            npj = 0
            ng = 0
            nsc = 0
            ntp = 0
            nou = 0
            for ph in range(2):
                if ph == 1 and TR <= 8:
                    break
                for wi in range(3):
                    dstb = [qb_, kb_, vb_][wi]
                    bB = [qbB, kbB, vbB][wi]
                    for tl in range(8):
                        for n2 in range(2):
                            ve.wait_ge(t_pj.h, npj + 1)
                            bank = pb[0 + npj % 2]
                            dst = dstb[:, tl * D + n2 * 512:tl * D + (n2 + 1) * 512]
                            if wi == 0:
                                ve.scalar_tensor_tensor(dst, bank[:, :], SC,
                                                        bB[:, n2 * 512:(n2 + 1) * 512],
                                                        ALU.mult, ALU.add)
                            else:
                                ve.tensor_add(dst, bank[:, :], bB[:, n2 * 512:(n2 + 1) * 512])
                            v_pj.inc(ve.drain())
                            npj += 1
                if TR <= 3:
                    continue
                for tl in range(8):
                    ve.wait_ge(t_g.h, ng + 1)
                    bank = pb[2 + ng % 2]
                    ve.tensor_add(tmpv[:, 0:128], bank[:, 0:128], biasGB[:, :])
                    ve.drain()
                    ve.tensor_reduce(g2b[:, :], tmpv[:, 0:128].rearrange("p (s f) -> p s f", f=4),
                                     AX.X, ALU.add)
                    v_g.inc(ve.drain())
                    ve.wait_ge(s_g.h, ng + 1)
                    ga = sgb[:, :].rearrange("p (h a) -> p h a", a=2)[:, :, 0]
                    gb = sgb[:, :].rearrange("p (h a) -> p h a", a=2)[:, :, 1]
                    ve.tensor_mul(tmpv[:, 0:16], gb, grepaB[:, :])
                    ve.drain()
                    ve.tensor_scalar(tmpv[:, 0:16], tmpv[:, 0:16], 1.0, None, ALU.subtract)
                    ve.drain()
                    ve.tensor_mul(tmpv[:, 0:16], ga, tmpv[:, 0:16])
                    v_g.inc(ve.drain())
                    ve.tensor_scalar(gates[:, tl * 16:(tl + 1) * 16], tmpv[:, 0:16],
                                     2.0, None, ALU.add)
                    v_g.inc(ve.drain())
                    ng += 1
                if TR <= 4:
                    continue
                for pc in range(2):
                    sc_ = scb[:, pc * 256:(pc + 1) * 256]
                    pr_ = prb[:, pc * 256:(pc + 1) * 256]
                    for i in range(4):
                        for j in range(4):
                            qt = qb_[:, (i * 2 + pc) * D:(i * 2 + pc + 1) * D]
                            kt = kb_[:, (j * 2 + pc) * D:(j * 2 + pc + 1) * D]
                            ve.tensor_mul(tmpv[:, :], qt, kt)
                            ve.drain()
                            ve.tensor_reduce(
                                sc_.rearrange("p (h x) -> p h x", x=16)[:, :, i * 4 + j],
                                tmpv[:, :].rearrange("p (h d) -> p h d", d=64),
                                AX.X, ALU.add)
                            ve.drain()
                    for i in range(4):
                        gslice = gates[:, (i * 2 + pc) * 16:(i * 2 + pc + 1) * 16]
                        gexp = gslice.rearrange("p (h u) -> p h u", u=1).to_broadcast([128, 16, 16])
                        pslice = pbiasB[:, :].rearrange("p (h x) -> p h x", x=16)
                        scv = sc_.rearrange("p (h x) -> p h x", x=16)
                        ve.tensor_mul(tmpv[:, 0:64].rearrange("p (h x) -> p h x", x=4),
                                      gexp[:, :, 0:4], pslice[:, :, i * 4:(i + 1) * 4])
                            
                        ve.drain()
                        ve.tensor_add(scv[:, :, i * 4:(i + 1) * 4],
                                      scv[:, :, i * 4:(i + 1) * 4],
                                      tmpv[:, 0:64].rearrange("p (h x) -> p h x", x=4))
                        ve.drain()
                    ve.tensor_reduce(nm64[:, :], sc_.rearrange("p (s j) -> p s j", j=4),
                                     AX.X, ALU.max, negate=True)
                    ve.drain()
                    ve.tensor_add(sc_.rearrange("p (s j) -> p s j", j=4),
                                  sc_.rearrange("p (s j) -> p s j", j=4),
                                  nm64[:, :].rearrange("p (s u) -> p s u", u=1)
                                  .to_broadcast([128, 64, 4]))
                    v_sc.inc(ve.drain())
                    ve.wait_ge(s_sc.h, ph * 2 + pc + 1)
                    ve.tensor_reduce(s64[:, :], pr_.rearrange("p (s j) -> p s j", j=4),
                                     AX.X, ALU.add)
                    ve.drain()
                    ve.reciprocal(s64[:, :], s64[:, :])
                    ve.drain()
                    ve.tensor_mul(pr_.rearrange("p (s j) -> p s j", j=4),
                                  pr_.rearrange("p (s j) -> p s j", j=4),
                                  s64[:, :].rearrange("p (s u) -> p s u", u=1)
                                  .to_broadcast([128, 64, 4]))
                    v_sc.inc(ve.drain())
                    nsc += 2
                if TR <= 5:
                    continue
                for tl in range(8):
                    i, pc = tl // 2, tl % 2
                    pr_ = prb[:, pc * 256:(pc + 1) * 256]
                    ctile = ctxb[:, tl * D:(tl + 1) * D]
                    for j in range(4):
                        pj = pr_.rearrange("p (h x) -> p h x", x=16)[:, :, i * 4 + j]
                        pjb = pj.rearrange("p (h u) -> p h u", u=1)\
                            .to_broadcast([128, 16, 64])
                        vt = vb_[:, (j * 2 + pc) * D:(j * 2 + pc + 1) * D]\
                            .rearrange("p (h d) -> p h d", d=64)
                        if j == 0:
                            ve.tensor_mul(ctile.rearrange("p (h d) -> p h d", d=64), vt, pjb)
                            ve.drain()
                        else:
                            ve.tensor_mul(tmpv[:, :].rearrange("p (h d) -> p h d", d=64), vt, pjb)
                            ve.drain()
                            if j < 3:
                                ve.tensor_add(ctile, ctile, tmpv[:, :])
                                ve.drain()
                            else:
                                v_cx.inc(ve.tensor_add(ctile, ctile, tmpv[:, :]))
                if TR <= 6:
                    continue
                for tl in range(8):
                    for dc in range(NC_D):
                        ve.wait_ge(t_tp.h, ntp + 1)
                        bank = pb[4 + ntp % 2]
                        srcp = bank[:, :].bitcast(BF16)[:, 0:128]
                        v_tp.inc(ve.tensor_copy(ctxT[:, dc * D + tl * 128:dc * D + (tl + 1) * 128],
                                                srcp))
                        ntp += 1
                if TR <= 7:
                    continue
                for tl in range(8):
                    i, pc = tl // 2, tl % 2
                    tg = tauglob(ph, i, pc)
                    for n2 in range(2):
                        ve.wait_ge(t_ou.h, nou + 1)
                        bank = pb[6 + nou % 2]
                        v_ou.inc(ve.tensor_add(
                            acc[:, tg * D + n2 * 512:tg * D + (n2 + 1) * 512],
                            bank[:, :], obB[:, n2 * 512:(n2 + 1) * 512]))
                        nou += 1
        blk.vector(v_main)

        def s_main(sc):
            ng = 0
            nsc = 0
            for ph in range(2):
                if ph == 1 and TR <= 8:
                    break
                if TR <= 3:
                    continue
                for tl in range(8):
                    sc.wait_ge(v_g.h, 3 * ng + 1)
                    s_g.inc(sc.activation(sgb[:, :], g2b[:, :], AF.Sigmoid))
                    ng += 1
                for pc in range(2):
                    if TR <= 4:
                        continue
                    sc.wait_ge(v_sc.h, nsc + 1)
                    s_sc.inc(sc.activation(prb[:, pc * 256:(pc + 1) * 256],
                                           scb[:, pc * 256:(pc + 1) * 256], AF.Exp))
                    nsc += 2
        blk.scalar(s_main)
        OU_TOTAL = v_ou.n
        if TR <= 8:
            def fin(sn):
                if v_ou.n:
                    sn.wait_ge(v_ou.h, v_ou.n)
                elif sems["v_cx"].n:
                    sn.wait_ge(sems["v_cx"].h, sems["v_cx"].n)
                elif sems["v_sc"].n:
                    sn.wait_ge(sems["v_sc"].h, sems["v_sc"].n)
                elif sems["v_g"].n:
                    sn.wait_ge(sems["v_g"].h, sems["v_g"].n)
                else:
                    sn.wait_ge(sems["v_pj"].h, sems["v_pj"].n)
            blk.sync(fin)
            return nc

        emit_ln_epilogue(nc, blk,
                         {"g_ep0": sems["g_ep0"], "g_ep1": sems["g_ep1"],
                          "v_ep": sems["v_ep"], "s_ep": sems["s_ep"],
                          "x_ep0": sems["x_ep0"], "x_ep1": sems["x_ep1"]},
                         lambda i: acc[:, i * D:(i + 1) * D],
                         x2, x3, stage, rs2, x2t, sqd, stats, marks,
                         [(v_ou, OU_TOTAL)], "ep", eps_col=epsc[:, :])
    ctx2.close()
    return [(sems["x_ep0"], sems["x_ep0"].n), (sems["x_ep1"], sems["x_ep1"].n)]


def inmap_l1attn(h, c, x2_chunks):
    return dict(x2=x2_chunks[c], qwT=h["qwT"], kwT=h["kwT"], vwT=h["vwT"],
                owT=h["owT"], qw_nat=h["qw_nat"], qb1_pc=h["qb1_pc"],
                qbBt=h["qbBt"], kbBt=h["kbBt"], vbBt=h["vbBt"], obBt=h["obBt"],
                grepT=h["grepT"], grepb_t=h["grepb_t"],
                gaBt=h["gaBt"], pbBt=h["pbBt"], ident=h["ident"])



def emit_l0attn(nc, blk, ctx, dram, entry, shared=None):
    NJ = 32
    (xTr, x_own, WaT, baQ, baK, baVt, snbT, WoTh, boT, relXs, relXA, relXB,
     chiT, augS, ident, x1, KTh, Vh, Eh, EhA, EhB) = (
        dram["xTr"], dram["x_own"], dram["WaT"], dram["baQ"], dram["baK"],
        dram["baVt"], dram["snbT"], dram["WoTh"], dram["boT"], dram["relXs"],
        dram["relXA"], dram["relXB"], dram["chiT"], dram["augS"], dram["ident"],
        dram["x1"], dram["KTh"], dram["Vh"], dram["Eh"], dram["EhA"], dram["EhB"])

    ctx2 = ExitStack()
    a = lambda nm, sh, dt: ctx2.enter_context(nc.sbuf_tensor("A_" + nm, sh, dt))
    wab = a("wab", [128, NC_D * 3 * D], BF16)
    xst = a("xst", [128, 2 * NC_D * 512], BF16)
    ktile = a("ktile", [128, 2 * NC_D * 512], BF16)
    vtile = a("vtile", [128, 2 * 4 * D], BF16)
    QTb = a("QTb", [128, NC_D * TOK], BF16)
    acc = a("acc", [128, NT * D], BF16)
    woH = a("woH", [128, 16 * D], BF16)
    KTb = [wab[:, 0:SEQ], wab[:, SEQ:2 * SEQ]]
    Vb = [wab[:, 8192:8192 + NJ * 65 + 63], wab[:, 10272:10272 + NJ * 65 + 63]]
    rhsT = [wab[:, 12352:12864], wab[:, 12864:13376]]
    tbuf = wab[:, 13376:15424]
    t2buf = wab[:, 15424:17472]
    augsb = wab[:, 17472:19520].bitcast(F32)
    etile = wab[:, 19520:20542]
    stage = wab[:, 21568:23616]
    wtile = [xst[:, 0:1536], xst[:, 1536:3072]]
    sqd = xst[:, 6144:8192].bitcast(F32)
    ctxP = ktile[0:64, 0:8192]
    rssc = vtile[:, 0:1024].bitcast(F32)
    rsb16 = vtile[:, 1024:1536]
    rs2 = vtile[:, 1600:5696].bitcast(F32)
    augb = a("augb", [128, 4 * 8], BF16)
    t2x = a("t2x", [128, 2048], BF16)
    tbx = a("tbx", [128, 2048], BF16)
    relXb = a("relXb", [128, 511 + 2 * 255], BF16)
    onesP = a("onesP", [128, 64], BF16)
    idb = a("idb", [128, 128], BF16)
    idf = a("idf", [128, 128], F32)
    baQs = a("baQs", [128, NC_D], F32)
    baKs = a("baKs", [128, NC_D], F32)
    baVb = a("baVb", [128, D], BF16)
    snb8 = a("snb8", [128, 1], F32)
    eps8 = a("eps8", [128, 1], F32)
    boTb = a("boTb", [128, D], BF16)
    x2t = QTb[:, 0:4096].bitcast(F32)
    stats = a("stats", [128, 8], F32)
    epsc = a("epsc", [128, 1], F32)

    scps = [ctx2.enter_context(nc.psum_tensor(f"A_scps{i}", [128, 1024], F32)) for i in range(2)]
    ctxps = ctx2.enter_context(nc.psum_tensor("A_ctxps", [128, 512], F32))
    miscps = [ctx2.enter_context(nc.psum_tensor(f"A_miscps{i}", [128, 512], F32)) for i in range(2)]
    outps = ctx2.enter_context(nc.psum_tensor("A_outps", [128, 512], F32))

    BH = B * 16

    if True:
        sems = mk_block_sems(ctx, nc, "A_", 
                ["g_m", "m_sy", "g_xs0", "g_xs1", "t_kv", "s_kt", "s_qt", "v_kv",
                 "x_kt0", "x_kt1", "x_v0", "x_v1",
                 "s_kl0", "s_kl1", "s_vl0", "s_vl1", "g_qr0", "g_qr1",
                 "g_ew0", "g_ew1", "g_er0", "g_er1",
                 "t_au", "v_au", "s_au", "t_qe", "v_qe",
                 "t_sc", "s_rl", "v_t2", "t_av", "v_cx", "s_dv",
                 "t_op", "v_op", "g_chi", "v_ms",
                 "g_ep0", "g_ep1", "v_ep", "s_ep", "x_ep0", "x_ep1"], shared)
        marks = {}
        emit_entry_barrier(blk, entry)
        g_m = sems["g_m"]

        def g0(gp):
            g_m.inc(gp.dma_start(wab[:, :].rearrange("p (c f) -> p c f", f=3 * D),
                                 WaT[:, :].rearrange("(c p) f -> p c f", p=128)), 16)
            g_m.inc(gp.dma_start(idb[:, :], ident[:, :]), 16)
            g_m.inc(gp.dma_start(augb[0:64, :].rearrange("p (s e) -> p s e", e=8),
                                 augS[:, :].rearrange("(s p) e -> p s e", p=64)), 16)
            g_m.inc(gp.dma_start(augb[64:128, :].rearrange("p (s e) -> p s e", e=8),
                                 augS[:, :].rearrange("(s p) e -> p s e", p=64)), 16)
            g_m.inc(gp.dma_start(relXb[0:64, 0:511], relXs[:, :]), 16)
            g_m.inc(gp.dma_start(relXb[64:128, 0:511], relXs[:, :]), 16)
            g_m.inc(gp.dma_start(relXb[0:64, 511:766], relXA[:, :]), 16)
            g_m.inc(gp.dma_start(relXb[64:128, 511:766], relXA[:, :]), 16)
            g_m.inc(gp.dma_start(relXb[0:64, 766:1021], relXB[:, :]), 16)
            g_m.inc(gp.dma_start(relXb[64:128, 766:1021], relXB[:, :]), 16)
            g_m.inc(gp.dma_start(woH[:, :], WoTh[:, :]), 16)
            g_m.inc(gp.dma_start(baVb[:, :], baVt[:, :]), 16)
            g_m.inc(gp.dma_start(boTb[:, :], boT[:, :]), 16)
        blk.gpsimd(g0)

        m_sy = sems["m_sy"]

        def sy0(sn):
            m_sy.inc(sn.dma_start(idf[:, :], ident[:, :]), 16)
            m_sy.inc(sn.dma_start(baQs[:, :], baQ[:, :]), 16)
            m_sy.inc(sn.dma_start(baKs[:, :], baK[:, :]), 16)
            m_sy.inc(sn.dma_start(snb8[:, :], snbT[:, :]), 16)
        blk.sync(sy0)
        GM_ALL = g_m.n
        MSY_ALL = m_sy.n
        v_kv = sems["v_kv"]

        def v0(ve):
            ve.wait_ge(g_m.h, GM_ALL)
            ve.wait_ge(m_sy.h, MSY_ALL)
            ve.memset(onesP[:, :], 1.0)
            ve.memset(eps8[:, :], 1e-8)
            ve.tensor_scalar(snb8[:, :], snb8[:, :], 8.0, None, ALU.mult)
            v_kv.inc(ve.drain())
        blk.vector(v0)
        V0_OK = v_kv.n

        g_xs = [sems["g_xs0"], sems["g_xs1"]]
        t_kv, s_kt, s_qt = sems["t_kv"], sems["s_kt"], sems["s_qt"]
        x_kt = [sems["x_kt0"], sems["x_kt1"]]
        x_v = [sems["x_v0"], sems["x_v1"]]
        OWN = (0, 8, 16, 24)

        def cum_t(i):
            return sum(16 + (8 if k in OWN else 0) for k in range(i + 1))

        def g1(gp):
            for i in range(32):
                p = i % 2
                if i >= 2:
                    gp.wait_ge(t_kv.h, cum_t(i - 1))
                g_xs[p].inc(gp.dma_start(
                    xst[:, p * NC_D * 512:(p + 1) * NC_D * 512]
                    .rearrange("p (c t) -> p c t", t=512),
                    xTr[:, i * 512:(i + 1) * 512].rearrange("(c p) t -> p c t", p=128)), 16)
        blk.gpsimd(g1)

        def t1(te):
            te.wait_ge(g_m.h, GM_ALL)
            te.wait_ge(m_sy.h, MSY_ALL)
            te.wait_ge(v_kv.h, V0_OK)
            nkt = 0
            nv = 0
            nqt = 0
            for i in range(32):
                p = i % 2
                te.wait_ge(g_xs[p].h, 16 * (i // 2 + 1))
                xb3 = xst[:, p * NC_D * 512:(p + 1) * NC_D * 512]\
                    .rearrange("p (c t) -> p c t", t=512)
                for fc in range(NC_D):
                    bank = scps[0][:, (nkt % 2) * 512:(nkt % 2) * 512 + 512]
                    if nkt >= 2:
                        te.wait_ge(s_kt.h, nkt - 1)
                    for c in range(NC_D):
                        inst = te.matmul(
                            bank,
                            wab[:, c * 3 * D + D + fc * 128:c * 3 * D + D + (fc + 1) * 128],
                            xb3[:, c, :],
                            start=(c == 0), stop=(c == NC_D - 1))
                    t_kv.inc(inst)
                    nkt += 1
                for tc in range(4):
                    for n2 in range(2):
                        bank = scps[1][:, (nv % 2) * 512:(nv % 2) * 512 + 512]
                        if nv >= 2:
                            te.wait_ge(v_kv.h, V0_OK + nv - 1)
                        for c in range(NC_D):
                            inst = te.matmul(
                                bank,
                                xb3[:, c, tc * 128:(tc + 1) * 128],
                                wab[:, c * 3 * D + 2 * D + n2 * 512:
                                    c * 3 * D + 2 * D + (n2 + 1) * 512],
                                start=(c == 0), stop=(c == NC_D - 1))
                        t_kv.inc(inst)
                        nv += 1
                if i in OWN:
                    bq = i // 8
                    for fc in range(NC_D):
                        bank = miscps[nqt % 2][:, :]
                        if nqt >= 2:
                            te.wait_ge(s_qt.h, nqt - 1)
                        for c in range(NC_D):
                            inst = te.matmul(
                                bank,
                                wab[:, c * 3 * D + fc * 128:c * 3 * D + (fc + 1) * 128],
                                xb3[:, c, :],
                                start=(c == 0), stop=(c == NC_D - 1))
                        t_kv.inc(inst)
                        nqt += 1
        blk.tensor(t1)

        def s1(sc):
            sc.wait_ge(m_sy.h, MSY_ALL)
            nkt = 0
            nqt = 0
            for i in range(32):
                p = i % 2
                base_tkv = cum_t(i - 1) if i > 0 else 0
                for fc in range(NC_D):
                    sc.wait_ge(t_kv.h, base_tkv + fc + 1)
                    if i >= 2 and fc == 0:
                        sc.wait_ge(x_kt[p].h, 16 * (i // 2))
                    bank = scps[0][:, (nkt % 2) * 512:(nkt % 2) * 512 + 512]
                    s_kt.inc(sc.activation(
                        ktile[:, p * NC_D * 512 + fc * 512:p * NC_D * 512 + (fc + 1) * 512],
                        bank, AF.Identity, bias=baKs[:, fc:fc + 1]))
                    nkt += 1
                if i in OWN:
                    bq = i // 8
                    for fc in range(NC_D):
                        sc.wait_ge(t_kv.h, base_tkv + 16 + fc + 1)
                        bank = miscps[nqt % 2][:, :]
                        s_qt.inc(sc.activation(
                            QTb[:, fc * TOK + bq * 512:fc * TOK + (bq + 1) * 512],
                            bank, AF.Identity, bias=baQs[:, fc:fc + 1]))
                        nqt += 1
        blk.scalar(s1)

        def v1f(ve):
            nv = 0
            for i in range(32):
                p = i % 2
                base_tkv = (cum_t(i - 1) if i > 0 else 0) + 8
                for g in range(8):
                    tc, n2 = g // 2, g % 2
                    ve.wait_ge(t_kv.h, base_tkv + g + 1)
                    if i >= 2 and g == 0:
                        ve.wait_ge(x_v[p].h, 16 * (i // 2))
                    bank = scps[1][:, (nv % 2) * 512:(nv % 2) * 512 + 512]
                    v_kv.inc(ve.tensor_add(
                        vtile[:, p * 4 * D + tc * D + n2 * 512:
                              p * 4 * D + tc * D + (n2 + 1) * 512],
                        bank, baVb[:, n2 * 512:(n2 + 1) * 512]))
                    nv += 1
        blk.vector(v1f)

        def sy1(sn):
            for i in range(32):
                p = i % 2
                sn.wait_ge(s_kt.h, 8 * (i + 1))
                x_kt[p].inc(sn.dma_start(
                    KTh[:, i * 512:(i + 1) * 512].rearrange("(c p) t -> p c t", p=128),
                    ktile[:, p * NC_D * 512:(p + 1) * NC_D * 512]
                    .rearrange("p (c t) -> p c t", t=512)), 16)
                sn.wait_ge(v_kv.h, V0_OK + 8 * (i + 1))
                x_v[p].inc(sn.dma_start(
                    Vh[i * 512:(i + 1) * 512, :].rearrange("(tc p) f -> p tc f", p=128),
                    vtile[:, p * 4 * D:(p + 1) * 4 * D]
                    .rearrange("p (tc f) -> p tc f", f=D)), 16)
        blk.sync(sy1)
        P1_KT = (x_kt[0].n, x_kt[1].n)
        P1_V = (x_v[0].n, x_v[1].n)
        P1_SQT = s_qt.n
        v_ms = sems["v_ms"]

        def vms(ve):
            ve.wait_ge(t_kv.h, cum_t(31))
            ve.wait_ge(s_qt.h, P1_SQT)
            v_ms.inc(ve.memset(wab[64:128, 0:2 * SEQ], 0.0))
            v_ms.inc(ve.memset(wab[64:128, 12352:13376], 0.0))
        blk.vector(vms)

        s_kl = [sems["s_kl0"], sems["s_kl1"]]
        s_vl = [sems["s_vl0"], sems["s_vl1"]]
        g_qr = [sems["g_qr0"], sems["g_qr1"]]
        g_ew = [sems["g_ew0"], sems["g_ew1"]]
        g_er = [sems["g_er0"], sems["g_er1"]]
        t_au, v_au, s_au = sems["t_au"], sems["v_au"], sems["s_au"]
        t_qe, v_qe = sems["t_qe"], sems["v_qe"]
        t_sc, s_rl, v_t2 = sems["t_sc"], sems["s_rl"], sems["v_t2"]
        t_av, v_cx, s_dv = sems["t_av"], sems["v_cx"], sems["s_dv"]
        t_op, v_op = sems["t_op"], sems["v_op"]

        band = {}
        for t_ in range(4):
            for dl in (-1, 0, 1):
                band.setdefault((t_ + dl) % NJ, []).append((t_, dl))
        BLKS = [(t_, dl) for t_ in range(4) for dl in (-1, 0, 1)]
        QSPECS = [(0, 0, 511), (1, 0, 511), (2, 0, 511), (3, 0, 511),
                  (0, 511, 255), (3, 766, 255)]

        def sy2(sn):
            sn.wait_ge(x_kt[0].h, P1_KT[0])
            sn.wait_ge(x_kt[1].h, P1_KT[1])
            sn.wait_ge(x_v[0].h, P1_V[0])
            sn.wait_ge(x_v[1].h, P1_V[1])
            for bh in range(BH):
                b, h = bh // 16, bh % 16
                pe = bh % 2
                if bh >= 2:
                    sn.wait_ge(t_sc.h, 32 * (bh - 1))
                s_kl[pe].inc(sn.dma_start(
                    KTb[pe][0:64, :], KTh[h * 64:(h + 1) * 64, b * SEQ:(b + 1) * SEQ]), 16)
                if bh >= 2:
                    sn.wait_ge(t_av.h, 32 * (bh - 1))
                s_vl[pe].inc(sn.dma_start(
                    Vb[pe][:, 0:NJ * 65].rearrange("p (j d) -> p j d", d=65)[:, :, 0:64],
                    Vh[b * SEQ:(b + 1) * SEQ, h * 64:(h + 1) * 64]
                    .rearrange("(j p) d -> p j d", p=128)), 16)
        blk.sync(sy2)

        def g2(gp):
            gp.wait_ge(s_qt.h, P1_SQT)
            gp.wait_ge(t_kv.h, cum_t(31))
            gp.wait_ge(sems["v_ms"].h, 2)
            g_chi = sems["g_chi"]
            g_chi.inc(gp.dma_start(KTb[0][64:72, :], chiT[:, :]), 16)
            g_chi.inc(gp.dma_start(KTb[1][64:72, :], chiT[:, :]), 16)
            for bh in range(BH):
                b, h = bh // 16, bh % 16
                pe = bh % 2
                for k in range(6):
                    gp.wait_ge(v_au.h, 10 * bh + 5 + k)
                    if bh >= 2 and k == 0:
                        gp.wait_ge(g_er[pe].h, 16 * 12 * (bh // 2))
                    if k < 4:
                        dst = bass.AP(Eh, pe * 512 * 511 + k * 128 * 511,
                                      [[511, 128], [1, 511]])
                        src = etile[:, (k % 2) * 511:(k % 2) * 511 + 511]
                    elif k == 4:
                        dst = bass.AP(EhA, pe * 128 * 255, [[255, 128], [1, 255]])
                        src = etile[:, 0:255]
                    else:
                        dst = bass.AP(EhB, pe * 128 * 255, [[255, 128], [1, 255]])
                        src = etile[:, 511:766]
                    g_ew[pe].inc(gp.dma_start(dst, src), 16)
                for bi, (t_, dl) in enumerate(BLKS):
                    if bi == 0:
                        gp.wait_ge(g_ew[pe].h, 16 * 6 * (bh // 2 + 1))
                        if bh >= 2:
                            gp.wait_ge(t_sc.h, 32 * (bh - 1))
                    if (t_, dl) == (0, -1):
                        srcap = bass.AP(EhA, pe * 128 * 255 + 127, [[254, 128], [1, 128]])
                    elif (t_, dl) == (3, 1):
                        srcap = bass.AP(EhB, pe * 128 * 255 + 127, [[254, 128], [1, 128]])
                    else:
                        srcap = bass.AP(Eh, pe * 512 * 511 + 128 * 511 * t_ + 128 * dl + 255,
                                        [[510, 128], [1, 128]])
                    g_er[pe].inc(gp.dma_start(
                        wtile[pe][:, bi * 128:(bi + 1) * 128], srcap), 16)
                if bh >= 2:
                    gp.wait_ge(t_sc.h, 32 * (bh - 1))
                g_qr[pe].inc(gp.dma_start(
                    rhsT[pe][0:64, :],
                    QTb[(h % 2) * 64:(h % 2) * 64 + 64,
                        (h // 2) * TOK + b * 512:(h // 2) * TOK + (b + 1) * 512]), 16)
        blk.gpsimd(g2)

        nmx_t = [0]

        def emit_augqr_step(te, bh, k):
            b, h = bh // 16, bh % 16
            hb = (h % 2) * 64

            def qsl(s):
                return QTb[hb:hb + 64, (h // 2) * TOK + b * 512 + s * 128:
                           (h // 2) * TOK + b * 512 + (s + 1) * 128]
            if k == 0 and bh >= 2:
                te.wait_ge(s_au.h, bh - 1)
            bank = miscps[nmx_t[0] % 2]
            if nmx_t[0] >= 2:
                te.wait_ge(v_au.h, nmx_t[0] - 1)
            if k < 4:
                t_au.inc(te.matmul(bank[64:72, 0:128],
                                   augb[hb:hb + 64, k * 8:(k + 1) * 8], qsl(k),
                                   start=True, stop=True))
            else:
                s, off, w = QSPECS[k - 4]
                t_qe.inc(te.matmul(bank[:, 0:w], qsl(s), relXb[hb:hb + 64, off:off + w],
                                   start=True, stop=True))
            nmx_t[0] += 1

        def emit_augqr(te, bh):
            for k in range(10):
                emit_augqr_step(te, bh, k)

        def tail_bh(te, pbh):
            pb_, ph = pbh // 16, pbh % 16
            te.wait_ge(s_dv.h, pbh + 1)
            te.wait_ge(v_op.h, 8 * (pbh // 16))
            t_op.inc(te.matmul(outps[0:64, :], onesP[64:65, 0:64],
                               rsb16[64:65, :], start=True, stop=True))
            if ph == 15:
                te.wait_ge(v_cx.h, 2 * (pbh + 1))
                for nop in range(8):
                    qs, n2 = nop // 2, nop % 2
                    te.wait_ge(v_op.h, 8 * pb_ + nop)
                    for hh in range(16):
                        inst = te.matmul(
                            outps[:, :],
                            ktile[:, hh * 512 + qs * 128:hh * 512 + (qs + 1) * 128],
                            woH[:, hh * D + n2 * 512:hh * D + (n2 + 1) * 512],
                            start=(hh == 0), stop=(hh == 15))
                    t_op.inc(inst)

        def t2f(te):
            te.wait_ge(s_qt.h, P1_SQT)
            te.wait_ge(sems["v_ms"].h, 2)
            te.wait_ge(s_kt.h, 8 * 32)
            te.wait_ge(v_kv.h, V0_OK + 8 * 32)
            te.wait_ge(sems["g_chi"].h, 32)
            emit_augqr(te, 0)
            emit_augqr(te, 1)
            for bh in range(BH):
                b, h = bh // 16, bh % 16
                pe = bh % 2
                te.wait_ge(s_kl[pe].h, 16 * (bh // 2 + 1))
                te.wait_ge(g_qr[pe].h, 16 * (bh // 2 + 1))
                te.wait_ge(s_au.h, bh + 1)
                te.wait_ge(s_vl[pe].h, 16 * (bh // 2 + 1))
                need_w = True
                for g in range(16):
                    for j in (2 * g, 2 * g + 1):
                        grp = scps[(j // 2) % 2][:, (j % 2) * 512:(j % 2) * 512 + 512]
                        if j % 2 == 0 and (16 * bh + g) >= 2:
                            te.wait_ge(s_rl.h, 16 * bh + g - 1)
                        isband = j in band
                        if isband:
                            if need_w:
                                te.wait_ge(g_er[pe].h, 16 * 12 * (bh // 2 + 1))
                                need_w = False
                            covered = set()
                            for (t_, dl) in band[j]:
                                bi = BLKS.index((t_, dl))
                                covered.add(t_)
                                te.matmul(grp[:, t_ * 128:(t_ + 1) * 128],
                                          wtile[pe][:, bi * 128:(bi + 1) * 128],
                                          idb[:, :], start=True, stop=True,
                                          skip_group_check=True)
                            for t_ in range(4):
                                inst = te.matmul(
                                    grp[:, t_ * 128:(t_ + 1) * 128],
                                    KTb[pe][:, j * 128:(j + 1) * 128],
                                    rhsT[pe][:, t_ * 128:(t_ + 1) * 128],
                                    start=(t_ not in covered), stop=True,
                                    skip_group_check=True)
                        else:
                            inst = te.matmul(grp, KTb[pe][:, j * 128:(j + 1) * 128],
                                             rhsT[pe][:, :],
                                             start=True, stop=True,
                                             skip_group_check=True)
                        t_sc.inc(inst)
                    if g >= 2:
                        for j in (2 * (g - 2), 2 * (g - 2) + 1):
                            te.wait_ge(v_t2.h, 16 * bh + g - 1)
                            if j == 0 and bh >= 1:
                                te.wait_ge(v_cx.h, 2 * bh)
                            k2 = j // 2
                            t2s = t2buf if k2 % 4 < 2 else t2x
                            inst = te.matmul(
                                ctxps[:, :], Vb[pe][:, j * 65:j * 65 + 128],
                                t2s[:, (k2 % 2) * 1024 + (j % 2) * 512:
                                    (k2 % 2) * 1024 + (j % 2) * 512 + 512],
                                start=(j == 0), stop=False, skip_group_check=True)
                            t_av.inc(inst)
                    if g < 10 and bh + 2 < BH:
                        emit_augqr_step(te, bh + 2, g)
                    if g == 1 and bh >= 1:
                        tail_bh(te, bh - 1)
                for k2 in (14, 15):
                    for j in (2 * k2, 2 * k2 + 1):
                        te.wait_ge(v_t2.h, 16 * bh + k2 + 1)
                        t2s = t2buf if k2 % 4 < 2 else t2x
                        inst = te.matmul(
                            ctxps[:, :], Vb[pe][:, j * 65:j * 65 + 128],
                            t2s[:, (k2 % 2) * 1024 + (j % 2) * 512:
                                (k2 % 2) * 1024 + (j % 2) * 512 + 512],
                            start=False, stop=(j == 31), skip_group_check=True)
                        t_av.inc(inst)
            tail_bh(te, BH - 1)
        blk.tensor(t2f)

        def s2f(sc):
            for bh in range(BH):
                b, h = bh // 16, bh % 16
                pe = bh % 2
                sc.wait_ge(v_au.h, 10 * bh + 4)
                if bh >= 2:
                    sc.wait_ge(t_sc.h, 32 * (bh - 1))
                s_au.inc(sc.activation(rhsT[pe][64:72, :],
                                       augsb[64:72, (bh % 2) * 512:(bh % 2) * 512 + 512],
                                       AF.Identity, bias=snb8[64:72, :]))
                for g in range(16):
                    sc.wait_ge(t_sc.h, 32 * bh + 2 * (g + 1))
                    gg = 16 * bh + g
                    if gg >= 4:
                        sc.wait_ge(v_t2.h, gg - 3)
                    tbs = tbuf if g % 4 < 2 else tbx
                    s_rl.inc(sc.activation(tbs[:, (g % 2) * 1024:(g % 2) * 1024 + 1024],
                                           scps[g % 2][:, :], AF.Relu, scale=0.125))
                sc.wait_ge(v_cx.h, 2 * bh + 1)
                sc.activation(rssc[64:65, :], ctxps[64:65, :], AF.Ln, bias=eps8[64:65, :])
                sc.drain()
                sc.activation(rssc[64:65, :], rssc[64:65, :], AF.Exp, scale=-1.0)
                sc.drain()
                s_dv.inc(sc.activation(rsb16[64:65, :], rssc[64:65, :], AF.Identity))
        blk.scalar(s2f)

        nmx_v = [0]
        nau_c = [0]
        nqe_c = [0]

        def v2f(ve):
            def ev_augqr_step(bh, k):
                pe = bh % 2
                bank = miscps[nmx_v[0] % 2]
                if k < 4:
                    ve.wait_ge(t_au.h, nau_c[0] + 1)
                    v_au.inc(ve.tensor_copy(
                        augsb[64:72, (bh % 2) * 512 + k * 128:(bh % 2) * 512 + (k + 1) * 128],
                        bank[64:72, 0:128]))
                    nau_c[0] += 1
                else:
                    kk = k - 4
                    s, off, w = QSPECS[kk]
                    ve.wait_ge(t_qe.h, nqe_c[0] + 1)
                    ew_total = 6 * (bh // 2) + kk
                    if ew_total >= 1:
                        ve.wait_ge(g_ew[pe].h, 16 * ew_total)
                    dsto = (kk % 2) * 511 if kk < 4 else (0 if kk == 4 else 511)
                    v_au.inc(ve.tensor_copy(etile[:, dsto:dsto + w], bank[:, 0:w]))
                    nqe_c[0] += 1
                nmx_v[0] += 1

            def ev_augqr(bh):
                for k in range(10):
                    ev_augqr_step(bh, k)

            ve.wait_ge(t_kv.h, cum_t(31))
            ve.memset(Vb[0][:, 0:NJ * 65].rearrange("p (j d) -> p j d", d=65)[:, :, 64:65], 1.0)
            ve.memset(Vb[1][:, 0:NJ * 65].rearrange("p (j d) -> p j d", d=65)[:, :, 64:65], 1.0)
            ve.drain()
            ev_augqr(0)
            ev_augqr(1)
            for bh in range(BH):
                b, h = bh // 16, bh % 16
                pe = bh % 2
                for g in range(16):
                    gg = 16 * bh + g
                    ve.wait_ge(s_rl.h, gg + 1)
                    war = 32 * bh + 2 * g - 6
                    if war > 0:
                        ve.wait_ge(t_av.h, war)
                    t2s = t2buf if g % 4 < 2 else t2x
                    tbs = tbuf if g % 4 < 2 else tbx
                    v_t2.inc(ve.tensor_mul(
                        t2s[:, (g % 2) * 1024:(g % 2) * 1024 + 1024],
                        tbs[:, (g % 2) * 1024:(g % 2) * 1024 + 1024],
                        tbs[:, (g % 2) * 1024:(g % 2) * 1024 + 1024]))
                    if g < 10 and bh + 2 < BH:
                        ev_augqr_step(bh + 2, g)
                ve.wait_ge(t_av.h, 32 * (bh + 1))
                v_cx.inc(ve.tensor_copy(ctxP[:, h * 512:(h + 1) * 512], ctxps[0:64, :]))
                ve.wait_ge(t_op.h, bh + 1 + 8 * b)
                v_cx.inc(ve.tensor_mul(ctxP[:, h * 512:(h + 1) * 512],
                                       ctxP[:, h * 512:(h + 1) * 512], outps[0:64, :]))
                if h == 15:
                    for nop in range(8):
                        qs, n2 = nop // 2, nop % 2
                        ve.wait_ge(t_op.h, bh + 2 + 8 * b + nop)
                        tau = b * 4 + qs
                        v_op.inc(ve.tensor_add(
                            acc[:, tau * D + n2 * 512:tau * D + (n2 + 1) * 512],
                            outps[:, :], boTb[:, n2 * 512:(n2 + 1) * 512]))
        blk.vector(v2f)
        OP_TOTAL = v_op.n

        emit_ln_epilogue(nc, blk,
                         {"g_ep0": sems["g_ep0"], "g_ep1": sems["g_ep1"],
                          "v_ep": sems["v_ep"], "s_ep": sems["s_ep"],
                          "x_ep0": sems["x_ep0"], "x_ep1": sems["x_ep1"]},
                         lambda i: acc[:, i * D:(i + 1) * D],
                         x_own, x1, stage, rs2, x2t, sqd, stats, marks,
                         [(v_op, OP_TOTAL)], "ep", eps_col=epsc[:, :])
    ctx2.close()
    return [(sems["x_ep0"], sems["x_ep0"].n), (sems["x_ep1"], sems["x_ep1"].n)]


def inmap_l0attn(h, c):
    return dict(xTr=h["xT_rot"][c], x_own=h["x_own"][c], WaT=h["WaT"],
                baQ=h["baQ"], baK=h["baK"], baVt=h["baVt"], snbT=h["snbT"],
                WoTh=h["WoTh"], boT=h["boT"], relXs=h["relXT_std"],
                relXA=h["relXT_wA"][c], relXB=h["relXT_wB"][c],
                chiT=h["chi"][c], augS=h["augsel"][c], ident=h["ident"])


def mk_block_sems(ctx, nc, px, names, shared):
    sems = {}
    for n in names:
        if shared and n in shared:
            sems[n] = shared[n]
        else:
            sems[n] = Sem(ctx.enter_context(nc.semaphore(px + n)))
    return sems


def emit_entry_barrier(blk, marks):
    if not marks:
        return

    def f(e):
        for s, n in marks:
            e.wait_ge(s.h, n)
    for reg in (blk.gpsimd, blk.sync, blk.tensor, blk.vector, blk.scalar):
        reg(f)


def build_fused():
    nc = new_launch()
    d = {}
    DT = nc.dram_tensor
    d["xTr"] = DT("xTr", [D, B * SEQ], F32, kind="ExternalInput")
    d["x_own"] = DT("x_own", [TOK, D], F32, kind="ExternalInput")
    d["WaT"] = DT("WaT", [D, 3 * D], F32, kind="ExternalInput")
    d["baQ"] = DT("baQ", [128, NC_D], F32, kind="ExternalInput")
    d["baK"] = DT("baK", [128, NC_D], F32, kind="ExternalInput")
    d["baVt"] = DT("baVt", [128, D], F32, kind="ExternalInput")
    d["snbT"] = DT("snbT", [128, 1], F32, kind="ExternalInput")
    d["WoTh"] = DT("WoTh", [128, 16 * D], F32, kind="ExternalInput")
    d["boT"] = DT("boT", [128, D], F32, kind="ExternalInput")
    d["relXs"] = DT("relXs", [64, 511], F32, kind="ExternalInput")
    d["relXA"] = DT("relXA", [64, 255], F32, kind="ExternalInput")
    d["relXB"] = DT("relXB", [64, 255], F32, kind="ExternalInput")
    d["chiT"] = DT("chiT", [8, SEQ], F32, kind="ExternalInput")
    d["augS"] = DT("augS", [4 * 64, 8], F32, kind="ExternalInput")
    d["ident"] = DT("ident", [128, 128], F32, kind="ExternalInput")
    d["rwT"] = DT("rwT", [D, 8], F32, kind="ExternalInput")
    d["rb_row"] = DT("rb_row", [1, 8], F32, kind="ExternalInput")
    d["upT"] = DT("upT", [9, D, 2048], F32, kind="ExternalInput")
    d["dnT"] = DT("dnT", [9, 2048, D], F32, kind="ExternalInput")
    d["qwT"] = DT("qwT", [D, D], F32, kind="ExternalInput")
    d["kwT"] = DT("kwT", [D, D], F32, kind="ExternalInput")
    d["vwT"] = DT("vwT", [D, D], F32, kind="ExternalInput")
    d["owT"] = DT("owT", [D, D], F32, kind="ExternalInput")
    d["qw_nat"] = DT("qw_nat", [D, D], F32, kind="ExternalInput")
    d["qb1_pc"] = DT("qb1_pc", [128, NC_D], F32, kind="ExternalInput")
    d["qbBt"] = DT("qbBt", [128, D], F32, kind="ExternalInput")
    d["kbBt"] = DT("kbBt", [128, D], F32, kind="ExternalInput")
    d["vbBt"] = DT("vbBt", [128, D], F32, kind="ExternalInput")
    d["obBt"] = DT("obBt", [128, D], F32, kind="ExternalInput")
    d["grepT"] = DT("grepT", [64, 8], F32, kind="ExternalInput")
    d["grepb_t"] = DT("grepb_t", [1, 128], F32, kind="ExternalInput")
    d["gaBt"] = DT("gaBt", [128, 16], F32, kind="ExternalInput")
    d["pbBt"] = DT("pbBt", [128, 256], F32, kind="ExternalInput")
    d["WgT"] = DT("WgT", [D, 4], F32, kind="ExternalInput")
    d["bg_row"] = DT("bg_row", [1, 4], F32, kind="ExternalInput")
    d["WiT"] = DT("WiT", [D, 4096], F32, kind="ExternalInput")
    d["WoT1"] = DT("WoT1", [4096, D], F32, kind="ExternalInput")
    d["AT"] = DT("AT", [4 * D, 4], F32, kind="ExternalInput")
    d["BTc"] = DT("BTc", [4, 4 * 4096], F32, kind="ExternalInput")
    d["out"] = DT("out", [TOK, D], F32, kind="ExternalOutput")
    d["x1"] = DT("x1", [TOK, D], F32, kind="Internal")
    d["x2"] = DT("x2", [TOK, D], F32, kind="Internal")
    d["x3"] = DT("x3", [TOK, D], F32, kind="Internal")
    d["KTh"] = DT("KTh", [D, B * SEQ], BF16, kind="Internal")
    d["Vh"] = DT("Vh", [B * SEQ, D], BF16, kind="Internal")
    d["Eh"] = DT("Eh", [2 * 512 * 511], BF16, kind="Internal")
    d["EhA"] = DT("EhA", [2 * 128 * 255], BF16, kind="Internal")
    d["EhB"] = DT("EhB", [2 * 128 * 255], BF16, kind="Internal")

    with ExitStack() as ctx:
        blk = ctx.enter_context(nc.Block())
        shared = {n: Sem(ctx.enter_context(nc.semaphore("S_" + n))) for n in
                  ["g_ep0", "g_ep1", "v_ep", "s_ep", "x_ep0", "x_ep1",
                   "g_x0", "g_x1", "t_x", "v_x"]}
        m = emit_l0attn(nc, blk, ctx, d, [], shared)
        m = emit_l0ffn(nc, blk, ctx, d, m, shared)
        m = emit_l1attn(nc, blk, ctx, d, m, shared)
        m = emit_l1ffn(nc, blk, ctx, d, m, shared)
    return nc


def inmap_fused(h, c):
    im = dict(xTr=h["xT_rot"][c], x_own=h["x_own"][c], WaT=h["WaT"],
              baQ=h["baQ"], baK=h["baK"], baVt=h["baVt"], snbT=h["snbT"],
              WoTh=h["WoTh"], boT=h["boT"], relXs=h["relXT_std"],
              relXA=h["relXT_wA"][c], relXB=h["relXT_wB"][c],
              chiT=h["chi"][c], augS=h["augsel"][c], ident=h["ident"],
              rwT=h["rwT"], rb_row=h["rb_row"], upT=h["upT"], dnT=h["dnT"],
              qwT=h["qwT"], kwT=h["kwT"], vwT=h["vwT"], owT=h["owT"],
              qw_nat=h["qw_nat"], qb1_pc=h["qb1_pc"], qbBt=h["qbBt"],
              kbBt=h["kbBt"], vbBt=h["vbBt"], obBt=h["obBt"], grepT=h["grepT"],
              grepb_t=h["grepb_t"], gaBt=h["gaBt"], pbBt=h["pbBt"],
              WgT=h["WgT"], bg_row=h["bg_row"], WiT=h["WiT"], WoT1=h["WoT1"],
              AT=h["AT"], BTc=h["BTc"])
    return im



def kernel(**inputs):
    h = prep_host(inputs)
    cores = list(range(NCORES))
    nc = build_fused()
    res = run_bass_kernel_spmd(nc, [inmap_fused(h, c) for c in cores], core_ids=cores)
    outc = [res.results[c]["out"] for c in cores]
    out = np.concatenate([o.reshape(B, POS, D) for o in outc], axis=1)
    return np.ascontiguousarray(out, dtype=np.float32)


# version 11 (final)
# speedup vs baseline: 1.1110x; 1.0622x over previous
import os
from contextlib import ExitStack

import numpy as np
import ml_dtypes
import concourse.bass as bass
import concourse.mybir as mybir
from concourse.bass_utils import run_bass_kernel_spmd

F32 = mybir.dt.float32
BF16 = mybir.dt.bfloat16
AF = mybir.ActivationFunctionType
ALU = mybir.AluOpType
AX = mybir.AxisListType

NCORES = 8
TOK = 2048
NT = TOK // 128
D = 1024
NC_D = D // 128
B = 4
SEQ = 4096
POS = SEQ // NCORES
EPS_LN = 1e-5


class Sem:
    def __init__(self, h):
        self.h = h
        self.n = 0

    def inc(self, inst, k=1):
        inst.then_inc(self.h, k)
        self.n += k
        return self.n



def _f32(x):
    return np.ascontiguousarray(x, dtype=np.float32)


def prep_host(inp):
    h = {}
    x = np.asarray(inp["L0_attn_iQ"], np.float32)
    xt3 = np.ascontiguousarray(x.reshape(B * SEQ, D).T).reshape(D, B, SEQ)
    h["xT_rot"] = [np.ascontiguousarray(np.roll(xt3, -POS * c, axis=2).reshape(D, B * SEQ))
                   for c in range(NCORES)]
    h["x_own"] = [np.ascontiguousarray(x[:, POS * c:POS * (c + 1), :].reshape(TOK, D))
                  for c in range(NCORES)]

    ba = np.asarray(inp["L0_attn_ba"], np.float32)
    Wo0 = np.asarray(inp["L0_attn_Wo"], np.float32)
    h["WaT"] = _f32(np.asarray(inp["L0_attn_Wa"]).T)
    h["baQ"] = _f32(ba[0:D].reshape(NC_D, 128).T)
    h["baK"] = _f32(ba[D:2 * D].reshape(NC_D, 128).T)
    h["baVt"] = _f32(np.tile(ba[2 * D:3 * D].reshape(1, -1), (128, 1)))
    h["snbT"] = _f32(np.tile(np.asarray(inp["L0_attn_sn_bias"]).reshape(1, 1), (128, 1)))
    h["WoTh"] = _f32(np.concatenate(
        [Wo0.T.reshape(16, 64, D).transpose(1, 0, 2).reshape(64, 16 * D),
         np.zeros((64, 16 * D), np.float32)], axis=0))
    h["boT"] = _f32(np.tile(np.asarray(inp["L0_attn_bo"]).reshape(1, -1), (128, 1)))
    rel = np.asarray(inp["L0_attn_rel_emb"], np.float32)
    idx = np.clip(np.arange(511) - 255, -16, 16) + 16
    h["relXT_std"] = _f32(rel[idx].T)
    idxA = np.clip(np.arange(255) - 255, -16, 16) + 16
    idxB = np.clip(np.arange(255) + 1, -16, 16) + 16
    h["relXT_wA"] = [_f32(rel[np.full(255, 32) if c == 0 else idxA].T) for c in range(NCORES)]
    h["relXT_wB"] = [_f32(rel[np.full(255, 0) if c == NCORES - 1 else idxB].T) for c in range(NCORES)]
    NJ = SEQ // 128
    h["chi"] = []
    h["augsel"] = []
    for c in range(NCORES):
        regions = np.zeros((NJ, 4), np.int8)
        for j in range(NJ):
            for t in range(4):
                if j == (t - 1) % NJ or j == t or j == t + 1:
                    regions[j, t] = 0
                elif j < t - 1 or (c > 0 and j >= NJ - 4 * c):
                    regions[j, t] = 1
                else:
                    regions[j, t] = 2
        variants, vmap, vj = [], {}, np.zeros(NJ, np.int64)
        for j in range(NJ):
            key = tuple(regions[j])
            if key not in vmap:
                vmap[key] = len(variants)
                variants.append(key)
            vj[j] = vmap[key]
        assert len(variants) <= 8, (c, len(variants))
        chi = np.zeros((8, SEQ), np.float32)
        for j in range(NJ):
            chi[vj[j], 128 * j:128 * (j + 1)] = 1.0
        h["chi"].append(chi)
        asel = np.zeros((4, 64, 8), np.float32)
        for v, key in enumerate(variants):
            for s in range(4):
                if key[s] == 1:
                    asel[s, :, v] = rel[0]
                elif key[s] == 2:
                    asel[s, :, v] = rel[32]
        h["augsel"].append(np.ascontiguousarray(asel.reshape(4 * 64, 8)))

    h["rwT"] = _f32(np.asarray(inp["L0_ffn_router_w"]).T)
    h["rb_row"] = _f32(np.asarray(inp["L0_ffn_router_b"]).reshape(1, -1))
    wu = np.asarray(inp["L0_ffn_w_up"], np.float32)
    wd = np.asarray(inp["L0_ffn_w_down"], np.float32)
    swu = np.asarray(inp["L0_ffn_shared_w_up"], np.float32)[0]
    swd = np.asarray(inp["L0_ffn_shared_w_down"], np.float32)[0]
    h["upT"] = _f32(np.stack([swu.T] + [wu[e].T for e in range(8)]))
    h["dnT"] = _f32(np.stack([swd.T] + [wd[e].T for e in range(8)]))

    h["qwT"] = _f32(np.asarray(inp["L1_attn_q_w"]).T)
    h["kwT"] = _f32(np.asarray(inp["L1_attn_k_w"]).T)
    h["vwT"] = _f32(np.asarray(inp["L1_attn_v_w"]).T)
    h["owT"] = _f32(np.asarray(inp["L1_attn_out_w"]).T)
    h["qw_nat"] = _f32(np.asarray(inp["L1_attn_q_w"]))
    h["qb1_pc"] = _f32(np.asarray(inp["L1_attn_q_b"]).reshape(NC_D, 128).T)
    h["qbBt"] = _f32(np.tile(np.asarray(inp["L1_attn_q_b"]).reshape(1, -1), (128, 1)))
    h["kbBt"] = _f32(np.tile(np.asarray(inp["L1_attn_k_b"]).reshape(1, -1), (128, 1)))
    h["vbBt"] = _f32(np.tile(np.asarray(inp["L1_attn_v_b"]).reshape(1, -1), (128, 1)))
    h["obBt"] = _f32(np.tile(np.asarray(inp["L1_attn_out_b"]).reshape(1, -1), (128, 1)))
    h["grepT"] = _f32(np.asarray(inp["L1_attn_grep_w"]).T)
    h["grepb_t"] = _f32(np.tile(np.asarray(inp["L1_attn_grep_b"]), 16).reshape(1, 128))
    h["gaBt"] = _f32(np.tile(np.asarray(inp["L1_attn_grep_a"]).reshape(1, -1), (128, 1)))
    rel1 = np.asarray(inp["L1_attn_rel_emb"], np.float32)
    pos4 = np.arange(4)
    relm = pos4[None, :] - pos4[:, None]
    bucket = (relm > 0).astype(np.int64) * 16 + np.abs(relm)
    h["pbBt"] = _f32(np.tile(rel1[bucket].transpose(2, 0, 1).reshape(1, 256), (128, 1)))

    h["WgT"] = _f32(np.asarray(inp["L1_ffn_Wg"]).T)
    h["bg_row"] = _f32(np.asarray(inp["L1_ffn_bg"]).reshape(1, -1))
    h["WiT"] = _f32(np.asarray(inp["L1_ffn_Wi"]).T)
    h["WoT1"] = _f32(np.asarray(inp["L1_ffn_Wo"]).T)
    h["AT"] = _f32(np.asarray(inp["L1_ffn_A"]).transpose(0, 2, 1).reshape(4 * D, 4))
    Bm = np.asarray(inp["L1_ffn_B"], np.float32)
    h["BTc"] = _f32(np.concatenate([Bm[e].T for e in range(4)], axis=1))

    h["ident"] = np.eye(128, dtype=np.float32)
    h["halfsel"] = np.stack([np.r_[np.ones(64), np.zeros(64)],
                             np.r_[np.zeros(64), np.ones(64)]]).astype(np.float32)
    return h



def emit_xT(nc, blk, sems, x_hbm, xT, stage, idb, pbank2, marks, tag):
    t_tr, v_tr = sems["t_" + tag], sems["v_" + tag]
    gs = [sems["g_" + tag + "0"], sems["g_" + tag + "1"]]
    g_bases = [gs[0].n, gs[1].n]
    t_base, v_base = t_tr.n, v_tr.n

    def g(gp):
        for i in range(NT):
            if i >= 2:
                gp.wait_ge(t_tr.h, t_base + (i - 1) * NC_D)
            gs[i % 2].inc(gp.dma_start(stage[:, (i % 2) * D:(i % 2 + 1) * D],
                                       x_hbm[i * 128:(i + 1) * 128, :]), 16)
    blk.gpsimd(g)

    def t(te):
        for i in range(NT):
            te.wait_ge(gs[i % 2].h, g_bases[i % 2] + 16 * (i // 2 + 1))
            for c in range(NC_D):
                k = i * NC_D + c
                if k >= 2:
                    te.wait_ge(v_tr.h, v_base + k - 1)
                dst = pbank2[k % 2][:, :].bitcast(BF16)[:, 0:128]
                t_tr.inc(te.transpose(
                    dst, stage[:, (i % 2) * D + c * 128:(i % 2) * D + (c + 1) * 128],
                    idb[:, :]))
    blk.tensor(t)

    def v(ve):
        for i in range(NT):
            for c in range(NC_D):
                k = i * NC_D + c
                ve.wait_ge(t_tr.h, t_base + k + 1)
                src = pbank2[k % 2][:, :].bitcast(BF16)[:, 0:128]
                v_tr.inc(ve.tensor_copy(xT[:, c * TOK + i * 128:c * TOK + (i + 1) * 128], src))
    blk.vector(v)
    marks["xT_done_" + tag] = (v_tr, v_tr.n)


def emit_ln_epilogue(nc, blk, sems, acc_src_fn, x_res_hbm, out_hbm, stage, rs2,
                     x2t, sqd, stats, marks, wait_marks, tag, out_extra=None,
                     eps_col=None, vf=None, sf=None):
    v_e, s_e = sems["v_" + tag], sems["s_" + tag]
    xos = [sems["x_" + tag + "0"], sems["x_" + tag + "1"]]
    xo_bases = [xos[0].n, xos[1].n]
    grs = [sems["g_" + tag + "0"], sems["g_" + tag + "1"]]
    gr_bases = [grs[0].n, grs[1].n]
    v_base, s_base = v_e.n, s_e.n

    def g(gp):
        for sm, cnt in wait_marks:
            gp.wait_ge(sm.h, cnt)
        for i in range(NT):
            if i >= 2:
                gp.wait_ge(v_e.h, v_base + (i - 1))
            grs[i % 2].inc(gp.dma_start(stage[:, (i % 2) * D:(i % 2 + 1) * D],
                                        x_res_hbm[i * 128:(i + 1) * 128, :]), 16)
    blk.gpsimd(g)

    def v(ve):
        ve.memset(eps_col, EPS_LN)
        for sm, cnt in wait_marks:
            ve.wait_ge(sm.h, cnt)
        for i in range(NT):
            s = i % 2
            rs = rs2[:, s * D:(s + 1) * D]
            xm = x2t[:, s * D:(s + 1) * D]
            mu = stats[:, 0:1]
            ve.wait_ge(grs[i % 2].h, gr_bases[i % 2] + 16 * (i // 2 + 1))
            if i >= 2:
                ve.wait_ge(xos[i % 2].h, xo_bases[i % 2] + 16 * (i // 2))
                ve.wait_ge(s_e.h, s_base + (i - 1))
            ve.tensor_add(rs, acc_src_fn(i), stage[:, s * D:(s + 1) * D]); ve.drain()
            ve.reduce_sum(mu, rs, AX.X); ve.drain()
            ve.tensor_scalar(mu, mu, 1.0 / D, None, ALU.mult); ve.drain()
            v_e.inc(ve.tensor_scalar(xm, rs, mu, None, ALU.subtract))
    blk.vector(v)

    def sc(se):
        for i in range(NT):
            s = i % 2
            rs = rs2[:, s * D:(s + 1) * D]
            xm = x2t[:, s * D:(s + 1) * D]
            ss = stats[:, 1:2]
            lnv = stats[:, 2:3]
            rstd = stats[:, 3:4]
            se.wait_ge(v_e.h, v_base + i + 1)
            se.activation(sqd[:, :], xm, AF.Square, accum_out=ss)
            se.drain()
            se.activation(lnv, ss, AF.Ln, bias=eps_col, scale=1.0 / D)
            se.drain()
            se.activation(rstd, lnv, AF.Exp, scale=-0.5)
            se.drain()
            inst = se.activation(rs, xm, AF.Identity, scale=rstd)
            if out_extra is not None:
                out_extra(i, se, rs)
            s_e.inc(inst)
    blk.scalar(sc)

    def sy(sn):
        for i in range(NT):
            s = i % 2
            rs = rs2[:, s * D:(s + 1) * D]
            sn.wait_ge(s_e.h, s_base + i + 1)
            xos[i % 2].inc(sn.dma_start(out_hbm[i * 128:(i + 1) * 128, :], rs), 16)
        sn.wait_ge(xos[0].h, xo_bases[0] + 16 * (NT // 2))
        sn.wait_ge(xos[1].h, xo_bases[1] + 16 * (NT // 2))
    blk.sync(sy)


SEM_NAMES_COMMON = ["g_x", "t_x", "v_x", "g_w", "g_m", "t_m", "v_m", "s_m",
                    "t_a", "v_a", "s_a", "g_ep", "v_ep", "s_ep", "x_ep",
                    "t_b", "v_b", "s_b", "g_b", "x_b"]


def new_launch():
    nc = bass.Bass("TRN2", target_bir_lowering=False)
    return nc


def emit_gate_smallops(blk, rsc, rbB, c_all, pb, pbase, t_rt, v_rt, s_rt, NE,
                       renorm=True):
    v_base = v_rt.n

    def v2(ve):
        lg = rsc[:, 0:NE]; ex = rsc[:, 8:8 + NE]; pr = rsc[:, 16:16 + NE]
        m1 = rsc[:, 24:25]; nm = rsc[:, 25:26]; se_ = rsc[:, 26:27]
        rse = rsc[:, 27:28]; m2 = rsc[:, 28:29]; den = rsc[:, 29:30]
        rden = rsc[:, 30:31]; msk = rsc[:, 32:32 + NE]; p2 = rsc[:, 40:40 + NE]
        ve.wait_ge(t_rt.h, 1)
        v_rt.inc(ve.tensor_copy(rbB[:, 0:NE], pb[pbase][:, 0:NE]))
        ve.drain()
        for i in range(NT):
            bank = pb[pbase + (i + 1) % 2]
            ve.wait_ge(t_rt.h, i + 2)
            ve.tensor_add(lg, bank[:, 0:NE], rbB[:, 0:NE]); ve.drain()
            v_rt.inc(ve.reduce_max(nm, lg, AX.X, negate=True))
            ve.wait_ge(s_rt.h, i + 1)
            ve.reciprocal(rse, se_); ve.drain()
            ve.tensor_scalar_mul(pr, ex, rse); ve.drain()
            ve.reduce_max(m1, pr, AX.X); ve.drain()
            ve.tensor_scalar(msk, pr, m1, None, ALU.is_ge); ve.drain()
            ve.scalar_tensor_tensor(p2, msk, -1e30, pr, ALU.mult, ALU.add); ve.drain()
            ve.reduce_max(m2, p2, AX.X); ve.drain()
            if renorm:
                ve.tensor_add(den, m1, m2); ve.drain()
                ve.reciprocal(rden, den); ve.drain()
                ve.tensor_scalar(msk, pr, m2, None, ALU.is_ge); ve.drain()
                ve.scalar_tensor_tensor(c_all[:, i * NE:(i + 1) * NE],
                                        msk, rden, pr, ALU.mult, ALU.mult)
            else:
                ve.tensor_scalar(msk, pr, m2, None, ALU.is_ge); ve.drain()
                ve.tensor_mul(c_all[:, i * NE:(i + 1) * NE], msk, pr)
            v_rt.inc(ve.drain())
    blk.vector(v2)

    def s2(sc):
        lg = rsc[:, 0:NE]; ex = rsc[:, 8:8 + NE]
        nm = rsc[:, 25:26]; se_ = rsc[:, 26:27]
        for i in range(NT):
            sc.wait_ge(v_rt.h, v_base + 1 + 2 * i + 1)
            s_rt.inc(sc.activation(ex, lg, AF.Exp, bias=nm, scale=1.0, accum_out=se_))
    blk.scalar(s2)



def emit_l0ffn(nc, blk, ctx, dram, entry, shared=None):
    x1, rwT, rb_row, upT, dnT, ident, x2 = (
        dram["x1"], dram["rwT"], dram["rb_row"], dram["upT"], dram["dnT"],
        dram["ident"], dram["x2"])

    ctx2 = ExitStack()
    a = lambda nm, sh, dt: ctx2.enter_context(nc.sbuf_tensor("F_" + nm, sh, dt))
    xT = a("xT", [128, NC_D * TOK], BF16)
    stage = a("stage", [128, 2 * D], BF16)
    wub = [a(f"wub{i}", [128, NC_D * 2048], BF16) for i in range(2)]
    wdb = a("wdb", [128, 16 * D], BF16)
    hsil = a("hsil", [128, 16 * 512], BF16)
    acc = a("acc", [128, NT * D], BF16)
    rwn = a("rwn", [128, NC_D * 8], BF16)
    rbr = a("rbr", [1, 8], BF16)
    rbB = a("rbB", [128, 8], F32)
    ones1 = a("ones1", [1, 128], BF16)
    idb = a("idb", [128, 128], BF16)
    c_all = a("c_all", [128, NT * 8], F32)
    rsc = a("rsc", [128, 64], F32)
    rs2 = a("rs2", [128, 2 * D], F32)
    x2t = a("x2t", [128, 2 * D], F32)
    sqd = a("sqd", [128, D], F32)
    stats = a("stats", [128, 8], F32)
    epsc = a("epsc", [128, 1], F32)

    pb = [ctx2.enter_context(nc.psum_tensor(f"F_pb{i}", [128, 512], F32)) for i in range(8)]

    if True:
        sems = mk_block_sems(ctx, nc, "F_",
                ["g_x0", "g_x1", "t_x", "v_x", "g_m", "t_rt", "v_rt", "s_rt",
                 "g_wu0", "g_wu1", "g_wd", "t_h", "s_sil", "t_dn", "v_dn",
                 "g_ep0", "g_ep1", "v_ep", "s_ep", "x_ep0", "x_ep1"], shared)
        marks = {}
        emit_entry_barrier(blk, entry)

        g_m = sems["g_m"]

        def g0(gp):
            g_m.inc(gp.dma_start(idb[:, :], ident[:, :]), 16)
            g_m.inc(gp.dma_start(rwn[:, :].rearrange("p (c e) -> p c e", e=8),
                                 rwT[:, :].rearrange("(c p) e -> p c e", p=128)), 16)
            g_m.inc(gp.dma_start(rbr[:, :], rb_row[:, :]), 16)
        blk.gpsimd(g0)
        v_rt = sems["v_rt"]

        def v0(ve):
            v_rt.inc(ve.memset(ones1[:, :], 1.0))
        blk.vector(v0)
        ONES_OK = v_rt.n

        t_x = sems["t_x"]

        def tw(te):
            te.wait_ge(g_m.h, 48)
        blk.tensor(tw)
        emit_xT(nc, blk, {"g_x0": sems["g_x0"], "g_x1": sems["g_x1"],
                          "t_x": t_x, "v_x": sems["v_x"]},
                x1, xT, stage, idb, pb[0:2], marks, "x")
        XT = marks["xT_done_x"]

        t_rt, s_rt = sems["t_rt"], sems["s_rt"]

        def t2(te):
            te.wait_ge(g_m.h, 48)
            te.wait_ge(v_rt.h, ONES_OK)
            t_rt.inc(te.matmul(pb[2][:, 0:8], ones1[:, :], rbr[:, :], start=True, stop=True))
            te.wait_ge(XT[0].h, XT[1])
            for i in range(NT):
                bank = pb[2 + (i + 1) % 2]
                if i == 1:
                    te.wait_ge(v_rt.h, ONES_OK + 1)
                elif i >= 2:
                    te.wait_ge(v_rt.h, ONES_OK + 1 + 2 * (i - 2) + 1)
                for c in range(NC_D):
                    inst = te.matmul(bank[:, 0:8],
                                     xT[:, c * TOK + i * 128:c * TOK + (i + 1) * 128],
                                     rwn[:, c * 8:(c + 1) * 8],
                                     start=(c == 0), stop=(c == NC_D - 1))
                t_rt.inc(inst)
        blk.tensor(t2)

        emit_gate_smallops(blk, rsc, rbB, c_all, pb, 2, t_rt, v_rt, s_rt, 8)
        RT_V_DONE = v_rt.n

        g_wu = [sems["g_wu0"], sems["g_wu1"]]
        g_wd = sems["g_wd"]
        t_h, s_sil, t_dn, v_dn = (sems["t_h"], sems["s_sil"],
                                  sems["t_dn"], sems["v_dn"])

        def g3(gp):
            for ei in range(9):
                p = ei % 2
                if ei >= 2:
                    gp.wait_ge(t_h.h, 64 * (ei - 1))
                g_wu[p].inc(gp.dma_start(wub[p][:, :].rearrange("p (c f) -> p c f", f=2048),
                                         upT[ei, :, :].rearrange("(c p) f -> p c f", p=128)), 16)
                if ei >= 1:
                    gp.wait_ge(t_dn.h, 32 * ei)
                g_wd.inc(gp.dma_start(wdb[:, :].rearrange("p (c f) -> p c f", f=D),
                                      dnT[ei, :, :].rearrange("(c p) f -> p c f", p=128)), 16)
        blk.gpsimd(g3)

        def t3(te):
            te.wait_ge(v_rt.h, RT_V_DONE)
            hg = 0
            dg = 0
            for ei in range(9):
                te.wait_ge(g_wu[ei % 2].h, 16 * (ei // 2 + 1))
                for tg in range(4):
                    for fc in range(16):
                        bank = pb[4 + hg % 2]
                        if hg >= 2:
                            te.wait_ge(s_sil.h, hg - 1)
                        for c in range(NC_D):
                            inst = te.matmul(
                                bank[:, :],
                                wub[ei % 2][:, c * 2048 + fc * 128:c * 2048 + (fc + 1) * 128],
                                xT[:, c * TOK + tg * 512:c * TOK + (tg + 1) * 512],
                                start=(c == 0), stop=(c == NC_D - 1))
                        t_h.inc(inst)
                        hg += 1
                    te.wait_ge(g_wd.h, 16 * (ei + 1))
                    te.wait_ge(s_sil.h, hg)
                    for tl in range(4):
                        for nd in range(2):
                            bank = pb[6 + dg % 2]
                            if dg >= 2:
                                te.wait_ge(v_dn.h, dg - 1)
                            for fc in range(16):
                                inst = te.matmul(
                                    bank[:, :],
                                    hsil[:, fc * 512 + tl * 128:fc * 512 + (tl + 1) * 128],
                                    wdb[:, fc * D + nd * 512:fc * D + (nd + 1) * 512],
                                    start=(fc == 0), stop=(fc == 15))
                            t_dn.inc(inst)
                            dg += 1
        blk.tensor(t3)

        def s3(sc):
            hg = 0
            for ei in range(9):
                for tg in range(4):
                    gidx = ei * 4 + tg
                    for fc in range(16):
                        sc.wait_ge(t_h.h, hg + 1)
                        if gidx >= 1 and fc == 0:
                            sc.wait_ge(t_dn.h, (gidx - 1) * 8 + 8)
                        bank = pb[4 + hg % 2]
                        s_sil.inc(sc.activation(hsil[:, fc * 512:(fc + 1) * 512],
                                                bank[:, :], AF.Silu))
                        hg += 1
        blk.scalar(s3)

        def v3(ve):
            dg = 0
            for ei in range(9):
                for tg in range(4):
                    for tl in range(4):
                        tau = tg * 4 + tl
                        for nd in range(2):
                            ve.wait_ge(t_dn.h, dg + 1)
                            bank = pb[6 + dg % 2]
                            dst = acc[:, tau * D + nd * 512:tau * D + (nd + 1) * 512]
                            if ei == 0:
                                v_dn.inc(ve.tensor_copy(dst, bank[:, :]))
                            else:
                                ccol = c_all[:, tau * 8 + (ei - 1):tau * 8 + ei]
                                v_dn.inc(ve.scalar_tensor_tensor(
                                    dst, bank[:, :], ccol, dst, ALU.mult, ALU.add))
                            dg += 1
        blk.vector(v3)
        DN_DONE = (v_dn, v_dn.n + 0)
        DN_TOTAL = 9 * 32
        assert v_dn.n == DN_TOTAL

        emit_ln_epilogue(nc, blk,
                         {"g_ep0": sems["g_ep0"], "g_ep1": sems["g_ep1"],
                          "v_ep": sems["v_ep"], "s_ep": sems["s_ep"],
                          "x_ep0": sems["x_ep0"], "x_ep1": sems["x_ep1"]},
                         lambda i: acc[:, i * D:(i + 1) * D],
                         x1, x2, stage, rs2, x2t, sqd, stats, marks,
                         [(v_dn, DN_TOTAL)], "ep", eps_col=epsc[:, :])
    ctx2.close()
    return [(sems["x_ep0"], sems["x_ep0"].n), (sems["x_ep1"], sems["x_ep1"].n)]


def inmap_l0ffn(h, c, x1_chunks):
    return dict(x1=x1_chunks[c], rwT=h["rwT"], rb_row=h["rb_row"],
                upT=h["upT"], dnT=h["dnT"], ident=h["ident"])



def emit_l1ffn(nc, blk, ctx, dram, entry, shared=None):
    (x3, WgT, bg_row, WiT, WoT1, AT, BTc, ident, out) = (
        dram["x3"], dram["WgT"], dram["bg_row"], dram["WiT"], dram["WoT1"],
        dram["AT"], dram["BTc"], dram["ident"], dram["out"])

    ctx2 = ExitStack()
    a = lambda nm, sh, dt: ctx2.enter_context(nc.sbuf_tensor("G_" + nm, sh, dt))
    xT = a("xT", [128, NC_D * TOK], BF16)
    stage = a("stage", [128, 2 * D], BF16)
    wgn = a("wgn", [128, NC_D * 4], BF16)
    bgr = a("bgr", [1, 4], BF16)
    bgB = a("bgB", [128, 8], F32)
    ones1 = a("ones1", [1, 128], BF16)
    idb = a("idb", [128, 128], BF16)
    w_all = a("w_all", [128, NT * 4], F32)
    rsc = a("rsc", [128, 64], F32)
    atn = a("atn", [128, 4 * NC_D * 4], BF16)
    btn = a("btn", [128, 4 * 2048], BF16)
    xa = a("xa", [128, 4 * TOK], BF16)
    win = a("win", [128, NC_D * 2048], BF16)
    won = a("won", [128, 16 * D], BF16)
    baset = a("baset", [128, 2 * 2048], BF16)
    rbuf = a("rbuf", [128, 4 * 2048], BF16)
    hbt = a("hbt", [128, 2048], BF16)
    hbT = a("hbT", [128, 16 * 128], BF16)
    accl = a("accl", [128, NT * D], BF16)
    ep32 = win[:, :].bitcast(F32)
    rs2 = ep32[:, 0:2 * D]
    x2t = ep32[:, 2 * D:4 * D]
    sqd = ep32[:, 4 * D:5 * D]
    stats = a("stats", [128, 8], F32)
    epsc = a("epsc", [128, 1], F32)

    pb = [ctx2.enter_context(nc.psum_tensor(f"G_pb{i}", [128, 512], F32)) for i in range(8)]

    if True:
        sems = mk_block_sems(ctx, nc, "G_",
                ["g_x0", "g_x1", "t_x", "v_x", "g_m", "t_rt", "v_rt", "s_rt", "g_w",
                 "t_xa", "v_xa", "t_bs", "v_bs", "t_rl", "s_rl", "v_hb",
                 "t_tp", "v_tp", "t_dn", "v_dn",
                 "g_ep0", "g_ep1", "v_ep", "s_ep", "x_ep0", "x_ep1"], shared)
        marks = {}
        emit_entry_barrier(blk, entry)

        g_m = sems["g_m"]

        def g0(gp):
            g_m.inc(gp.dma_start(idb[:, :], ident[:, :]), 16)
            g_m.inc(gp.dma_start(wgn[:, :].rearrange("p (c e) -> p c e", e=4),
                                 WgT[:, :].rearrange("(c p) e -> p c e", p=128)), 16)
            g_m.inc(gp.dma_start(bgr[:, :], bg_row[:, :]), 16)
            g_m.inc(gp.dma_start(atn[:, :].rearrange("p (k r) -> p k r", r=4),
                                 AT[:, :].rearrange("(k p) r -> p k r", p=128)), 16)
        blk.gpsimd(g0)
        v_rt = sems["v_rt"]

        def v0(ve):
            v_rt.inc(ve.memset(ones1[:, :], 1.0))
        blk.vector(v0)

        def vpad(ve):
            ve.memset(xa[:, :], 0.0)
            ve.memset(btn[:, :], 0.0)
        blk.vector(vpad)

        t_x = sems["t_x"]

        def tw(te):
            te.wait_ge(g_m.h, 64)
        blk.tensor(tw)
        emit_xT(nc, blk, {"g_x0": sems["g_x0"], "g_x1": sems["g_x1"],
                          "t_x": t_x, "v_x": sems["v_x"]},
                x3, xT, stage, idb, pb[0:2], marks, "x")
        XT = marks["xT_done_x"]

        t_rt, s_rt = sems["t_rt"], sems["s_rt"]

        def t2(te):
            te.wait_ge(g_m.h, 64)
            te.wait_ge(v_rt.h, 1)
            t_rt.inc(te.matmul(pb[2][:, 0:4], ones1[:, :], bgr[:, :], start=True, stop=True))
            te.wait_ge(XT[0].h, XT[1])
            for i in range(NT):
                bank = pb[2 + (i + 1) % 2]
                if i == 1:
                    te.wait_ge(v_rt.h, 2)
                elif i >= 2:
                    te.wait_ge(v_rt.h, 2 + 2 * (i - 2) + 1)
                for c in range(NC_D):
                    inst = te.matmul(bank[:, 0:4],
                                     xT[:, c * TOK + i * 128:c * TOK + (i + 1) * 128],
                                     wgn[:, c * 4:(c + 1) * 4],
                                     start=(c == 0), stop=(c == NC_D - 1))
                t_rt.inc(inst)
        blk.tensor(t2)
        emit_gate_smallops(blk, rsc, bgB, w_all, pb, 2, t_rt, v_rt, s_rt, 4,
                           renorm=False)
        GT_V_DONE = v_rt.n

        t_xa, v_xa = sems["t_xa"], sems["v_xa"]

        def t3(te):
            te.wait_ge(g_m.h, 64)
            te.wait_ge(XT[0].h, XT[1])
            k = 0
            for e in range(4):
                for tg in range(4):
                    bank = pb[4 + k % 2]
                    if k >= 2:
                        te.wait_ge(v_xa.h, k - 1)
                    for c in range(NC_D):
                        inst = te.matmul(bank[0:4, :],
                                         atn[:, (e * NC_D + c) * 4:(e * NC_D + c + 1) * 4],
                                         xT[:, c * TOK + tg * 512:c * TOK + (tg + 1) * 512],
                                         start=(c == 0), stop=(c == NC_D - 1))
                    t_xa.inc(inst)
                    k += 1
        blk.tensor(t3)

        def v3(ve):
            k = 0
            for e in range(4):
                for tg in range(4):
                    ve.wait_ge(t_xa.h, k + 1)
                    bank = pb[4 + k % 2]
                    v_xa.inc(ve.tensor_copy(xa[0:4, e * TOK + tg * 512:e * TOK + (tg + 1) * 512],
                                            bank[0:4, :]))
                    k += 1
        blk.vector(v3)
        XA_DONE = v_xa.n

        g_w = sems["g_w"]
        t_bs, v_bs = sems["t_bs"], sems["v_bs"]
        t_rl, s_rl = sems["t_rl"], sems["s_rl"]
        v_hb = sems["v_hb"]
        t_tp, v_tp = sems["t_tp"], sems["v_tp"]
        t_dn, v_dn = sems["t_dn"], sems["v_dn"]

        def gw(gp):
            for fh in range(2):
                if fh == 1:
                    gp.wait_ge(t_dn.h, 32)
                g_w.inc(gp.dma_start(win[:, :].rearrange("p (c f) -> p c f", f=2048),
                                     WiT[:, fh * 2048:(fh + 1) * 2048]
                                     .rearrange("(c p) f -> p c f", p=128)), 16)
                g_w.inc(gp.dma_start(won[:, :].rearrange("p (c f) -> p c f", f=D),
                                     WoT1[fh * 2048:(fh + 1) * 2048, :]
                                     .rearrange("(c p) f -> p c f", p=128)), 16)
                if fh == 0:
                    gp.wait_ge(v_xa.h, 1)
                g_w.inc(gp.dma_start(btn[0:4, :].rearrange("p (e f) -> p e f", f=2048),
                                     BTc[:, :].rearrange("p (e f) -> p e f", f=4096)[:, :, fh * 2048:(fh + 1) * 2048]), 16)
        blk.gpsimd(gw)

        def t4(te):
            te.wait_ge(v_rt.h, GT_V_DONE)
            te.wait_ge(v_xa.h, XA_DONE)
            nb = 0
            nr = 0
            ntp = 0
            nd = 0
            for fh in range(2):
                te.wait_ge(g_w.h, 48 * fh + 48)
                for tau in range(NT):
                    for ft in range(4):
                        bank = pb[0 + nb % 2]
                        if nb >= 2:
                            te.wait_ge(v_bs.h, nb - 1)
                        for c in range(NC_D):
                            inst = te.matmul(
                                bank[:, :],
                                xT[:, c * TOK + tau * 128:c * TOK + (tau + 1) * 128],
                                win[:, c * 2048 + ft * 512:c * 2048 + (ft + 1) * 512],
                                start=(c == 0), stop=(c == NC_D - 1))
                        t_bs.inc(inst)
                        nb += 1
                    te.wait_ge(v_bs.h, nb)
                    for e in range(4):
                        for ft in range(4):
                            bank = pb[2 + nr % 2]
                            if nr >= 2:
                                te.wait_ge(s_rl.h, nr - 1)
                            slot = (fh * NT + tau) % 2
                            te.matmul(bank[:, :], idb[:, :],
                                      baset[:, slot * 2048 + ft * 512:
                                            slot * 2048 + (ft + 1) * 512],
                                      start=True, stop=False, skip_group_check=True)
                            inst = te.matmul(
                                bank[:, :],
                                xa[:, e * TOK + tau * 128:e * TOK + (tau + 1) * 128],
                                btn[:, e * 2048 + ft * 512:e * 2048 + (ft + 1) * 512],
                                start=False, stop=True, skip_group_check=True)
                            t_rl.inc(inst)
                            nr += 1
                    te.wait_ge(v_hb.h, (fh * NT + tau) + 1)
                    for tc in range(16):
                        bank = pb[4 + ntp % 2]
                        if ntp >= 2:
                            te.wait_ge(v_tp.h, ntp - 1)
                        dstp = bank[:, :].bitcast(BF16)[:, 0:128]
                        t_tp.inc(te.transpose(dstp, hbt[:, tc * 128:(tc + 1) * 128], idb[:, :]))
                        ntp += 1
                    te.wait_ge(v_tp.h, ntp)
                    for ndt in range(2):
                        bank = pb[6 + nd % 2]
                        if nd >= 2:
                            te.wait_ge(v_dn.h, nd - 1)
                        for tc in range(16):
                            inst = te.matmul(
                                bank[:, :],
                                hbT[:, tc * 128:(tc + 1) * 128],
                                won[:, tc * D + ndt * 512:tc * D + (ndt + 1) * 512],
                                start=(tc == 0), stop=(tc == 15))
                        t_dn.inc(inst)
                        nd += 1
        blk.tensor(t4)

        def s4(sc):
            nr = 0
            for fh in range(2):
                for tau in range(NT):
                    k = fh * NT + tau
                    for e in range(4):
                        wcol = w_all[:, tau * 4 + e:tau * 4 + e + 1]
                        for ft in range(4):
                            sc.wait_ge(t_rl.h, nr + 1)
                            if k >= 1 and e == 0 and ft == 0:
                                sc.wait_ge(v_hb.h, k)
                            bank = pb[2 + nr % 2]
                            s_rl.inc(sc.activation(
                                rbuf[:, e * 2048 + ft * 512:e * 2048 + (ft + 1) * 512],
                                bank[:, :], AF.Relu, scale=wcol))
                            nr += 1
        blk.scalar(s4)

        def v456(ve):
            nb = 0
            ntp = 0
            nd = 0
            for fh in range(2):
                for tau in range(NT):
                    k = fh * NT + tau
                    slot = k % 2
                    for ft in range(4):
                        ve.wait_ge(t_bs.h, nb + 1)
                        if k >= 2 and ft == 0:
                            ve.wait_ge(t_rl.h, 16 * (k - 1))
                        bank = pb[0 + nb % 2]
                        v_bs.inc(ve.tensor_copy(
                            baset[:, slot * 2048 + ft * 512:slot * 2048 + (ft + 1) * 512],
                            bank[:, :]))
                        nb += 1
                    ve.wait_ge(s_rl.h, 16 * (k + 1))
                    if k >= 1:
                        ve.wait_ge(t_tp.h, 16 * k)
                    ve.tensor_add(hbt[:, 0:2048], rbuf[:, 0:2048], rbuf[:, 2048:4096])
                    ve.drain()
                    ve.tensor_add(hbt[:, 0:2048], hbt[:, 0:2048], rbuf[:, 4096:6144])
                    ve.drain()
                    v_hb.inc(ve.tensor_add(hbt[:, 0:2048], hbt[:, 0:2048],
                                           rbuf[:, 6144:8192]))
                    for tc in range(16):
                        ve.wait_ge(t_tp.h, ntp + 1)
                        bank = pb[4 + ntp % 2]
                        srcp = bank[:, :].bitcast(BF16)[:, 0:128]
                        v_tp.inc(ve.tensor_copy(hbT[:, tc * 128:(tc + 1) * 128], srcp))
                        ntp += 1
                    for ndt in range(2):
                        ve.wait_ge(t_dn.h, nd + 1)
                        bank = pb[6 + nd % 2]
                        dst = accl[:, tau * D + ndt * 512:tau * D + (ndt + 1) * 512]
                        if fh == 0:
                            v_dn.inc(ve.tensor_copy(dst, bank[:, :]))
                        else:
                            v_dn.inc(ve.tensor_add(dst, dst, bank[:, :]))
                        nd += 1
        blk.vector(v456)

        DN_TOTAL = v_dn.n

        emit_ln_epilogue(nc, blk,
                         {"g_ep0": sems["g_ep0"], "g_ep1": sems["g_ep1"],
                          "v_ep": sems["v_ep"], "s_ep": sems["s_ep"],
                          "x_ep0": sems["x_ep0"], "x_ep1": sems["x_ep1"]},
                         lambda i: accl[:, i * D:(i + 1) * D],
                         x3, out, stage, rs2, x2t, sqd, stats, marks,
                         [(v_dn, DN_TOTAL)], "ep", eps_col=epsc[:, :])
    ctx2.close()
    return [(sems["x_ep0"], sems["x_ep0"].n), (sems["x_ep1"], sems["x_ep1"].n)]


def inmap_l1ffn(h, c, x3_chunks):
    return dict(x3=x3_chunks[c], WgT=h["WgT"], bg_row=h["bg_row"], WiT=h["WiT"],
                WoT1=h["WoT1"], AT=h["AT"], BTc=h["BTc"], ident=h["ident"])



def emit_l1attn(nc, blk, ctx, dram, entry, shared=None):
    TR = 9
    (x2, qwT, kwT, vwT, owT, qw_nat, qb1_pc, qbBt, kbBt, vbBt, obBt, grepT,
     grepb_t, gaBt, pbBt, ident, x3) = (
        dram["x2"], dram["qwT"], dram["kwT"], dram["vwT"], dram["owT"],
        dram["qw_nat"], dram["qb1_pc"], dram["qbBt"], dram["kbBt"],
        dram["vbBt"], dram["obBt"], dram["grepT"], dram["grepb_t"],
        dram["gaBt"], dram["pbBt"], dram["ident"], dram["x3"])

    SC = 0.125

    ctx2 = ExitStack()
    a = lambda nm, sh, dt: ctx2.enter_context(nc.sbuf_tensor("L_" + nm, sh, dt))
    xT = a("xT", [128, NC_D * TOK], BF16)
    stage = a("stage", [128, 2 * D], BF16)
    idb = a("idb", [128, 128], BF16)
    ones1 = a("ones1", [1, 128], BF16)
    qwn = a("qwn", [128, NC_D * D], BF16)
    grepb = a("grepb", [128, 8], BF16)
    qbc = a("qbc", [128, NC_D], BF16)
    grepbt = a("grepbt", [1, 128], F32)
    bgrow = a("bgrow", [1, 128], F32)
    bgrow16 = a("bgrow16", [1, 128], BF16)
    WGT = a("WGT", [128, NC_D * 128], BF16)
    biasGB = a("biasGB", [128, 128], F32)
    pbiasB = a("pbiasB", [128, 256], F32)
    grepaB = a("grepaB", [128, 16], F32)
    qbB = a("qbB", [128, D], F32)
    kbB = a("kbB", [128, D], F32)
    vbB = a("vbB", [128, D], F32)
    obB = a("obB", [128, D], F32)
    wdb = [a(f"wdb{i}", [128, NC_D * D], BF16) for i in range(2)]
    qb_ = a("qb_", [128, 8 * D], BF16)
    kb_ = a("kb_", [128, 8 * D], BF16)
    vb_ = a("vb_", [128, 8 * D], BF16)
    ctxb = a("ctxb", [128, 8 * D], BF16)
    ctxT = qwn[:, :]
    acc = a("acc", [128, NT * D], BF16)
    gates = a("gates", [128, 8 * 16], F32)
    g2b = a("g2b", [128, 32], F32)
    sgb = a("sgb", [128, 32], F32)
    scb = a("scb", [128, 2 * 256], F32)
    prb = a("prb", [128, 2 * 256], F32)
    nm64 = a("nm64", [128, 64], F32)
    s64 = a("s64", [128, 64], F32)
    tmpv = stage[:, 0:D]
    rs2 = kb_[:, :].bitcast(F32)[:, 0:2 * D]
    x2t = kb_[:, :].bitcast(F32)[:, 2 * D:4 * D]
    sqd = vb_[:, :].bitcast(F32)[:, 0:D]
    stats = a("stats", [128, 8], F32)
    epsc = a("epsc", [128, 1], F32)

    pb = [ctx2.enter_context(nc.psum_tensor(f"L_pb{i}", [128, 512], F32)) for i in range(8)]

    if True:
        sems = mk_block_sems(ctx, nc, "L_",
                ["g_x0", "g_x1", "t_x", "v_x", "g_m", "g_w0", "g_w1",
                 "t_wg", "v_wg", "t_pj", "v_pj", "t_g", "v_g", "s_g",
                 "v_sc", "s_sc", "v_cx", "t_tp", "v_tp", "t_ou", "v_ou",
                 "g_ep0", "g_ep1", "v_ep", "s_ep", "x_ep0", "x_ep1"], shared)
        marks = {}
        emit_entry_barrier(blk, entry)

        g_m = sems["g_m"]

        def g0(gp):
            g_m.inc(gp.dma_start(idb[:, :], ident[:, :]), 16)
            g_m.inc(gp.dma_start(qwn[:, :].rearrange("p (c f) -> p c f", f=D),
                                 qw_nat[:, :].rearrange("(c p) f -> p c f", p=128)), 16)
            g_m.inc(gp.dma_start(grepb[0:64, :], grepT[:, :]), 16)
            g_m.inc(gp.dma_start(grepb[64:128, :], grepT[:, :]), 16)
            g_m.inc(gp.dma_start(qbc[:, :], qb1_pc[:, :]), 16)
            g_m.inc(gp.dma_start(qbB[:, :], qbBt[:, :]), 16)
            g_m.inc(gp.dma_start(kbB[:, :], kbBt[:, :]), 16)
            g_m.inc(gp.dma_start(vbB[:, :], vbBt[:, :]), 16)
            g_m.inc(gp.dma_start(obB[:, :], obBt[:, :]), 16)
            g_m.inc(gp.dma_start(grepbt[:, :], grepb_t[:, :]), 16)
            g_m.inc(gp.dma_start(grepaB[:, :], gaBt[:, :]), 16)
            g_m.inc(gp.dma_start(pbiasB[:, :], pbBt[:, :]), 16)
        blk.gpsimd(g0)
        GM_ALL = g_m.n

        v_wg = sems["v_wg"]

        def v0(ve):
            ve.memset(ones1[:, :], 1.0)
            ve.wait_ge(g_m.h, GM_ALL)
            ve.tensor_scalar(qbB[:, :], qbB[:, :], SC, None, ALU.mult)
            v_wg.inc(ve.drain())
        blk.vector(v0)
        V0_DONE = v_wg.n

        t_x = sems["t_x"]

        def tw(te):
            te.wait_ge(g_m.h, GM_ALL)
        blk.tensor(tw)
        emit_xT(nc, blk, {"g_x0": sems["g_x0"], "g_x1": sems["g_x1"],
                          "t_x": t_x, "v_x": sems["v_x"]},
                x2, xT, stage, idb, pb[0:2], marks, "x")
        XT = marks["xT_done_x"]
        if TR <= 1:
            def fin(sn):
                sn.wait_ge(XT[0].h, XT[1])
            blk.sync(fin)
            return nc

        t_wg = sems["t_wg"]
        NB = 11

        SUB = int(os.environ.get("L1A_SUB", "3"))

        def t_wgf(te):
            te.wait_ge(v_wg.h, V0_DONE)
            for h in range(16):
                for dc in range(NC_D):
                    k = h * NC_D + dc
                    bank = pb[2 + k % 2]
                    if k >= 2:
                        te.wait_ge(v_wg.h, V0_DONE + k - 1)
                    lhsT = qwn[(h % 2) * 64:(h % 2) * 64 + 64, :]\
                        .rearrange("p (c f) -> p c f", f=D)[:, h // 2, dc * 128:(dc + 1) * 128]
                    t_wg.inc(te.matmul(bank[:, 0:8], lhsT,
                                       grepb[(h % 2) * 64:(h % 2) * 64 + 64, :],
                                       start=True, stop=True))
            if SUB <= 1:
                return
            for h in range(16):
                k2 = 128 + h
                bank = pb[2 + k2 % 2]
                te.wait_ge(v_wg.h, V0_DONE + k2 - 1)
                lhs_qb = qbc[(h % 2) * 64:(h % 2) * 64 + 64, :]\
                    .rearrange("p (c u) -> p c u", u=1)[:, h // 2, :]
                t_wg.inc(te.matmul(bank[0:1, 0:8], lhs_qb,
                                   grepb[(h % 2) * 64:(h % 2) * 64 + 64, :],
                                   start=True, stop=True))
            if SUB <= 2:
                return
            te.wait_ge(v_wg.h, V0_DONE + 144 + 2)
            t_wg.inc(te.matmul(pb[5][:, 0:128], ones1[:, :], bgrow16[:, :],
                               start=True, stop=True))
        blk.tensor(t_wgf)

        def v_wgf(ve):
            for h in range(16):
                for dc in range(NC_D):
                    k = h * NC_D + dc
                    ve.wait_ge(t_wg.h, k + 1)
                    bank = pb[2 + k % 2]
                    ve.tensor_scalar(WGT[:, dc * 128 + h * 8:dc * 128 + (h + 1) * 8],
                                     bank[:, 0:8], SC, None, ALU.mult)
                    v_wg.inc(ve.drain())
            if SUB <= 1:
                return
            for h in range(16):
                ve.wait_ge(t_wg.h, 128 + h + 1)
                bank = pb[2 + (128 + h) % 2]
                v_wg.inc(ve.tensor_copy(bgrow[0:1, h * 8:(h + 1) * 8], bank[0:1, 0:8]))
            if SUB <= 2:
                return
            ve.drain()
            ve.scalar_tensor_tensor(bgrow[0:1, :], bgrow[0:1, :],
                                    SC, grepbt[0:1, :], ALU.mult, ALU.add)
            ve.drain()
            ve.tensor_copy(bgrow16[0:1, :], bgrow[0:1, :])
            v_wg.inc(ve.drain())
            v_wg.inc(ve.drain())
            ve.wait_ge(t_wg.h, 145)
            v_wg.inc(ve.tensor_copy(biasGB[:, :], pb[5][:, 0:128]))
        blk.vector(v_wgf)
        WG_DONE = (v_wg, v_wg.n)
        if TR <= 2:
            def fin(sn):
                sn.wait_ge(v_wg.h, WG_DONE[1])
            blk.sync(fin)
            return nc

        g_w = [sems["g_w0"], sems["g_w1"]]
        t_pj, v_pj = sems["t_pj"], sems["v_pj"]
        t_g, v_g, s_g = sems["t_g"], sems["v_g"], sems["s_g"]
        v_sc, s_sc = sems["v_sc"], sems["s_sc"]
        v_cx = sems["v_cx"]
        t_tp, v_tp = sems["t_tp"], sems["v_tp"]
        t_ou, v_ou = sems["t_ou"], sems["v_ou"]

        def tauglob(ph, i, pc):
            return i * 4 + ph * 2 + pc

        def gw(gp):
            k = 0
            for ph in range(2):
                for wi, wt in enumerate([qwT, kwT, vwT, owT]):
                    p = k % 2
                    if k >= 2:
                        if k % 4 == 3:
                            gp.wait_ge(t_pj.h, (ph * 3 + 2) * 16)
                        elif wi == 0:
                            gp.wait_ge(t_pj.h, 3 * 16)
                        elif wi == 1:
                            gp.wait_ge(t_ou.h, 16)
                        elif wi == 2:
                            gp.wait_ge(t_pj.h, (ph * 3 + 1) * 16)
                    g_w[p].inc(gp.dma_start(wdb[p][:, :].rearrange("p (c f) -> p c f", f=D),
                                            wt[:, :].rearrange("(c p) f -> p c f", p=128)), 16)
                    k += 1
        blk.gpsimd(gw)

        def t_main(te):
            te.wait_ge(XT[0].h, XT[1])
            te.wait_ge(v_wg.h, WG_DONE[1])
            npj = 0
            ng = 0
            ntp = 0
            nou = 0
            for ph in range(2):
                if ph == 1 and TR <= 8:
                    break
                for wi in range(3):
                    kk = ph * 4 + wi
                    te.wait_ge(g_w[kk % 2].h, 16 * (kk // 2 + 1))
                    for tl in range(8):
                        i, pc = tl // 2, tl % 2
                        tg = tauglob(ph, i, pc)
                        for n2 in range(2):
                            bank = pb[0 + npj % 2]
                            if npj >= 2:
                                te.wait_ge(v_pj.h, npj - 1)
                            for c in range(NC_D):
                                inst = te.matmul(
                                    bank[:, :],
                                    xT[:, c * TOK + tg * 128:c * TOK + (tg + 1) * 128],
                                    wdb[kk % 2][:, c * D + n2 * 512:c * D + (n2 + 1) * 512],
                                    start=(c == 0), stop=(c == NC_D - 1))
                            t_pj.inc(inst)
                            npj += 1
                if TR <= 3:
                    continue
                for tl in range(8):
                    i, pc = tl // 2, tl % 2
                    tg = tauglob(ph, i, pc)
                    bank = pb[2 + ng % 2]
                    if ng >= 2:
                        te.wait_ge(v_g.h, 3 * (ng - 1))
                    for c in range(NC_D):
                        inst = te.matmul(bank[:, 0:128],
                                         xT[:, c * TOK + tg * 128:c * TOK + (tg + 1) * 128],
                                         WGT[:, c * 128:(c + 1) * 128],
                                         start=(c == 0), stop=(c == NC_D - 1))
                    t_g.inc(inst)
                    ng += 1
                if TR <= 6:
                    continue
                for tl in range(8):
                    te.wait_ge(v_cx.h, ph * 8 + tl + 1)
                    for dc in range(NC_D):
                        bank = pb[4 + ntp % 2]
                        if ntp >= 2:
                            te.wait_ge(v_tp.h, ntp - 1)
                        dstp = bank[:, :].bitcast(BF16)[:, 0:128]
                        t_tp.inc(te.transpose(dstp, ctxb[:, tl * D + dc * 128:tl * D + (dc + 1) * 128],
                                              idb[:, :]))
                        ntp += 1
                if TR <= 7:
                    continue
                kk_ow = ph * 4 + 3
                te.wait_ge(g_w[kk_ow % 2].h, 16 * (kk_ow // 2 + 1))
                for tl in range(8):
                    i, pc = tl // 2, tl % 2
                    tg = tauglob(ph, i, pc)
                    te.wait_ge(v_tp.h, ph * 64 + (tl + 1) * 8)
                    for n2 in range(2):
                        bank = pb[6 + nou % 2]
                        if nou >= 2:
                            te.wait_ge(v_ou.h, nou - 1)
                        for dc in range(NC_D):
                            inst = te.matmul(
                                bank[:, :],
                                ctxT[:, dc * D + tl * 128:dc * D + (tl + 1) * 128],
                                wdb[kk_ow % 2][:, dc * D + n2 * 512:dc * D + (n2 + 1) * 512],
                                start=(dc == 0), stop=(dc == NC_D - 1))
                        t_ou.inc(inst)
                        nou += 1
        blk.tensor(t_main)

        def v_main(ve):
            npj = 0
            ng = 0
            nsc = 0
            ntp = 0
            nou = 0
            for ph in range(2):
                if ph == 1 and TR <= 8:
                    break
                for wi in range(3):
                    dstb = [qb_, kb_, vb_][wi]
                    bB = [qbB, kbB, vbB][wi]
                    for tl in range(8):
                        for n2 in range(2):
                            ve.wait_ge(t_pj.h, npj + 1)
                            bank = pb[0 + npj % 2]
                            dst = dstb[:, tl * D + n2 * 512:tl * D + (n2 + 1) * 512]
                            if wi == 0:
                                ve.scalar_tensor_tensor(dst, bank[:, :], SC,
                                                        bB[:, n2 * 512:(n2 + 1) * 512],
                                                        ALU.mult, ALU.add)
                            else:
                                ve.tensor_add(dst, bank[:, :], bB[:, n2 * 512:(n2 + 1) * 512])
                            v_pj.inc(ve.drain())
                            npj += 1
                if TR <= 3:
                    continue
                for tl in range(8):
                    ve.wait_ge(t_g.h, ng + 1)
                    bank = pb[2 + ng % 2]
                    ve.tensor_add(tmpv[:, 0:128], bank[:, 0:128], biasGB[:, :])
                    ve.drain()
                    ve.tensor_reduce(g2b[:, :], tmpv[:, 0:128].rearrange("p (s f) -> p s f", f=4),
                                     AX.X, ALU.add)
                    v_g.inc(ve.drain())
                    ve.wait_ge(s_g.h, ng + 1)
                    ga = sgb[:, :].rearrange("p (h a) -> p h a", a=2)[:, :, 0]
                    gb = sgb[:, :].rearrange("p (h a) -> p h a", a=2)[:, :, 1]
                    ve.tensor_mul(tmpv[:, 0:16], gb, grepaB[:, :])
                    ve.drain()
                    ve.tensor_scalar(tmpv[:, 0:16], tmpv[:, 0:16], 1.0, None, ALU.subtract)
                    ve.drain()
                    ve.tensor_mul(tmpv[:, 0:16], ga, tmpv[:, 0:16])
                    v_g.inc(ve.drain())
                    ve.tensor_scalar(gates[:, tl * 16:(tl + 1) * 16], tmpv[:, 0:16],
                                     2.0, None, ALU.add)
                    v_g.inc(ve.drain())
                    ng += 1
                if TR <= 4:
                    continue
                for pc in range(2):
                    sc_ = scb[:, pc * 256:(pc + 1) * 256]
                    pr_ = prb[:, pc * 256:(pc + 1) * 256]
                    for i in range(4):
                        for j in range(4):
                            qt = qb_[:, (i * 2 + pc) * D:(i * 2 + pc + 1) * D]
                            kt = kb_[:, (j * 2 + pc) * D:(j * 2 + pc + 1) * D]
                            ve.tensor_mul(tmpv[:, :], qt, kt)
                            ve.drain()
                            ve.tensor_reduce(
                                sc_.rearrange("p (h x) -> p h x", x=16)[:, :, i * 4 + j],
                                tmpv[:, :].rearrange("p (h d) -> p h d", d=64),
                                AX.X, ALU.add)
                            ve.drain()
                    for i in range(4):
                        gslice = gates[:, (i * 2 + pc) * 16:(i * 2 + pc + 1) * 16]
                        gexp = gslice.rearrange("p (h u) -> p h u", u=1).to_broadcast([128, 16, 16])
                        pslice = pbiasB[:, :].rearrange("p (h x) -> p h x", x=16)
                        scv = sc_.rearrange("p (h x) -> p h x", x=16)
                        ve.tensor_mul(tmpv[:, 0:64].rearrange("p (h x) -> p h x", x=4),
                                      gexp[:, :, 0:4], pslice[:, :, i * 4:(i + 1) * 4])
                            
                        ve.drain()
                        ve.tensor_add(scv[:, :, i * 4:(i + 1) * 4],
                                      scv[:, :, i * 4:(i + 1) * 4],
                                      tmpv[:, 0:64].rearrange("p (h x) -> p h x", x=4))
                        ve.drain()
                    ve.tensor_reduce(nm64[:, :], sc_.rearrange("p (s j) -> p s j", j=4),
                                     AX.X, ALU.max, negate=True)
                    ve.drain()
                    ve.tensor_add(sc_.rearrange("p (s j) -> p s j", j=4),
                                  sc_.rearrange("p (s j) -> p s j", j=4),
                                  nm64[:, :].rearrange("p (s u) -> p s u", u=1)
                                  .to_broadcast([128, 64, 4]))
                    v_sc.inc(ve.drain())
                    ve.wait_ge(s_sc.h, ph * 2 + pc + 1)
                    ve.tensor_reduce(s64[:, :], pr_.rearrange("p (s j) -> p s j", j=4),
                                     AX.X, ALU.add)
                    ve.drain()
                    ve.reciprocal(s64[:, :], s64[:, :])
                    ve.drain()
                    ve.tensor_mul(pr_.rearrange("p (s j) -> p s j", j=4),
                                  pr_.rearrange("p (s j) -> p s j", j=4),
                                  s64[:, :].rearrange("p (s u) -> p s u", u=1)
                                  .to_broadcast([128, 64, 4]))
                    v_sc.inc(ve.drain())
                    nsc += 2
                if TR <= 5:
                    continue
                for tl in range(8):
                    i, pc = tl // 2, tl % 2
                    pr_ = prb[:, pc * 256:(pc + 1) * 256]
                    ctile = ctxb[:, tl * D:(tl + 1) * D]
                    for j in range(4):
                        pj = pr_.rearrange("p (h x) -> p h x", x=16)[:, :, i * 4 + j]
                        pjb = pj.rearrange("p (h u) -> p h u", u=1)\
                            .to_broadcast([128, 16, 64])
                        vt = vb_[:, (j * 2 + pc) * D:(j * 2 + pc + 1) * D]\
                            .rearrange("p (h d) -> p h d", d=64)
                        if j == 0:
                            ve.tensor_mul(ctile.rearrange("p (h d) -> p h d", d=64), vt, pjb)
                            ve.drain()
                        else:
                            ve.tensor_mul(tmpv[:, :].rearrange("p (h d) -> p h d", d=64), vt, pjb)
                            ve.drain()
                            if j < 3:
                                ve.tensor_add(ctile, ctile, tmpv[:, :])
                                ve.drain()
                            else:
                                v_cx.inc(ve.tensor_add(ctile, ctile, tmpv[:, :]))
                if TR <= 6:
                    continue
                for tl in range(8):
                    for dc in range(NC_D):
                        ve.wait_ge(t_tp.h, ntp + 1)
                        bank = pb[4 + ntp % 2]
                        srcp = bank[:, :].bitcast(BF16)[:, 0:128]
                        v_tp.inc(ve.tensor_copy(ctxT[:, dc * D + tl * 128:dc * D + (tl + 1) * 128],
                                                srcp))
                        ntp += 1
                if TR <= 7:
                    continue
                for tl in range(8):
                    i, pc = tl // 2, tl % 2
                    tg = tauglob(ph, i, pc)
                    for n2 in range(2):
                        ve.wait_ge(t_ou.h, nou + 1)
                        bank = pb[6 + nou % 2]
                        v_ou.inc(ve.tensor_add(
                            acc[:, tg * D + n2 * 512:tg * D + (n2 + 1) * 512],
                            bank[:, :], obB[:, n2 * 512:(n2 + 1) * 512]))
                        nou += 1
        blk.vector(v_main)

        def s_main(sc):
            ng = 0
            nsc = 0
            for ph in range(2):
                if ph == 1 and TR <= 8:
                    break
                if TR <= 3:
                    continue
                for tl in range(8):
                    sc.wait_ge(v_g.h, 3 * ng + 1)
                    s_g.inc(sc.activation(sgb[:, :], g2b[:, :], AF.Sigmoid))
                    ng += 1
                for pc in range(2):
                    if TR <= 4:
                        continue
                    sc.wait_ge(v_sc.h, nsc + 1)
                    s_sc.inc(sc.activation(prb[:, pc * 256:(pc + 1) * 256],
                                           scb[:, pc * 256:(pc + 1) * 256], AF.Exp))
                    nsc += 2
        blk.scalar(s_main)
        OU_TOTAL = v_ou.n
        if TR <= 8:
            def fin(sn):
                if v_ou.n:
                    sn.wait_ge(v_ou.h, v_ou.n)
                elif sems["v_cx"].n:
                    sn.wait_ge(sems["v_cx"].h, sems["v_cx"].n)
                elif sems["v_sc"].n:
                    sn.wait_ge(sems["v_sc"].h, sems["v_sc"].n)
                elif sems["v_g"].n:
                    sn.wait_ge(sems["v_g"].h, sems["v_g"].n)
                else:
                    sn.wait_ge(sems["v_pj"].h, sems["v_pj"].n)
            blk.sync(fin)
            return nc

        emit_ln_epilogue(nc, blk,
                         {"g_ep0": sems["g_ep0"], "g_ep1": sems["g_ep1"],
                          "v_ep": sems["v_ep"], "s_ep": sems["s_ep"],
                          "x_ep0": sems["x_ep0"], "x_ep1": sems["x_ep1"]},
                         lambda i: acc[:, i * D:(i + 1) * D],
                         x2, x3, stage, rs2, x2t, sqd, stats, marks,
                         [(v_ou, OU_TOTAL)], "ep", eps_col=epsc[:, :])
    ctx2.close()
    return [(sems["x_ep0"], sems["x_ep0"].n), (sems["x_ep1"], sems["x_ep1"].n)]


def inmap_l1attn(h, c, x2_chunks):
    return dict(x2=x2_chunks[c], qwT=h["qwT"], kwT=h["kwT"], vwT=h["vwT"],
                owT=h["owT"], qw_nat=h["qw_nat"], qb1_pc=h["qb1_pc"],
                qbBt=h["qbBt"], kbBt=h["kbBt"], vbBt=h["vbBt"], obBt=h["obBt"],
                grepT=h["grepT"], grepb_t=h["grepb_t"],
                gaBt=h["gaBt"], pbBt=h["pbBt"], ident=h["ident"])



def emit_l0attn(nc, blk, ctx, dram, entry, shared=None):
    NJ = 32
    (xTr, x_own, WaT, baQ, baK, baVt, snbT, WoTh, boT, relXs, relXA, relXB,
     chiT, augS, ident, x1, KTh, Vh, Eh, EhA, EhB) = (
        dram["xTr"], dram["x_own"], dram["WaT"], dram["baQ"], dram["baK"],
        dram["baVt"], dram["snbT"], dram["WoTh"], dram["boT"], dram["relXs"],
        dram["relXA"], dram["relXB"], dram["chiT"], dram["augS"], dram["ident"],
        dram["x1"], dram["KTh"], dram["Vh"], dram["Eh"], dram["EhA"], dram["EhB"])

    ctx2 = ExitStack()
    a = lambda nm, sh, dt: ctx2.enter_context(nc.sbuf_tensor("A_" + nm, sh, dt))
    wab = a("wab", [128, NC_D * 3 * D], BF16)
    xst = a("xst", [128, 2 * NC_D * 512], BF16)
    ktile = a("ktile", [128, 2 * NC_D * 512], BF16)
    vtile = a("vtile", [128, 2 * 4 * D], BF16)
    QTb = a("QTb", [128, NC_D * TOK], BF16)
    acc = a("acc", [128, NT * D], BF16)
    woH = a("woH", [128, 16 * D], BF16)
    KTb = [wab[:, 0:SEQ], wab[:, SEQ:2 * SEQ]]
    Vb = [wab[:, 8192:8192 + NJ * 65 + 63], wab[:, 10272:10272 + NJ * 65 + 63]]
    rhsT = [wab[:, 12352:12864], wab[:, 12864:13376]]
    tbuf = wab[:, 13376:15424]
    t2buf = wab[:, 15424:17472]
    augsb = wab[:, 17472:19520].bitcast(F32)
    etile = wab[:, 19520:20542]
    stage = wab[:, 21568:23616]
    wtile = [xst[:, 0:1536], xst[:, 1536:3072]]
    sqd = xst[:, 6144:8192].bitcast(F32)
    ctxP = ktile[0:64, 0:8192]
    rssc = vtile[:, 0:1024].bitcast(F32)
    rsb16 = vtile[:, 1024:1536]
    rs2 = vtile[:, 1600:5696].bitcast(F32)
    augb = a("augb", [128, 4 * 8], BF16)
    t2x = a("t2x", [128, 2048], BF16)
    tbx = a("tbx", [128, 2048], BF16)
    relXb = a("relXb", [128, 511 + 2 * 255], BF16)
    onesP = a("onesP", [128, 64], BF16)
    idb = a("idb", [128, 128], BF16)
    idf = a("idf", [128, 128], F32)
    baQs = a("baQs", [128, NC_D], F32)
    baKs = a("baKs", [128, NC_D], F32)
    baVb = a("baVb", [128, D], BF16)
    snb8 = a("snb8", [128, 1], F32)
    eps8 = a("eps8", [128, 1], F32)
    boTb = a("boTb", [128, D], BF16)
    x2t = QTb[:, 0:4096].bitcast(F32)
    stats = a("stats", [128, 8], F32)
    epsc = a("epsc", [128, 1], F32)

    scps = [ctx2.enter_context(nc.psum_tensor(f"A_scps{i}", [128, 1024], F32)) for i in range(2)]
    ctxps = ctx2.enter_context(nc.psum_tensor("A_ctxps", [128, 512], F32))
    miscps = [ctx2.enter_context(nc.psum_tensor(f"A_miscps{i}", [128, 512], F32)) for i in range(2)]
    outps = ctx2.enter_context(nc.psum_tensor("A_outps", [128, 512], F32))

    BH = B * 16

    if True:
        sems = mk_block_sems(ctx, nc, "A_", 
                ["g_m", "m_sy", "g_xs0", "g_xs1", "t_kv", "s_kt", "s_qt", "v_kv",
                 "x_kt0", "x_kt1", "x_v0", "x_v1",
                 "s_kl0", "s_kl1", "s_vl0", "s_vl1", "g_qr0", "g_qr1",
                 "g_ew0", "g_ew1", "g_er0", "g_er1",
                 "t_au", "v_au", "s_au", "t_qe", "v_qe",
                 "t_sc", "s_rl", "v_t2", "t_av", "v_cx", "s_dv",
                 "t_op", "v_op", "g_chi", "v_ms",
                 "g_ep0", "g_ep1", "v_ep", "s_ep", "x_ep0", "x_ep1"], shared)
        marks = {}
        emit_entry_barrier(blk, entry)
        g_m = sems["g_m"]

        def g0(gp):
            g_m.inc(gp.dma_start(wab[:, :].rearrange("p (c f) -> p c f", f=3 * D),
                                 WaT[:, :].rearrange("(c p) f -> p c f", p=128)), 16)
            g_m.inc(gp.dma_start(idb[:, :], ident[:, :]), 16)
            g_m.inc(gp.dma_start(augb[0:64, :].rearrange("p (s e) -> p s e", e=8),
                                 augS[:, :].rearrange("(s p) e -> p s e", p=64)), 16)
            g_m.inc(gp.dma_start(augb[64:128, :].rearrange("p (s e) -> p s e", e=8),
                                 augS[:, :].rearrange("(s p) e -> p s e", p=64)), 16)
            g_m.inc(gp.dma_start(relXb[0:64, 0:511], relXs[:, :]), 16)
            g_m.inc(gp.dma_start(relXb[64:128, 0:511], relXs[:, :]), 16)
            g_m.inc(gp.dma_start(relXb[0:64, 511:766], relXA[:, :]), 16)
            g_m.inc(gp.dma_start(relXb[64:128, 511:766], relXA[:, :]), 16)
            g_m.inc(gp.dma_start(relXb[0:64, 766:1021], relXB[:, :]), 16)
            g_m.inc(gp.dma_start(relXb[64:128, 766:1021], relXB[:, :]), 16)
            g_m.inc(gp.dma_start(woH[:, :], WoTh[:, :]), 16)
            g_m.inc(gp.dma_start(baVb[:, :], baVt[:, :]), 16)
            g_m.inc(gp.dma_start(boTb[:, :], boT[:, :]), 16)
        blk.gpsimd(g0)

        m_sy = sems["m_sy"]

        def sy0(sn):
            m_sy.inc(sn.dma_start(idf[:, :], ident[:, :]), 16)
            m_sy.inc(sn.dma_start(baQs[:, :], baQ[:, :]), 16)
            m_sy.inc(sn.dma_start(baKs[:, :], baK[:, :]), 16)
            m_sy.inc(sn.dma_start(snb8[:, :], snbT[:, :]), 16)
        blk.sync(sy0)
        GM_ALL = g_m.n
        MSY_ALL = m_sy.n
        v_kv = sems["v_kv"]

        def v0(ve):
            ve.wait_ge(g_m.h, GM_ALL)
            ve.wait_ge(m_sy.h, MSY_ALL)
            ve.memset(onesP[:, :], 1.0)
            ve.memset(eps8[:, :], 1e-8)
            ve.tensor_scalar(snb8[:, :], snb8[:, :], 8.0, None, ALU.mult)
            v_kv.inc(ve.drain())
        blk.vector(v0)
        V0_OK = v_kv.n

        g_xs = [sems["g_xs0"], sems["g_xs1"]]
        t_kv, s_kt, s_qt = sems["t_kv"], sems["s_kt"], sems["s_qt"]
        x_kt = [sems["x_kt0"], sems["x_kt1"]]
        x_v = [sems["x_v0"], sems["x_v1"]]
        OWN = (0, 8, 16, 24)

        def cum_t(i):
            return sum(16 + (8 if k in OWN else 0) for k in range(i + 1))

        def g1(gp):
            for i in range(32):
                p = i % 2
                if i >= 2:
                    gp.wait_ge(t_kv.h, cum_t(i - 1))
                g_xs[p].inc(gp.dma_start(
                    xst[:, p * NC_D * 512:(p + 1) * NC_D * 512]
                    .rearrange("p (c t) -> p c t", t=512),
                    xTr[:, i * 512:(i + 1) * 512].rearrange("(c p) t -> p c t", p=128)), 16)
        blk.gpsimd(g1)

        def t1(te):
            te.wait_ge(g_m.h, GM_ALL)
            te.wait_ge(m_sy.h, MSY_ALL)
            te.wait_ge(v_kv.h, V0_OK)
            nkt = 0
            nv = 0
            nqt = 0
            for i in range(32):
                p = i % 2
                te.wait_ge(g_xs[p].h, 16 * (i // 2 + 1))
                xb3 = xst[:, p * NC_D * 512:(p + 1) * NC_D * 512]\
                    .rearrange("p (c t) -> p c t", t=512)
                for fc in range(NC_D):
                    bank = scps[0][:, (nkt % 2) * 512:(nkt % 2) * 512 + 512]
                    if nkt >= 2:
                        te.wait_ge(s_kt.h, nkt - 1)
                    for c in range(NC_D):
                        inst = te.matmul(
                            bank,
                            wab[:, c * 3 * D + D + fc * 128:c * 3 * D + D + (fc + 1) * 128],
                            xb3[:, c, :],
                            start=(c == 0), stop=(c == NC_D - 1))
                    t_kv.inc(inst)
                    nkt += 1
                for tc in range(4):
                    for n2 in range(2):
                        bank = scps[1][:, (nv % 2) * 512:(nv % 2) * 512 + 512]
                        if nv >= 2:
                            te.wait_ge(v_kv.h, V0_OK + nv - 1)
                        for c in range(NC_D):
                            inst = te.matmul(
                                bank,
                                xb3[:, c, tc * 128:(tc + 1) * 128],
                                wab[:, c * 3 * D + 2 * D + n2 * 512:
                                    c * 3 * D + 2 * D + (n2 + 1) * 512],
                                start=(c == 0), stop=(c == NC_D - 1))
                        t_kv.inc(inst)
                        nv += 1
                if i in OWN:
                    bq = i // 8
                    for fc in range(NC_D):
                        bank = miscps[nqt % 2][:, :]
                        if nqt >= 2:
                            te.wait_ge(s_qt.h, nqt - 1)
                        for c in range(NC_D):
                            inst = te.matmul(
                                bank,
                                wab[:, c * 3 * D + fc * 128:c * 3 * D + (fc + 1) * 128],
                                xb3[:, c, :],
                                start=(c == 0), stop=(c == NC_D - 1))
                        t_kv.inc(inst)
                        nqt += 1
        blk.tensor(t1)

        def s1(sc):
            sc.wait_ge(m_sy.h, MSY_ALL)
            nkt = 0
            nqt = 0
            for i in range(32):
                p = i % 2
                base_tkv = cum_t(i - 1) if i > 0 else 0
                for fc in range(NC_D):
                    sc.wait_ge(t_kv.h, base_tkv + fc + 1)
                    if i >= 2 and fc == 0:
                        sc.wait_ge(x_kt[p].h, 16 * (i // 2))
                    bank = scps[0][:, (nkt % 2) * 512:(nkt % 2) * 512 + 512]
                    s_kt.inc(sc.activation(
                        ktile[:, p * NC_D * 512 + fc * 512:p * NC_D * 512 + (fc + 1) * 512],
                        bank, AF.Identity, bias=baKs[:, fc:fc + 1]))
                    nkt += 1
                if i in OWN:
                    bq = i // 8
                    for fc in range(NC_D):
                        sc.wait_ge(t_kv.h, base_tkv + 16 + fc + 1)
                        bank = miscps[nqt % 2][:, :]
                        s_qt.inc(sc.activation(
                            QTb[:, fc * TOK + bq * 512:fc * TOK + (bq + 1) * 512],
                            bank, AF.Identity, bias=baQs[:, fc:fc + 1]))
                        nqt += 1
        blk.scalar(s1)

        def v1f(ve):
            nv = 0
            for i in range(32):
                p = i % 2
                base_tkv = (cum_t(i - 1) if i > 0 else 0) + 8
                for g in range(8):
                    tc, n2 = g // 2, g % 2
                    ve.wait_ge(t_kv.h, base_tkv + g + 1)
                    if i >= 2 and g == 0:
                        ve.wait_ge(x_v[p].h, 16 * (i // 2))
                    bank = scps[1][:, (nv % 2) * 512:(nv % 2) * 512 + 512]
                    v_kv.inc(ve.tensor_add(
                        vtile[:, p * 4 * D + tc * D + n2 * 512:
                              p * 4 * D + tc * D + (n2 + 1) * 512],
                        bank, baVb[:, n2 * 512:(n2 + 1) * 512]))
                    nv += 1
        blk.vector(v1f)

        def sy1(sn):
            for i in range(32):
                p = i % 2
                sn.wait_ge(s_kt.h, 8 * (i + 1))
                x_kt[p].inc(sn.dma_start(
                    KTh[:, i * 512:(i + 1) * 512].rearrange("(c p) t -> p c t", p=128),
                    ktile[:, p * NC_D * 512:(p + 1) * NC_D * 512]
                    .rearrange("p (c t) -> p c t", t=512)), 16)
                sn.wait_ge(v_kv.h, V0_OK + 8 * (i + 1))
                x_v[p].inc(sn.dma_start(
                    Vh[i * 512:(i + 1) * 512, :].rearrange("(tc p) f -> p tc f", p=128),
                    vtile[:, p * 4 * D:(p + 1) * 4 * D]
                    .rearrange("p (tc f) -> p tc f", f=D)), 16)
        blk.sync(sy1)
        P1_KT = (x_kt[0].n, x_kt[1].n)
        P1_V = (x_v[0].n, x_v[1].n)
        P1_SQT = s_qt.n
        v_ms = sems["v_ms"]

        def vms(ve):
            ve.wait_ge(t_kv.h, cum_t(31))
            ve.wait_ge(s_qt.h, P1_SQT)
            v_ms.inc(ve.memset(wab[64:128, 0:2 * SEQ], 0.0))
            v_ms.inc(ve.memset(wab[64:128, 12352:13376], 0.0))
        blk.vector(vms)

        s_kl = [sems["s_kl0"], sems["s_kl1"]]
        s_vl = [sems["s_vl0"], sems["s_vl1"]]
        g_qr = [sems["g_qr0"], sems["g_qr1"]]
        g_ew = [sems["g_ew0"], sems["g_ew1"]]
        g_er = [sems["g_er0"], sems["g_er1"]]
        t_au, v_au, s_au = sems["t_au"], sems["v_au"], sems["s_au"]
        t_qe, v_qe = sems["t_qe"], sems["v_qe"]
        t_sc, s_rl, v_t2 = sems["t_sc"], sems["s_rl"], sems["v_t2"]
        t_av, v_cx, s_dv = sems["t_av"], sems["v_cx"], sems["s_dv"]
        t_op, v_op = sems["t_op"], sems["v_op"]

        band = {}
        for t_ in range(4):
            for dl in (-1, 0, 1):
                band.setdefault((t_ + dl) % NJ, []).append((t_, dl))
        BLKS = [(t_, dl) for t_ in range(4) for dl in (-1, 0, 1)]
        QSPECS = [(0, 0, 511), (1, 0, 511), (2, 0, 511), (3, 0, 511),
                  (0, 511, 255), (3, 766, 255)]

        def sy2(sn):
            sn.wait_ge(x_kt[0].h, P1_KT[0])
            sn.wait_ge(x_kt[1].h, P1_KT[1])
            sn.wait_ge(x_v[0].h, P1_V[0])
            sn.wait_ge(x_v[1].h, P1_V[1])
            for bh in range(BH):
                b, h = bh // 16, bh % 16
                pe = bh % 2
                if bh >= 2:
                    sn.wait_ge(t_sc.h, 32 * (bh - 1))
                s_kl[pe].inc(sn.dma_start(
                    KTb[pe][0:64, :], KTh[h * 64:(h + 1) * 64, b * SEQ:(b + 1) * SEQ]), 16)
                if bh >= 2:
                    sn.wait_ge(t_av.h, 32 * (bh - 1))
                s_vl[pe].inc(sn.dma_start(
                    Vb[pe][:, 0:NJ * 65].rearrange("p (j d) -> p j d", d=65)[:, :, 0:64],
                    Vh[b * SEQ:(b + 1) * SEQ, h * 64:(h + 1) * 64]
                    .rearrange("(j p) d -> p j d", p=128)), 16)
        blk.sync(sy2)

        def g2(gp):
            gp.wait_ge(s_qt.h, P1_SQT)
            gp.wait_ge(t_kv.h, cum_t(31))
            gp.wait_ge(sems["v_ms"].h, 2)
            g_chi = sems["g_chi"]
            g_chi.inc(gp.dma_start(KTb[0][64:72, :], chiT[:, :]), 16)
            g_chi.inc(gp.dma_start(KTb[1][64:72, :], chiT[:, :]), 16)
            for bh in range(BH):
                b, h = bh // 16, bh % 16
                pe = bh % 2
                for k in range(6):
                    gp.wait_ge(v_au.h, 10 * bh + 5 + k)
                    if bh >= 2 and k == 0:
                        gp.wait_ge(g_er[pe].h, 16 * 12 * (bh // 2))
                    if k < 4:
                        dst = bass.AP(Eh, pe * 512 * 511 + k * 128 * 511,
                                      [[511, 128], [1, 511]])
                        src = etile[:, (k % 2) * 511:(k % 2) * 511 + 511]
                    elif k == 4:
                        dst = bass.AP(EhA, pe * 128 * 255, [[255, 128], [1, 255]])
                        src = etile[:, 0:255]
                    else:
                        dst = bass.AP(EhB, pe * 128 * 255, [[255, 128], [1, 255]])
                        src = etile[:, 511:766]
                    g_ew[pe].inc(gp.dma_start(dst, src), 16)
                for bi, (t_, dl) in enumerate(BLKS):
                    if bi == 0:
                        gp.wait_ge(g_ew[pe].h, 16 * 6 * (bh // 2 + 1))
                        if bh >= 2:
                            gp.wait_ge(t_sc.h, 32 * (bh - 1))
                    if (t_, dl) == (0, -1):
                        srcap = bass.AP(EhA, pe * 128 * 255 + 127, [[254, 128], [1, 128]])
                    elif (t_, dl) == (3, 1):
                        srcap = bass.AP(EhB, pe * 128 * 255 + 127, [[254, 128], [1, 128]])
                    else:
                        srcap = bass.AP(Eh, pe * 512 * 511 + 128 * 511 * t_ + 128 * dl + 255,
                                        [[510, 128], [1, 128]])
                    g_er[pe].inc(gp.dma_start(
                        wtile[pe][:, bi * 128:(bi + 1) * 128], srcap), 16)
                if bh >= 2:
                    gp.wait_ge(t_sc.h, 32 * (bh - 1))
                g_qr[pe].inc(gp.dma_start(
                    rhsT[pe][0:64, :],
                    QTb[(h % 2) * 64:(h % 2) * 64 + 64,
                        (h // 2) * TOK + b * 512:(h // 2) * TOK + (b + 1) * 512]), 16)
        blk.gpsimd(g2)

        nmx_t = [0]

        def emit_augqr_step(te, bh, k):
            b, h = bh // 16, bh % 16
            hb = (h % 2) * 64

            def qsl(s):
                return QTb[hb:hb + 64, (h // 2) * TOK + b * 512 + s * 128:
                           (h // 2) * TOK + b * 512 + (s + 1) * 128]
            if k == 0 and bh >= 2:
                te.wait_ge(s_au.h, bh - 1)
            bank = miscps[nmx_t[0] % 2]
            if nmx_t[0] >= 2:
                te.wait_ge(v_au.h, nmx_t[0] - 1)
            if k < 4:
                t_au.inc(te.matmul(bank[64:72, 0:128],
                                   augb[hb:hb + 64, k * 8:(k + 1) * 8], qsl(k),
                                   start=True, stop=True))
            else:
                s, off, w = QSPECS[k - 4]
                t_qe.inc(te.matmul(bank[:, 0:w], qsl(s), relXb[hb:hb + 64, off:off + w],
                                   start=True, stop=True))
            nmx_t[0] += 1

        def emit_augqr(te, bh):
            for k in range(10):
                emit_augqr_step(te, bh, k)

        def tail_bh(te, pbh):
            pb_, ph = pbh // 16, pbh % 16
            te.wait_ge(s_dv.h, pbh + 1)
            te.wait_ge(v_op.h, 8 * (pbh // 16))
            t_op.inc(te.matmul(outps[0:64, :], onesP[64:65, 0:64],
                               rsb16[64:65, :], start=True, stop=True))
            if ph == 15:
                te.wait_ge(v_cx.h, 2 * (pbh + 1))
                for nop in range(8):
                    qs, n2 = nop // 2, nop % 2
                    te.wait_ge(v_op.h, 8 * pb_ + nop)
                    for hh in range(16):
                        inst = te.matmul(
                            outps[:, :],
                            ktile[:, hh * 512 + qs * 128:hh * 512 + (qs + 1) * 128],
                            woH[:, hh * D + n2 * 512:hh * D + (n2 + 1) * 512],
                            start=(hh == 0), stop=(hh == 15))
                    t_op.inc(inst)

        def t2f(te):
            te.wait_ge(s_qt.h, P1_SQT)
            te.wait_ge(sems["v_ms"].h, 2)
            te.wait_ge(s_kt.h, 8 * 32)
            te.wait_ge(v_kv.h, V0_OK + 8 * 32)
            te.wait_ge(sems["g_chi"].h, 32)
            emit_augqr(te, 0)
            emit_augqr(te, 1)
            for bh in range(BH):
                b, h = bh // 16, bh % 16
                pe = bh % 2
                te.wait_ge(s_kl[pe].h, 16 * (bh // 2 + 1))
                te.wait_ge(g_qr[pe].h, 16 * (bh // 2 + 1))
                te.wait_ge(s_au.h, bh + 1)
                te.wait_ge(s_vl[pe].h, 16 * (bh // 2 + 1))
                need_w = True
                for g in range(16):
                    for j in (2 * g, 2 * g + 1):
                        grp = scps[(j // 2) % 2][:, (j % 2) * 512:(j % 2) * 512 + 512]
                        if j % 2 == 0 and (16 * bh + g) >= 2:
                            te.wait_ge(s_rl.h, 16 * bh + g - 1)
                        isband = j in band
                        if isband:
                            if need_w:
                                te.wait_ge(g_er[pe].h, 16 * 12 * (bh // 2 + 1))
                                need_w = False
                            covered = set()
                            for (t_, dl) in band[j]:
                                bi = BLKS.index((t_, dl))
                                covered.add(t_)
                                te.matmul(grp[:, t_ * 128:(t_ + 1) * 128],
                                          wtile[pe][:, bi * 128:(bi + 1) * 128],
                                          idb[:, :], start=True, stop=True,
                                          skip_group_check=True)
                            for t_ in range(4):
                                inst = te.matmul(
                                    grp[:, t_ * 128:(t_ + 1) * 128],
                                    KTb[pe][:, j * 128:(j + 1) * 128],
                                    rhsT[pe][:, t_ * 128:(t_ + 1) * 128],
                                    start=(t_ not in covered), stop=True,
                                    skip_group_check=True)
                        else:
                            inst = te.matmul(grp, KTb[pe][:, j * 128:(j + 1) * 128],
                                             rhsT[pe][:, :],
                                             start=True, stop=True,
                                             skip_group_check=True)
                        t_sc.inc(inst)
                    if g >= 2:
                        for j in (2 * (g - 2), 2 * (g - 2) + 1):
                            te.wait_ge(v_t2.h, 16 * bh + g - 1)
                            if j == 0 and bh >= 1:
                                te.wait_ge(v_cx.h, 2 * bh - 1)
                            k2 = j // 2
                            t2s = t2buf if k2 % 4 < 2 else t2x
                            inst = te.matmul(
                                ctxps[:, :], Vb[pe][:, j * 65:j * 65 + 128],
                                t2s[:, (k2 % 2) * 1024 + (j % 2) * 512:
                                    (k2 % 2) * 1024 + (j % 2) * 512 + 512],
                                start=(j == 0), stop=False, skip_group_check=True)
                            t_av.inc(inst)
                    if g < 10 and bh + 2 < BH:
                        emit_augqr_step(te, bh + 2, g)
                    if g == 1 and bh >= 1:
                        tail_bh(te, bh - 1)
                for k2 in (14, 15):
                    for j in (2 * k2, 2 * k2 + 1):
                        te.wait_ge(v_t2.h, 16 * bh + k2 + 1)
                        t2s = t2buf if k2 % 4 < 2 else t2x
                        inst = te.matmul(
                            ctxps[:, :], Vb[pe][:, j * 65:j * 65 + 128],
                            t2s[:, (k2 % 2) * 1024 + (j % 2) * 512:
                                (k2 % 2) * 1024 + (j % 2) * 512 + 512],
                            start=False, stop=(j == 31), skip_group_check=True)
                        t_av.inc(inst)
            tail_bh(te, BH - 1)
        blk.tensor(t2f)

        def s2f(sc):
            for bh in range(BH):
                b, h = bh // 16, bh % 16
                pe = bh % 2
                sc.wait_ge(v_au.h, 10 * bh + 4)
                if bh >= 2:
                    sc.wait_ge(t_sc.h, 32 * (bh - 1))
                s_au.inc(sc.activation(rhsT[pe][64:72, :],
                                       augsb[64:72, (bh % 2) * 512:(bh % 2) * 512 + 512],
                                       AF.Identity, bias=snb8[64:72, :]))
                for g in range(16):
                    sc.wait_ge(t_sc.h, 32 * bh + 2 * (g + 1))
                    gg = 16 * bh + g
                    if gg >= 4:
                        sc.wait_ge(v_t2.h, gg - 3)
                    tbs = tbuf if g % 4 < 2 else tbx
                    s_rl.inc(sc.activation(tbs[:, (g % 2) * 1024:(g % 2) * 1024 + 1024],
                                           scps[g % 2][:, :], AF.Relu, scale=0.125))
                sc.wait_ge(t_av.h, 32 * (bh + 1))
                sc.activation(rssc[64:65, :], ctxps[64:65, :], AF.Ln, bias=eps8[64:65, :])
                sc.drain()
                sc.activation(rssc[64:65, :], rssc[64:65, :], AF.Exp, scale=-1.0)
                sc.drain()
                s_dv.inc(sc.activation(rsb16[64:65, :], rssc[64:65, :], AF.Identity))
        blk.scalar(s2f)

        nmx_v = [0]
        nau_c = [0]
        nqe_c = [0]

        def v2f(ve):
            def ev_augqr_step(bh, k):
                pe = bh % 2
                bank = miscps[nmx_v[0] % 2]
                if k < 4:
                    ve.wait_ge(t_au.h, nau_c[0] + 1)
                    v_au.inc(ve.tensor_copy(
                        augsb[64:72, (bh % 2) * 512 + k * 128:(bh % 2) * 512 + (k + 1) * 128],
                        bank[64:72, 0:128]))
                    nau_c[0] += 1
                else:
                    kk = k - 4
                    s, off, w = QSPECS[kk]
                    ve.wait_ge(t_qe.h, nqe_c[0] + 1)
                    ew_total = 6 * (bh // 2) + kk
                    if ew_total >= 1:
                        ve.wait_ge(g_ew[pe].h, 16 * ew_total)
                    dsto = (kk % 2) * 511 if kk < 4 else (0 if kk == 4 else 511)
                    v_au.inc(ve.tensor_copy(etile[:, dsto:dsto + w], bank[:, 0:w]))
                    nqe_c[0] += 1
                nmx_v[0] += 1

            def ev_augqr(bh):
                for k in range(10):
                    ev_augqr_step(bh, k)

            ve.wait_ge(t_kv.h, cum_t(31))
            ve.memset(Vb[0][:, 0:NJ * 65].rearrange("p (j d) -> p j d", d=65)[:, :, 64:65], 1.0)
            ve.memset(Vb[1][:, 0:NJ * 65].rearrange("p (j d) -> p j d", d=65)[:, :, 64:65], 1.0)
            ve.drain()
            ev_augqr(0)
            ev_augqr(1)
            for bh in range(BH):
                b, h = bh // 16, bh % 16
                pe = bh % 2
                for g in range(16):
                    gg = 16 * bh + g
                    ve.wait_ge(s_rl.h, gg + 1)
                    war = 32 * bh + 2 * g - 6
                    if war > 0:
                        ve.wait_ge(t_av.h, war)
                    t2s = t2buf if g % 4 < 2 else t2x
                    tbs = tbuf if g % 4 < 2 else tbx
                    v_t2.inc(ve.tensor_mul(
                        t2s[:, (g % 2) * 1024:(g % 2) * 1024 + 1024],
                        tbs[:, (g % 2) * 1024:(g % 2) * 1024 + 1024],
                        tbs[:, (g % 2) * 1024:(g % 2) * 1024 + 1024]))
                    if g < 10 and bh + 2 < BH:
                        ev_augqr_step(bh + 2, g)
                ve.wait_ge(t_av.h, 32 * (bh + 1))
                v_cx.inc(ve.tensor_copy(ctxP[:, h * 512:(h + 1) * 512], ctxps[0:64, :]))
                ve.wait_ge(t_op.h, bh + 1 + 8 * b)
                v_cx.inc(ve.tensor_mul(ctxP[:, h * 512:(h + 1) * 512],
                                       ctxP[:, h * 512:(h + 1) * 512], outps[0:64, :]))
                if h == 15:
                    for nop in range(8):
                        qs, n2 = nop // 2, nop % 2
                        ve.wait_ge(t_op.h, bh + 2 + 8 * b + nop)
                        tau = b * 4 + qs
                        v_op.inc(ve.tensor_add(
                            acc[:, tau * D + n2 * 512:tau * D + (n2 + 1) * 512],
                            outps[:, :], boTb[:, n2 * 512:(n2 + 1) * 512]))
        blk.vector(v2f)
        OP_TOTAL = v_op.n

        emit_ln_epilogue(nc, blk,
                         {"g_ep0": sems["g_ep0"], "g_ep1": sems["g_ep1"],
                          "v_ep": sems["v_ep"], "s_ep": sems["s_ep"],
                          "x_ep0": sems["x_ep0"], "x_ep1": sems["x_ep1"]},
                         lambda i: acc[:, i * D:(i + 1) * D],
                         x_own, x1, stage, rs2, x2t, sqd, stats, marks,
                         [(v_op, OP_TOTAL)], "ep", eps_col=epsc[:, :])
    ctx2.close()
    return [(sems["x_ep0"], sems["x_ep0"].n), (sems["x_ep1"], sems["x_ep1"].n)]


def inmap_l0attn(h, c):
    return dict(xTr=h["xT_rot"][c], x_own=h["x_own"][c], WaT=h["WaT"],
                baQ=h["baQ"], baK=h["baK"], baVt=h["baVt"], snbT=h["snbT"],
                WoTh=h["WoTh"], boT=h["boT"], relXs=h["relXT_std"],
                relXA=h["relXT_wA"][c], relXB=h["relXT_wB"][c],
                chiT=h["chi"][c], augS=h["augsel"][c], ident=h["ident"])


def mk_block_sems(ctx, nc, px, names, shared):
    sems = {}
    for n in names:
        if shared and n in shared:
            sems[n] = shared[n]
        else:
            sems[n] = Sem(ctx.enter_context(nc.semaphore(px + n)))
    return sems


def emit_entry_barrier(blk, marks):
    if not marks:
        return

    def f(e):
        for s, n in marks:
            e.wait_ge(s.h, n)
    for reg in (blk.gpsimd, blk.sync, blk.tensor, blk.vector, blk.scalar):
        reg(f)


def build_fused():
    nc = new_launch()
    d = {}
    DT = nc.dram_tensor
    d["xTr"] = DT("xTr", [D, B * SEQ], F32, kind="ExternalInput")
    d["x_own"] = DT("x_own", [TOK, D], F32, kind="ExternalInput")
    d["WaT"] = DT("WaT", [D, 3 * D], F32, kind="ExternalInput")
    d["baQ"] = DT("baQ", [128, NC_D], F32, kind="ExternalInput")
    d["baK"] = DT("baK", [128, NC_D], F32, kind="ExternalInput")
    d["baVt"] = DT("baVt", [128, D], F32, kind="ExternalInput")
    d["snbT"] = DT("snbT", [128, 1], F32, kind="ExternalInput")
    d["WoTh"] = DT("WoTh", [128, 16 * D], F32, kind="ExternalInput")
    d["boT"] = DT("boT", [128, D], F32, kind="ExternalInput")
    d["relXs"] = DT("relXs", [64, 511], F32, kind="ExternalInput")
    d["relXA"] = DT("relXA", [64, 255], F32, kind="ExternalInput")
    d["relXB"] = DT("relXB", [64, 255], F32, kind="ExternalInput")
    d["chiT"] = DT("chiT", [8, SEQ], F32, kind="ExternalInput")
    d["augS"] = DT("augS", [4 * 64, 8], F32, kind="ExternalInput")
    d["ident"] = DT("ident", [128, 128], F32, kind="ExternalInput")
    d["rwT"] = DT("rwT", [D, 8], F32, kind="ExternalInput")
    d["rb_row"] = DT("rb_row", [1, 8], F32, kind="ExternalInput")
    d["upT"] = DT("upT", [9, D, 2048], F32, kind="ExternalInput")
    d["dnT"] = DT("dnT", [9, 2048, D], F32, kind="ExternalInput")
    d["qwT"] = DT("qwT", [D, D], F32, kind="ExternalInput")
    d["kwT"] = DT("kwT", [D, D], F32, kind="ExternalInput")
    d["vwT"] = DT("vwT", [D, D], F32, kind="ExternalInput")
    d["owT"] = DT("owT", [D, D], F32, kind="ExternalInput")
    d["qw_nat"] = DT("qw_nat", [D, D], F32, kind="ExternalInput")
    d["qb1_pc"] = DT("qb1_pc", [128, NC_D], F32, kind="ExternalInput")
    d["qbBt"] = DT("qbBt", [128, D], F32, kind="ExternalInput")
    d["kbBt"] = DT("kbBt", [128, D], F32, kind="ExternalInput")
    d["vbBt"] = DT("vbBt", [128, D], F32, kind="ExternalInput")
    d["obBt"] = DT("obBt", [128, D], F32, kind="ExternalInput")
    d["grepT"] = DT("grepT", [64, 8], F32, kind="ExternalInput")
    d["grepb_t"] = DT("grepb_t", [1, 128], F32, kind="ExternalInput")
    d["gaBt"] = DT("gaBt", [128, 16], F32, kind="ExternalInput")
    d["pbBt"] = DT("pbBt", [128, 256], F32, kind="ExternalInput")
    d["WgT"] = DT("WgT", [D, 4], F32, kind="ExternalInput")
    d["bg_row"] = DT("bg_row", [1, 4], F32, kind="ExternalInput")
    d["WiT"] = DT("WiT", [D, 4096], F32, kind="ExternalInput")
    d["WoT1"] = DT("WoT1", [4096, D], F32, kind="ExternalInput")
    d["AT"] = DT("AT", [4 * D, 4], F32, kind="ExternalInput")
    d["BTc"] = DT("BTc", [4, 4 * 4096], F32, kind="ExternalInput")
    d["out"] = DT("out", [TOK, D], F32, kind="ExternalOutput")
    d["x1"] = DT("x1", [TOK, D], F32, kind="Internal")
    d["x2"] = DT("x2", [TOK, D], F32, kind="Internal")
    d["x3"] = DT("x3", [TOK, D], F32, kind="Internal")
    d["KTh"] = DT("KTh", [D, B * SEQ], BF16, kind="Internal")
    d["Vh"] = DT("Vh", [B * SEQ, D], BF16, kind="Internal")
    d["Eh"] = DT("Eh", [2 * 512 * 511], BF16, kind="Internal")
    d["EhA"] = DT("EhA", [2 * 128 * 255], BF16, kind="Internal")
    d["EhB"] = DT("EhB", [2 * 128 * 255], BF16, kind="Internal")

    with ExitStack() as ctx:
        blk = ctx.enter_context(nc.Block())
        shared = {n: Sem(ctx.enter_context(nc.semaphore("S_" + n))) for n in
                  ["g_ep0", "g_ep1", "v_ep", "s_ep", "x_ep0", "x_ep1",
                   "g_x0", "g_x1", "t_x", "v_x"]}
        m = emit_l0attn(nc, blk, ctx, d, [], shared)
        m = emit_l0ffn(nc, blk, ctx, d, m, shared)
        m = emit_l1attn(nc, blk, ctx, d, m, shared)
        m = emit_l1ffn(nc, blk, ctx, d, m, shared)
    return nc


def inmap_fused(h, c):
    im = dict(xTr=h["xT_rot"][c], x_own=h["x_own"][c], WaT=h["WaT"],
              baQ=h["baQ"], baK=h["baK"], baVt=h["baVt"], snbT=h["snbT"],
              WoTh=h["WoTh"], boT=h["boT"], relXs=h["relXT_std"],
              relXA=h["relXT_wA"][c], relXB=h["relXT_wB"][c],
              chiT=h["chi"][c], augS=h["augsel"][c], ident=h["ident"],
              rwT=h["rwT"], rb_row=h["rb_row"], upT=h["upT"], dnT=h["dnT"],
              qwT=h["qwT"], kwT=h["kwT"], vwT=h["vwT"], owT=h["owT"],
              qw_nat=h["qw_nat"], qb1_pc=h["qb1_pc"], qbBt=h["qbBt"],
              kbBt=h["kbBt"], vbBt=h["vbBt"], obBt=h["obBt"], grepT=h["grepT"],
              grepb_t=h["grepb_t"], gaBt=h["gaBt"], pbBt=h["pbBt"],
              WgT=h["WgT"], bg_row=h["bg_row"], WiT=h["WiT"], WoT1=h["WoT1"],
              AT=h["AT"], BTc=h["BTc"])
    return im



def kernel(**inputs):
    h = prep_host(inputs)
    cores = list(range(NCORES))
    nc = build_fused()
    res = run_bass_kernel_spmd(nc, [inmap_fused(h, c) for c in cores], core_ids=cores)
    outc = [res.results[c]["out"] for c in cores]
    out = np.concatenate([o.reshape(B, POS, D) for o in outc], axis=1)
    return np.ascontiguousarray(out, dtype=np.float32)
